# version 03
# baseline (speedup 1.0000x reference)
import contextlib
import numpy as np
import ml_dtypes
import concourse.bass as bass
import concourse.mybir as mybir
from concourse.bass_utils import run_bass_kernel_spmd

F32 = mybir.dt.float32
BF16 = mybir.dt.bfloat16
I32 = mybir.dt.int32
I16 = mybir.dt.int16
AF = mybir.ActivationFunctionType
ALU = mybir.AluOpType
AXX = mybir.AxisListType.X

T_SEQ = 2048
DM = 2048
NT = T_SEQ // 128

ENGINES = ("tensor", "vector", "scalar", "sync", "gpsimd")


class Res:
    __slots__ = ("name", "last_w", "readers")

    def __init__(self, name):
        self.name = name
        self.last_w = None
        self.readers = []


class B:

    def __init__(self, nc):
        self.nc = nc
        self.q = {e: [] for e in ENGINES}
        self.cnt = {}
        self.watermark = {e: {} for e in ENGINES}
        self.dma_idx = {"sync": 0, "gpsimd": 0, "scalar": 0}
        self.pending = {e: None for e in ENGINES}

    def sem(self, name):
        self.cnt[name] = 0

    def _waits(self, eng, toks):
        out = {}
        for t in toks:
            if t is None:
                continue
            s, v = t
            if v <= self.watermark[eng].get(s, 0):
                continue
            out[s] = max(out.get(s, 0), v)
        for s, v in out.items():
            self.watermark[eng][s] = v
        return list(out.items())

    def phase_barrier(self):
        toks = self.barrier_token()
        for e in ENGINES:
            self.pending[e] = toks

    def emit(self, eng, fn, reads=(), writes=(), sem=None, inc=0, extra_waits=()):
        toks = list(extra_waits)
        if self.pending[eng]:
            toks.extend(self.pending[eng])
            self.pending[eng] = None
        for r in reads:
            toks.append(r.last_w)
        for r in writes:
            toks.append(r.last_w)
            toks.extend(r.readers)
        if eng == "tensor":
            toks = [t for t in toks if t is None or t[0] != "t_tensor"]
        waits = self._waits(eng, toks)
        tok = None
        if inc:
            self.cnt[sem] += inc
            tok = (sem, self.cnt[sem])
        self.q[eng].append((fn, waits, sem if inc else None, inc))
        for r in reads:
            if tok is not None:
                r.readers.append(tok)
        for r in writes:
            r.last_w = tok
            r.readers = []
        return tok

    def op(self, eng, fn, reads=(), writes=(), extra_waits=()):
        return self.emit(eng, fn, reads, writes, sem="t_" + eng, inc=1,
                         extra_waits=extra_waits)

    NDMA = 8

    def dma(self, queue, fn, reads=(), writes=(), extra_waits=()):
        idx = self.dma_idx[queue]
        self.dma_idx[queue] = (idx + 1) % self.NDMA
        sem = f"d_{queue}{idx}"
        prev = (sem, self.cnt[sem]) if self.cnt[sem] > 0 else None
        ew = list(extra_waits) + ([prev] if prev else [])
        return self.emit(queue, fn, reads, writes, sem=sem, inc=16,
                         extra_waits=ew)

    def barrier_token(self):
        return [(s, v) for s, v in self.cnt.items() if v > 0]

    def run(self, sems):
        nc = self.nc
        with nc.Block() as blk:
            for eng in ENGINES:
                stream = self.q[eng]
                if not stream:
                    continue

                def make(eng, stream):
                    def body(e):
                        for fn, waits, sname, inc in stream:
                            for ws, wv in waits:
                                e.wait_ge(sems[ws], wv)
                            inst = fn(e)
                            if sname is not None:
                                inst.then_inc(sems[sname], inc)
                    return body

                getattr(blk, eng)(make(eng, stream))


SEM_NAMES = (["t_tensor", "t_vector", "t_scalar", "t_sync", "t_gpsimd"]
             + [f"d_sync{i}" for i in range(B.NDMA)]
             + [f"d_gpsimd{i}" for i in range(B.NDMA)]
             + [f"d_scalar{i}" for i in range(B.NDMA)])


def build_nc(upto="full", dump=None):
    nc = bass.Bass("TRN2", target_bir_lowering=False)

    dr = {}

    def din(name, shape):
        dr[name] = nc.dram_tensor(name, list(shape), F32, kind="ExternalInput").ap()

    din("x0", (T_SEQ, DM))
    din("l0_wq", (DM, DM)); din("l0_wk", (DM, DM)); din("l0_wv", (DM, DM))
    din("l0_wo", (DM, DM)); din("l0_bo", (1, DM))
    din("l0_router", (DM, 8))
    din("l0_gate_up", (8, DM, DM)); din("l0_down", (8, 1024, DM))
    din("l0_sh_gate", (DM, 1024)); din("l0_sh_up", (DM, 1024))
    din("l0_sh_down", (1024, DM))
    din("l1_wq", (DM, DM)); din("l1_wk", (DM, DM)); din("l1_wv", (DM, 4096))
    din("l1_wo", (4096, DM)); din("l1_wa", (DM, 16)); din("l1_wb", (DM, 16))
    din("l1_qcw", (DM, 4)); din("l1_qcb", (1, DM))
    din("l1_kcw", (DM, 4)); din("l1_kcb", (1, DM))
    din("l1_vcw", (4096, 4)); din("l1_vcb", (1, 4096))
    din("l1_w1", (8192, DM)); din("l1_w2", (DM, 8192))

    out = nc.dram_tensor("out", [T_SEQ, DM], F32, kind="ExternalOutput").ap()

    dr["ident"] = nc.inline_tensor(np.eye(128, dtype=ml_dtypes.bfloat16), "identb").ap()
    dr["identf"] = nc.inline_tensor(np.eye(128, dtype=np.float32), "identf").ap()
    msu = -np.triu(np.ones((128, 128), dtype=np.float32), 1)
    mui = np.triu(np.ones((128, 128), dtype=np.float32))
    dr["mask_su"] = nc.inline_tensor(msu.astype(ml_dtypes.bfloat16), "mask_su").ap()
    dr["mask_uif"] = nc.inline_tensor(mui.astype(np.float32), "mask_uif").ap()
    dr["mask_uib"] = nc.inline_tensor(mui.astype(ml_dtypes.bfloat16), "mask_uib").ap()
    dr["ones_row"] = nc.inline_tensor(np.ones((1, 128), dtype=np.float32), "ones_row").ap()
    dr["ones_rowb"] = nc.inline_tensor(np.ones((1, 128), dtype=ml_dtypes.bfloat16), "ones_rowb").ap()
    dr["ones_colb"] = nc.inline_tensor(np.ones((128, 1), dtype=ml_dtypes.bfloat16), "ones_colb").ap()

    x1_s = nc.dram_tensor("x1_s", [T_SEQ, DM], F32, kind="Internal").ap()
    q_s = nc.dram_tensor("q_s", [DM, T_SEQ], BF16, kind="Internal").ap()
    k_s = nc.dram_tensor("k_s", [DM, T_SEQ], BF16, kind="Internal").ap()
    v_s = nc.dram_tensor("v_s", [4096, T_SEQ], BF16, kind="Internal").ap()
    outs_s = nc.dram_tensor("outs_s", [4096, T_SEQ], BF16, kind="Internal").ap()
    o1_s = nc.dram_tensor("o1_s", [DM, T_SEQ], BF16, kind="Internal").ap()
    wq1t_s = nc.dram_tensor("wq1t_s", [DM, 8192], BF16, kind="Internal").ap()
    wq2t_s = nc.dram_tensor("wq2t_s", [8192, DM], BF16, kind="Internal").ap()
    xa_s = nc.dram_tensor("xa_s", [T_SEQ, DM], F32, kind="Internal").ap()

    es = contextlib.ExitStack()
    with es:
        A = es.enter_context(nc.sbuf_tensor("A", [128, NT * DM], BF16))
        Bacc = es.enter_context(nc.sbuf_tensor("Bacc", [128, NT * DM], BF16))
        C1 = es.enter_context(nc.sbuf_tensor("C1", [128, 24576], BF16))
        D1 = es.enter_context(nc.sbuf_tensor("D1", [128, 2048], F32))
        D2 = es.enter_context(nc.sbuf_tensor("D2", [128, 4096], BF16))
        ITL = es.enter_context(nc.sbuf_tensor("ITL", [128, 1024], I16))
        idb = es.enter_context(nc.sbuf_tensor("idb", [128, 128], BF16))
        idf = es.enter_context(nc.sbuf_tensor("idf", [128, 128], F32))
        msut = es.enter_context(nc.sbuf_tensor("msut", [128, 128], BF16))
        muift = es.enter_context(nc.sbuf_tensor("muift", [128, 128], F32))
        onesr = es.enter_context(nc.sbuf_tensor("onesr", [1, 128], F32))
        onesrb = es.enter_context(nc.sbuf_tensor("onesrb", [1, 128], BF16))
        stat = es.enter_context(nc.sbuf_tensor("stat", [128, 96], F32))
        wsm = es.enter_context(nc.sbuf_tensor("wsm", [128, 128], BF16))
        rw = es.enter_context(nc.sbuf_tensor("rw", [128, 128], F32))
        smallf = es.enter_context(nc.sbuf_tensor("smallf", [128, 64], F32))
        cst = es.enter_context(nc.sbuf_tensor("cst", [128, 8], F32))
        hwrk = es.enter_context(nc.sbuf_tensor("hwrk", [128, 256], BF16))
        muibt = es.enter_context(nc.sbuf_tensor("muibt", [128, 128], BF16))
        onescb = es.enter_context(nc.sbuf_tensor("onescb", [128, 1], BF16))
        statx = es.enter_context(nc.sbuf_tensor("statx", [128, 1216], F32))
        scanb = es.enter_context(nc.sbuf_tensor("scanb", [128, 1920], BF16))
        bT = es.enter_context(nc.sbuf_tensor("bT", [128, 256], BF16))
        PA = es.enter_context(nc.psum_tensor("PA", [128, 2048], F32))
        PBp = es.enter_context(nc.psum_tensor("PBp", [128, 1024], F32))
        PT = es.enter_context(nc.psum_tensor("PT", [128, 1024], BF16))
        PT2 = es.enter_context(nc.psum_tensor("PT2", [128, 1024], BF16))
        sems = {n: es.enter_context(nc.semaphore(n)) for n in SEM_NAMES}

        b = B(nc)
        for n in SEM_NAMES:
            b.sem(n)

        RA = [Res(f"A{i}") for i in range(NT)]
        RB = [Res(f"B{i}") for i in range(NT)]
        RQ = [Res(f"Q{i}") for i in range(3)]
        RO = [Res(f"O{i}") for i in range(2)]
        RD1, RD2, RIT = Res("D1"), Res("D2"), Res("ITL")
        RIT1 = Res("ITL1")
        Rstat, Rrw = Res("stat"), Res("rw")
        Rsm, Rbob, Rwsm = Res("smallf"), Res("bob"), Res("wsm")
        Rhw = Res("hwrk")
        Rhw1 = Res("hwrk1")
        RHF = [Res(f"hf{i}") for i in range(8)]
        Rconst = Res("const")
        Rout = Res("out")
        Rscr = {n: Res(n) for n in
                ("x1_s", "q_s", "k_s", "v_s", "outs_s", "wq1t_s", "wq2t_s",
                 "o1_s", "xa_s")}
        RPA = [Res(f"PA{i}") for i in range(4)]
        RPB = [Res(f"PB{i}") for i in range(2)]
        RPT, RPT2 = Res("PT"), Res("PT2")

        def PAb(i):
            return PA[:, i * 512:(i + 1) * 512]

        def Q(k, lo, hi):
            return C1[:, k * 8192 + lo: k * 8192 + hi]

        Rcst = Res("cst")
        Rstx, Rscb, RbT = Res("statx"), Res("scanb"), Res("bT")
        CEPS, CHALF, CNEG1, CONE = (lambda c: None), None, None, None

        def cstc(i):
            return cst[:, i:i + 1]

        def load_consts():
            b.op("vector", lambda e: e.memset(cst[:, 0:1], 1e-5), writes=[Rcst])
            b.op("vector", lambda e: e.memset(cst[:, 1:2], 0.5), writes=[Rcst])
            b.op("vector", lambda e: e.memset(cst[:, 2:3], -1.0), writes=[Rcst])
            b.op("vector", lambda e: e.memset(cst[:, 3:4], 1.0), writes=[Rcst])
            b.dma("sync", lambda e: e.dma_start(muibt[:], dr["mask_uib"][:, :]), writes=[Rconst])
            b.dma("sync", lambda e: e.dma_start(onescb[:], dr["ones_colb"][:, :]), writes=[Rconst])
            b.dma("sync", lambda e: e.dma_start(idb[:], dr["ident"][:, :]), writes=[Rconst])
            b.dma("sync", lambda e: e.dma_start(idf[:], dr["identf"][:, :]), writes=[Rconst])
            b.dma("sync", lambda e: e.dma_start(msut[:], dr["mask_su"][:, :]), writes=[Rconst])
            b.dma("sync", lambda e: e.dma_start(muift[:], dr["mask_uif"][:, :]), writes=[Rconst])
            b.dma("sync", lambda e: e.dma_start(onesr[:], dr["ones_row"][:, :]), writes=[Rconst])
            b.dma("sync", lambda e: e.dma_start(onesrb[:], dr["ones_rowb"][:, :]), writes=[Rconst])
            b.dma("gpsimd", lambda e: e.dma_start(
                wsm[:, 0:128].rearrange("p (c e) -> p c e", e=8),
                dr["l0_router"].rearrange("(c p) e -> p c e", p=128)),
                writes=[Rwsm])

        def transpose(dst_ap, src_ap, rdst, rsrc, use2=False):
            pr = RPT2 if use2 else RPT
            pap = (PT2 if use2 else PT)[:, 0:128]
            b.op("tensor", lambda e: e.transpose(pap, src_ap, idb[:]),
                 reads=[rsrc, Rconst], writes=[pr])
            b.op("vector", lambda e: e.tensor_copy(dst_ap, pap),
                 reads=[pr], writes=[rdst])

        def wload(dst_ap, src_ap, wres, extra=()):
            if not isinstance(wres, (list, tuple)):
                wres = [wres]
            return b.dma("gpsimd", lambda e: e.dma_start(dst_ap, src_ap),
                         writes=wres, extra_waits=extra)

        def phase_ln(xin_ap, s1_track):
            for i in range(NT):
                b.dma("sync", lambda e, i=i: e.dma_start(
                    D1[:, :], xin_ap[i * 128:(i + 1) * 128, :]), writes=[RD1])
                b.op("vector", lambda e: e.reduce_sum(stat[:, 0:1], D1[:, :], AXX),
                     reads=[RD1], writes=[Rstat])
                b.op("scalar", lambda e: e.activation(
                    D2[:, 0:2048], D1[:, :], AF.Square, accum_out=stat[:, 1:2]),
                    reads=[RD1], writes=[RD2, Rstat])
                b.op("vector", lambda e: e.tensor_scalar(
                    stat[:, 2:3], stat[:, 0:1], 1.0 / DM, None, ALU.mult),
                    reads=[Rstat], writes=[Rstat])
                b.op("vector", lambda e: e.tensor_scalar(
                    stat[:, 3:4], stat[:, 1:2], 1.0 / DM, None, ALU.mult),
                    reads=[Rstat], writes=[Rstat])
                b.op("vector", lambda e: e.scalar_tensor_tensor(
                    stat[:, 4:5], stat[:, 2:3], stat[:, 2:3], stat[:, 3:4],
                    ALU.mult, ALU.subtract),
                    reads=[Rstat], writes=[Rstat])
                b.op("scalar", lambda e: e.activation(
                    stat[:, 5:6], stat[:, 4:5], AF.Sqrt, bias=cstc(0), scale=cstc(2)),
                    reads=[Rstat, Rcst], writes=[Rstat])
                b.op("vector", lambda e: e.reciprocal(stat[:, 6:7], stat[:, 5:6]),
                     reads=[Rstat], writes=[Rstat])
                b.op("vector", lambda e: e.tensor_scalar(
                    D2[:, 0:2048], D1[:, :], stat[:, 2:3], stat[:, 6:7],
                    ALU.subtract, ALU.mult),
                    reads=[RD1, Rstat], writes=[RD2])
                if s1_track:
                    b.op("vector", lambda e, i=i: e.reduce_max(
                        stat[:, 32 + i:33 + i], D2[:, 0:2048], AXX,
                        apply_absolute_value=True),
                        reads=[RD2], writes=[Rstat])
                for c in range(16):
                    transpose(A[:, c * 2048 + i * 128: c * 2048 + (i + 1) * 128],
                              D2[:, c * 128:(c + 1) * 128],
                              RA[c], RD2, use2=(c % 2 == 1))

        def proj_modeT(wdram, fdim, evac):
            nq = fdim // 512
            for fq in range(nq):
                k = fq % 2
                wr = RQ[k]
                wload(Q(k, 0, 8192).rearrange("p (c f) -> p c f", f=512),
                      wdram[:, fq * 512:(fq + 1) * 512]
                      .rearrange("(c p) f -> p c f", p=128), wr)
                for i in range(NT):
                    bank = RPA[i % 2]
                    bap = PAb(i % 2)
                    for kc in range(16):
                        b.op("tensor", lambda e, i=i, kc=kc, k=k, bap=bap: e.matmul(
                            bap,
                            A[:, kc * 2048 + i * 128: kc * 2048 + (i + 1) * 128],
                            Q(k, kc * 512, (kc + 1) * 512),
                            start=(kc == 0), stop=(kc == 15)),
                            reads=[RA[kc], wr], writes=[bank])
                    evac(i, fq, bap, bank)

        def phase_l0_kv():
            def evk(i, fq, bap, bank):
                b.op("scalar", lambda e, i=i, fq=fq, bap=bap: e.activation(
                    Bacc[:, i * 2048 + fq * 512: i * 2048 + (fq + 1) * 512],
                    bap, AF.Relu),
                    reads=[bank], writes=[RB[i]])
            proj_modeT(dr["l0_wk"], DM, evk)

            b.op("vector", lambda e: e.memset(
                D2[:, 0:520].rearrange("p (h c) -> p h c", c=65)[:, :, 64:65], 1.0),
                writes=[RD2])

            kv_first = {}

            def kvslot(j):
                if j < 7:
                    return RPB[0], PBp[:, j * 65:(j + 1) * 65]
                if j < 14:
                    return RPB[1], PBp[:, 512 + (j - 7) * 65: 512 + (j - 6) * 65]
                return RPA[3], PA[:, 1536 + (j - 14) * 65: 1536 + (j - 13) * 65]

            def evv(i, fq, bap, bank):
                b.op("vector", lambda e, bap=bap: e.tensor_copy(
                    D2[:, 0:520].rearrange("p (h c) -> p h c", c=65)[:, :, 0:64],
                    bap.rearrange("p (h c) -> p h c", c=64)),
                    reads=[bank], writes=[RD2])
                for hh in range(4):
                    h0 = 8 * fq + 2 * hh
                    j = h0 // 2
                    kr, kap = kvslot(j)
                    for half in range(2):
                        h = h0 + half
                        po = 64 * half
                        first = (id(kr), po) not in kv_first
                        kv_first[(id(kr), po)] = True
                        b.op("tensor",
                             lambda e, i=i, h=h, hh=hh, half=half, kap=kap, po=po, first=first: e.matmul(
                                 kap[po:po + 64, :],
                                 Bacc[:, i * 2048 + h * 64: i * 2048 + (h + 1) * 64],
                                 D2[:, (2 * hh + half) * 65:(2 * hh + half + 1) * 65],
                                 start=first, stop=(i == NT - 1),
                                 skip_group_check=True,
                                 tile_position=(0, po)),
                             reads=[RB[i], RD2], writes=[kr])
            proj_modeT(dr["l0_wv"], DM, evv)
            for j in range(16):
                kr, kap = kvslot(j)
                b.op("vector", lambda e, kap=kap, j=j: e.tensor_copy(
                    scanb[:, j * 65:(j + 1) * 65], kap),
                    reads=[kr], writes=[Rscb])

        def phase_l0_qo(sub=3):
            b.dma("gpsimd", lambda e: e.dma_start(D2[0:1, 0:2048], dr["l0_bo"][0:1, :]),
                  writes=[RD2])
            for fb in range(4):
                b.op("tensor", lambda e, fb=fb: e.matmul(
                    PAb(0), onesrb[:, :], D2[0:1, fb * 512:(fb + 1) * 512],
                    start=True, stop=True),
                    reads=[RD2, Rconst], writes=[RPA[0]])
                b.op("vector", lambda e, fb=fb: e.tensor_copy(
                    C1[:, 22528:24576][:, fb * 512:(fb + 1) * 512], PAb(0)),
                    reads=[RPA[0]], writes=[Rbob])

            def oslot(ts, half):
                lo = half * 512 + ts * 65
                return RPB[half], PBp[:, lo:lo + 65]

            if sub < 2:
                return
            for fq in range(4):
                k = fq % 2
                wr = RQ[k]
                wload(Q(k, 0, 8192).rearrange("p (c f) -> p c f", f=512),
                      dr["l0_wq"][:, fq * 512:(fq + 1) * 512]
                      .rearrange("(c p) f -> p c f", p=128), wr)
                for fc in range(4):
                    h0 = fq * 8 + fc * 2
                    j = h0 // 2
                    for tg in range(4):
                        bank = RPA[tg % 2]
                        bap = PAb(tg % 2)
                        for kc in range(16):
                            b.op("tensor", lambda e, fc=fc, tg=tg, kc=kc, k=k, bap=bap: e.matmul(
                                bap,
                                Q(k, kc * 512 + fc * 128, kc * 512 + (fc + 1) * 128),
                                A[:, kc * 2048 + tg * 512:kc * 2048 + (tg + 1) * 512],
                                start=(kc == 0), stop=(kc == 15)),
                                reads=[RA[kc], wr], writes=[bank])
                        b.op("scalar", lambda e, bap=bap: e.activation(
                            D2[:, 1024:1536], bap, AF.Relu),
                            reads=[bank], writes=[RD2])
                        for half in range(2):
                            po = 64 * half
                            for ts in range(4):
                                sr, sap = oslot(ts, half)
                                b.op("tensor", lambda e, ts=ts, j=j, po=po, sap=sap: e.matmul(
                                    sap,
                                    D2[po:po + 64, 1024 + ts * 128:1024 + (ts + 1) * 128],
                                    scanb[po:po + 64, j * 65:(j + 1) * 65],
                                    start=True, stop=True, skip_group_check=True),
                                    reads=[RD2, Rscb], writes=[sr])
                        for g in range(2):
                            b.op("vector", lambda e, g=g: e.tensor_scalar_max(
                                smallf[:, g * 4:(g + 1) * 4],
                                PBp[:, g * 512 + 64: g * 512 + 64 + 260]
                                .rearrange("p (s c) -> p s c", c=65)[:, :, 0:1],
                                1e-6),
                                reads=[RPB[g]], writes=[Rsm])
                        b.op("vector", lambda e: e.reciprocal(
                            smallf[:, 8:16], smallf[:, 0:8]),
                            reads=[Rsm], writes=[Rsm])
                        for ts in range(4):
                            for half in range(2):
                                sr, sap = oslot(ts, half)
                                b.op("vector", lambda e, ts=ts, half=half, sap=sap: e.tensor_scalar(
                                    D2[:, 2048 + ts * 128 + half * 64:
                                        2048 + ts * 128 + half * 64 + 64],
                                    sap[:, 0:64],
                                    smallf[:, 8 + half * 4 + ts:9 + half * 4 + ts],
                                    None, ALU.mult),
                                    reads=[sr, Rsm], writes=[RD2])
                        for ts in range(4):
                            transpose(D2[:, 2560 + ts * 128:2560 + (ts + 1) * 128],
                                      D2[:, 2048 + ts * 128:2048 + (ts + 1) * 128],
                                      RD2, RD2, use2=(ts % 2 == 1))
                        b.dma("sync", lambda e, h0=h0, tg=tg: e.dma_start(
                            o1_s[h0 * 64:h0 * 64 + 128,
                                 tg * 512:(tg + 1) * 512],
                            D2[:, 2560:3072]),
                            reads=[RD2], writes=[Rscr["o1_s"]])
            if sub < 3:
                return
            for fh in range(2):
                wr = [RQ[0], RQ[1]]
                wload(C1[:, 0:16384].rearrange("p (c f) -> p c f", f=1024),
                      dr["l0_wo"][:, fh * 1024:(fh + 1) * 1024]
                      .rearrange("(c p) f -> p c f", p=128), wr)
                for i in range(NT):
                    ob = RO[i % 2]
                    ocol = 16384 + (i % 2) * 2048
                    b.dma("sync", lambda e, i=i, ocol=ocol: e.dma_start(
                        C1[:, ocol:ocol + 2048].rearrange("p (c f) -> p c f", f=128),
                        o1_s[:, i * 128:(i + 1) * 128]
                        .rearrange("(c p) f -> p c f", p=128)),
                        reads=[Rscr["o1_s"]], writes=[ob])
                    for fb in range(2):
                        bank = RPA[2 * (i % 2) + fb]
                        bap = PAb(2 * (i % 2) + fb)
                        for c in range(16):
                            b.op("tensor", lambda e, c=c, fb=fb, bap=bap, ocol=ocol: e.matmul(
                                bap, C1[:, ocol + c * 128:ocol + (c + 1) * 128],
                                C1[:, c * 1024 + fb * 512:c * 1024 + (fb + 1) * 512],
                                start=(c == 0), stop=(c == 15)),
                                reads=[ob] + wr, writes=[bank])
                        b.op("vector", lambda e, i=i, fh=fh, fb=fb, bap=bap: e.tensor_tensor(
                            Bacc[:, i * 2048 + fh * 1024 + fb * 512:
                                 i * 2048 + fh * 1024 + (fb + 1) * 512],
                            bap, C1[:, 22528:24576][:, fh * 1024 + fb * 512:
                                      fh * 1024 + (fb + 1) * 512],
                            ALU.add),
                            reads=[bank, Rbob], writes=[RB[i]])

        def phase_router():
            for i in range(NT):
                bap = PAb(i % 2)
                bank = RPA[i % 2]
                for kc in range(16):
                    b.op("tensor", lambda e, i=i, kc=kc, bap=bap: e.matmul(
                        bap[:, 0:8],
                        A[:, kc * 2048 + i * 128: kc * 2048 + (i + 1) * 128],
                        wsm[:, kc * 8:(kc + 1) * 8],
                        start=(kc == 0), stop=(kc == 15)),
                        reads=[RA[kc], Rwsm], writes=[bank])
                lg = smallf[:, 16:24]
                b.op("vector", lambda e, bap=bap: e.tensor_copy(lg, bap[:, 0:8]),
                     reads=[bank], writes=[Rsm])
                b.op("vector", lambda e: e.reduce_max(smallf[:, 24:25], lg, AXX),
                     reads=[Rsm], writes=[Rsm])
                b.op("vector", lambda e: e.tensor_scalar(
                    smallf[:, 32:40], lg, smallf[:, 24:25], None, ALU.is_ge),
                    reads=[Rsm], writes=[Rsm])
                b.op("vector", lambda e: e.scalar_tensor_tensor(
                    smallf[:, 40:48], smallf[:, 32:40], -1e30, lg,
                    ALU.mult, ALU.add),
                    reads=[Rsm], writes=[Rsm])
                b.op("vector", lambda e: e.reduce_max(
                    smallf[:, 25:26], smallf[:, 40:48], AXX),
                    reads=[Rsm], writes=[Rsm])
                b.op("vector", lambda e: e.tensor_scalar(
                    smallf[:, 32:40], lg, smallf[:, 25:26], None, ALU.is_ge),
                    reads=[Rsm], writes=[Rsm])
                b.op("scalar", lambda e: e.activation(
                    smallf[:, 40:48], lg, AF.Sigmoid),
                    reads=[Rsm], writes=[Rsm])
                b.op("vector", lambda e, i=i: e.tensor_tensor(
                    rw[:, i * 8:(i + 1) * 8], smallf[:, 40:48], smallf[:, 32:40],
                    ALU.mult),
                    reads=[Rsm], writes=[Rrw])

        def phase_l0_moe(nexp=9):
            for e in list(range(8))[:max(0, nexp - 1)] + [8]:
                shared = (e == 8)
                for q in range(4):
                    for fh in range(2):
                        k = 1 + ((q * 2 + fh) % 2)
                        dsrc = (dr["l0_sh_down"] if shared else dr["l0_down"][e])
                        wload(Q(k, 0, 8192).rearrange("p (c f) -> p c f", f=1024),
                              dsrc[:, fh * 1024:(fh + 1) * 1024]
                              .rearrange("(c p) f -> p c f", p=128), RQ[k])
                    for fb in range(8):
                        d = fb % 2
                        gud = Q(0, d * 4096, (d + 1) * 4096).rearrange(
                            "p (c f) -> p c f", f=256)
                        if shared:
                            wload(gud[:, :, 0:128],
                                  dr["l0_sh_gate"][:, fb * 128:(fb + 1) * 128]
                                  .rearrange("(c p) f -> p c f", p=128), RQ[0])
                            wload(gud[:, :, 128:256],
                                  dr["l0_sh_up"][:, fb * 128:(fb + 1) * 128]
                                  .rearrange("(c p) f -> p c f", p=128), RQ[0])
                        else:
                            wload(gud[:, :, 0:128],
                                  dr["l0_gate_up"][e][:, fb * 128:(fb + 1) * 128]
                                  .rearrange("(c p) f -> p c f", p=128), RQ[0])
                            wload(gud[:, :, 128:256],
                                  dr["l0_gate_up"][e][:, 1024 + fb * 128:1024 + (fb + 1) * 128]
                                  .rearrange("(c p) f -> p c f", p=128), RQ[0])
                        for t4 in range(4):
                            i = q * 4 + t4
                            bap = PBp[:, (t4 % 2) * 512:(t4 % 2) * 512 + 512]
                            bank = RPB[t4 % 2]
                            for kc in range(16):
                                b.op("tensor", lambda e_, i=i, kc=kc, d=d, bap=bap: e_.matmul(
                                    bap[:, 0:256],
                                    A[:, kc * 2048 + i * 128: kc * 2048 + (i + 1) * 128],
                                    Q(0, d * 4096 + kc * 256, d * 4096 + (kc + 1) * 256),
                                    start=(kc == 0), stop=(kc == 15)),
                                    reads=[RA[kc], RQ[0]], writes=[bank])
                            hw = hwrk[:, 0:256] if t4 % 2 == 0 else scanb[:, 1040:1296]
                            rhw = Rhw if t4 % 2 == 0 else Rhw1
                            ptx = PT if t4 % 2 == 0 else PT2
                            rptx = RPT if t4 % 2 == 0 else RPT2
                            if shared:
                                b.op("scalar", lambda e_, bap=bap, hw=hw: e_.activation(
                                    hw[:, 0:128], bap[:, 0:128], AF.Silu),
                                    reads=[bank], writes=[rhw])
                                b.op("vector", lambda e_, bap=bap, hw=hw: e_.tensor_tensor(
                                    hw[:, 128:256], bap[:, 128:256],
                                    hw[:, 0:128], ALU.mult),
                                    reads=[bank, rhw], writes=[rhw])
                            else:
                                b.op("scalar", lambda e_, i=i, e=e, bap=bap, hw=hw: e_.activation(
                                    hw[:, 0:128], bap[:, 0:128], AF.Silu,
                                    scale=rw[:, i * 8 + e:i * 8 + e + 1]),
                                    reads=[bank, Rrw], writes=[rhw])
                                b.op("vector", lambda e_, i=i, e=e, bap=bap, hw=hw: e_.scalar_tensor_tensor(
                                    hw[:, 128:256], bap[:, 128:256],
                                    rw[:, i * 8 + e:i * 8 + e + 1],
                                    hw[:, 0:128], ALU.mult, ALU.mult),
                                    reads=[bank, Rrw, rhw], writes=[rhw])
                            b.op("tensor", lambda e_, hw=hw, ptx=ptx: e_.transpose(
                                ptx[:, 0:128], hw[:, 128:256], idb[:]),
                                reads=[rhw, Rconst], writes=[rptx])
                            b.op("vector", lambda e_, fb=fb, t4=t4, ptx=ptx: e_.tensor_copy(
                                D2[:, fb * 512 + t4 * 128: fb * 512 + (t4 + 1) * 128],
                                ptx[:, 0:128]),
                                reads=[rptx], writes=[RHF[fb]])
                    for fh in range(2):
                        k = 1 + ((q * 2 + fh) % 2)
                        for t4 in range(4):
                            i = q * 4 + t4
                            bap = PA[:, (t4 % 2) * 1024:(t4 % 2) * 1024 + 1024]
                            banks = [RPA[(t4 % 2) * 2], RPA[(t4 % 2) * 2 + 1]]
                            for ic in range(8):
                                for fb2 in range(2):
                                    b.op("tensor", lambda e_, ic=ic, fb2=fb2, t4=t4, k=k, bap=bap: e_.matmul(
                                        bap[:, fb2 * 512:(fb2 + 1) * 512],
                                        D2[:, ic * 512 + t4 * 128: ic * 512 + (t4 + 1) * 128],
                                        Q(k, ic * 1024 + fb2 * 512, ic * 1024 + (fb2 + 1) * 512),
                                        start=(ic == 0), stop=(ic == 7)),
                                        reads=[RHF[ic], RQ[k]], writes=[banks[fb2]])
                            b.op("vector", lambda e_, i=i, fh=fh, bap=bap: e_.tensor_tensor(
                                Bacc[:, i * 2048 + fh * 1024: i * 2048 + (fh + 1) * 1024],
                                bap,
                                Bacc[:, i * 2048 + fh * 1024: i * 2048 + (fh + 1) * 1024],
                                ALU.add),
                                reads=banks + [RB[i]], writes=[RB[i]])

        def phase_x1_ln():
            for i in range(NT):
                xcol = (i % 2) * 2048
                xres = RQ[i % 2]
                b.dma("gpsimd", lambda e, i=i, xcol=xcol: e.dma_start(
                    Q(0, xcol, xcol + 2048), dr["x0"][i * 128:(i + 1) * 128, :]),
                    writes=[xres])
                b.op("vector", lambda e, i=i, xcol=xcol: e.tensor_tensor(
                    D1[:, :], Q(0, xcol, xcol + 2048),
                    Bacc[:, i * 2048:(i + 1) * 2048], ALU.add),
                    reads=[xres, RB[i]], writes=[RD1])
                b.dma("sync", lambda e, i=i: e.dma_start(
                    x1_s[i * 128:(i + 1) * 128, :], D1[:, :]),
                    reads=[RD1], writes=[Rscr["x1_s"]])
                b.op("vector", lambda e: e.reduce_sum(stat[:, 0:1], D1[:, :], AXX),
                     reads=[RD1], writes=[Rstat])
                b.op("scalar", lambda e: e.activation(
                    D2[:, 0:2048], D1[:, :], AF.Square, accum_out=stat[:, 1:2]),
                    reads=[RD1], writes=[RD2, Rstat])
                b.op("vector", lambda e: e.tensor_scalar(
                    stat[:, 2:3], stat[:, 0:1], 1.0 / DM, None, ALU.mult),
                    reads=[Rstat], writes=[Rstat])
                b.op("vector", lambda e: e.tensor_scalar(
                    stat[:, 3:4], stat[:, 1:2], 1.0 / DM, None, ALU.mult),
                    reads=[Rstat], writes=[Rstat])
                b.op("vector", lambda e: e.scalar_tensor_tensor(
                    stat[:, 4:5], stat[:, 2:3], stat[:, 2:3], stat[:, 3:4],
                    ALU.mult, ALU.subtract),
                    reads=[Rstat], writes=[Rstat])
                b.op("scalar", lambda e: e.activation(
                    stat[:, 5:6], stat[:, 4:5], AF.Sqrt, bias=cstc(0), scale=cstc(2)),
                    reads=[Rstat, Rcst], writes=[Rstat])
                b.op("vector", lambda e: e.reciprocal(stat[:, 6:7], stat[:, 5:6]),
                     reads=[Rstat], writes=[Rstat])
                b.op("vector", lambda e: e.tensor_scalar(
                    D2[:, 0:2048], D1[:, :], stat[:, 2:3], stat[:, 6:7],
                    ALU.subtract, ALU.mult),
                    reads=[RD1, Rstat], writes=[RD2])
                b.op("vector", lambda e, i=i: e.reduce_max(
                    stat[:, 32 + i:33 + i], D2[:, 0:2048], AXX,
                    apply_absolute_value=True),
                    reads=[RD2], writes=[Rstat])
                for c in range(16):
                    transpose(A[:, c * 2048 + i * 128: c * 2048 + (i + 1) * 128],
                              D2[:, c * 128:(c + 1) * 128],
                              RA[c], RD2, use2=(c % 2 == 1))

        def phase_l1_proj():
            b.dma("sync", lambda e: e.dma_start(
                statx[:, 0:64].rearrange("p (c j) -> p c j", j=4),
                dr["l1_qcw"].rearrange("(c p) j -> p c j", p=128)), writes=[Rstx])
            b.dma("sync", lambda e: e.dma_start(
                statx[:, 64:128].rearrange("p (c j) -> p c j", j=4),
                dr["l1_kcw"].rearrange("(c p) j -> p c j", p=128)), writes=[Rstx])
            b.dma("sync", lambda e: e.dma_start(
                statx[:, 128:256].rearrange("p (c j) -> p c j", j=4),
                dr["l1_vcw"].rearrange("(c p) j -> p c j", p=128)), writes=[Rstx])
            b.dma("sync", lambda e: e.dma_start(
                statx[:, 256:272].rearrange("p (c o) -> p c o", o=1),
                dr["l1_qcb"].rearrange("o (c p) -> p c o", p=128),
                allow_slow_non_contiguous=True), writes=[Rstx])
            b.dma("sync", lambda e: e.dma_start(
                statx[:, 272:288].rearrange("p (c o) -> p c o", o=1),
                dr["l1_kcb"].rearrange("o (c p) -> p c o", p=128),
                allow_slow_non_contiguous=True), writes=[Rstx])
            b.dma("sync", lambda e: e.dma_start(
                statx[:, 288:320].rearrange("p (c o) -> p c o", o=1),
                dr["l1_vcb"].rearrange("o (c p) -> p c o", p=128),
                allow_slow_non_contiguous=True), writes=[Rstx])

            plans = [
                (dr["l1_wq"], q_s, 4, True, 0, 256),
                (dr["l1_wk"], k_s, 4, True, 64, 272),
                (dr["l1_wv"], v_s, 8, False, 128, 288),
            ]
            ZC, T1 = 0, 516
            for wd, scr, nq, l2, cwo, cbo in plans:
                for fq in range(nq):
                    k_ = fq % 2
                    wload(Q(k_, 0, 8192).rearrange("p (c f) -> p c f", f=512),
                          wd[:, fq * 512:(fq + 1) * 512]
                          .rearrange("(c p) f -> p c f", p=128), RQ[k_])
                    for fc in range(4):
                        ch = fq * 4 + fc
                        b.op("vector", lambda e: e.memset(
                            scanb[:, ZC:ZC + 3], 0.0), writes=[Rscb])
                        for tg in range(4):
                            bank = RPA[tg % 2]
                            bap = PAb(tg % 2)
                            for kc in range(16):
                                b.op("tensor", lambda e, fc=fc, tg=tg, kc=kc, k_=k_, bap=bap: e.matmul(
                                    bap,
                                    Q(k_, kc * 512 + fc * 128, kc * 512 + (fc + 1) * 128),
                                    A[:, kc * 2048 + tg * 512:kc * 2048 + (tg + 1) * 512],
                                    start=(kc == 0), stop=(kc == 15)),
                                    reads=[RA[kc], RQ[k_]], writes=[bank])
                            b.op("vector", lambda e, bap=bap: e.tensor_copy(
                                scanb[:, ZC + 3:ZC + 515], bap),
                                reads=[bank], writes=[Rscb])
                            b.op("vector", lambda e, ch=ch, cwo=cwo: e.tensor_scalar(
                                scanb[:, T1:T1 + 512], scanb[:, ZC:ZC + 512],
                                statx[:, cwo + ch * 4:cwo + ch * 4 + 1], None, ALU.mult),
                                reads=[Rscb, Rstx], writes=[Rscb])
                            for j in (1, 2, 3):
                                b.op("vector", lambda e, j=j, ch=ch, cwo=cwo: e.scalar_tensor_tensor(
                                    scanb[:, T1:T1 + 512], scanb[:, ZC + j:ZC + j + 512],
                                    statx[:, cwo + ch * 4 + j:cwo + ch * 4 + j + 1],
                                    scanb[:, T1:T1 + 512], ALU.mult, ALU.add),
                                    reads=[Rscb, Rstx], writes=[Rscb])
                            b.op("vector", lambda e: e.tensor_copy(
                                scanb[:, ZC:ZC + 3], scanb[:, ZC + 512:ZC + 515]),
                                reads=[Rscb], writes=[Rscb])
                            b.op("scalar", lambda e, tg=tg, ch=ch, cbo=cbo: e.activation(
                                D2[:, tg * 512:(tg + 1) * 512], scanb[:, T1:T1 + 512],
                                AF.Silu, bias=statx[:, cbo + ch:cbo + ch + 1]),
                                reads=[Rscb, Rstx], writes=[RD2])
                            if l2:
                                b.op("vector", lambda e, tg=tg: e.tensor_tensor(
                                    scanb[:, T1:T1 + 512],
                                    D2[:, tg * 512:(tg + 1) * 512],
                                    D2[:, tg * 512:(tg + 1) * 512], ALU.mult),
                                    reads=[RD2], writes=[Rscb])
                                b.op("tensor", lambda e: e.matmul(
                                    PBp[0:1, 0:512], onescb[:], scanb[:, T1:T1 + 512],
                                    start=True, stop=True),
                                    reads=[Rscb, Rconst], writes=[RPB[0]])
                                b.op("scalar", lambda e: e.activation(
                                    D1[0:1, 0:512], PBp[0:1, 0:512], AF.Sqrt),
                                    reads=[RPB[0]], writes=[RD1])
                                b.op("vector", lambda e: e.tensor_scalar_max(
                                    D1[0:1, 0:512], D1[0:1, 0:512], 1e-12),
                                    reads=[RD1], writes=[RD1])
                                b.op("vector", lambda e: e.reciprocal(
                                    D1[0:1, 512:1024], D1[0:1, 0:512]),
                                    reads=[RD1], writes=[RD1])
                                b.op("tensor", lambda e: e.matmul(
                                    PBp[:, 512:1024], onesr[:], D1[0:1, 512:1024],
                                    start=True, stop=True),
                                    reads=[RD1, Rconst], writes=[RPB[1]])
                                b.op("vector", lambda e, tg=tg: e.tensor_tensor(
                                    D2[:, tg * 512:(tg + 1) * 512],
                                    D2[:, tg * 512:(tg + 1) * 512],
                                    PBp[:, 512:1024], ALU.mult),
                                    reads=[RD2, RPB[1]], writes=[RD2])
                        b.dma("sync", lambda e, scr=scr, ch=ch: e.dma_start(
                            scr[ch * 128:(ch + 1) * 128, :], D2[:, 0:2048]),
                            reads=[RD2], writes=[Rscr[{id(q_s): "q_s", id(k_s): "k_s", id(v_s): "v_s"}[id(scr)]]])
            b.dma("gpsimd", lambda e: e.dma_start(
                C1[:, 22528:24576][:, 0:256].rearrange("p (c f) -> p c f", f=16),
                dr["l1_wa"].rearrange("(c p) f -> p c f", p=128)), writes=[Rbob])
            b.dma("gpsimd", lambda e: e.dma_start(
                C1[:, 22528:24576][:, 256:512].rearrange("p (c f) -> p c f", f=16),
                dr["l1_wb"].rearrange("(c p) f -> p c f", p=128)), writes=[Rbob])
            for tg in range(4):
                for which in range(2):
                    bap = PBp[0:16, which * 512:which * 512 + 512]
                    bank = RPB[which]
                    for kc in range(16):
                        b.op("tensor", lambda e, tg=tg, kc=kc, which=which, bap=bap: e.matmul(
                            bap,
                            C1[:, 22528:24576][:, which * 256 + kc * 16:which * 256 + (kc + 1) * 16],
                            A[:, kc * 2048 + tg * 512:kc * 2048 + (tg + 1) * 512],
                            start=(kc == 0), stop=(kc == 15)),
                            reads=[RA[kc], Rbob], writes=[bank])
                b.op("scalar", lambda e: e.activation(
                    statx[0:16, 704:1216], PBp[0:16, 0:512], AF.Sigmoid),
                    reads=[RPB[0]], writes=[Rstx])
                b.op("scalar", lambda e: e.activation(
                    D2[0:16, 0:512], PBp[0:16, 512:1024], AF.Sigmoid),
                    reads=[RPB[1]], writes=[RD2])
                b.op("scalar", lambda e: e.activation(
                    statx[0:16, 704:1216], statx[0:16, 704:1216], AF.Ln),
                    reads=[Rstx], writes=[Rstx])
                for blk in range(4):
                    cidx = tg * 4 + blk
                    b.op("tensor", lambda e, blk=blk: e.transpose(
                        PA[:, 0:16], statx[0:16, 704 + blk * 128:704 + (blk + 1) * 128],
                        idf[0:16, 0:16]),
                        reads=[Rstx, Rconst], writes=[RPA[0]])
                    b.op("vector", lambda e, cidx=cidx: e.tensor_copy(
                        statx[:, 320 + cidx * 16:320 + (cidx + 1) * 16], PA[:, 0:16]),
                        reads=[RPA[0]], writes=[Rstx])
                    b.op("tensor", lambda e, blk=blk: e.transpose(
                        PT[:, 0:16], D2[0:16, blk * 128:(blk + 1) * 128], idb[0:16, 0:16]),
                        reads=[RD2, Rconst], writes=[RPT])
                    b.op("vector", lambda e, cidx=cidx: e.tensor_copy(
                        bT[:, cidx * 16:(cidx + 1) * 16], PT[:, 0:16]),
                        reads=[RPT], writes=[RbT])

        def phase_l1_scan():
            RD2o = Res("d2odd")
            RD1h = [Res(f"st{h}") for h in range(8)]
            Rstx0, Rstx1 = Res("sx0"), Res("sx1")

            def SL(par, off, w):
                if par == 0:
                    return scanb[:, off:off + w], Rscb
                return D2[:, 2048 + off:2048 + off + w], RD2o
            FS = {0: dict(eL=592, eLn=593, negL=594, eLCb=595,
                          expD=736, w1f=864),
                  1: dict(eL=216, eLn=217, negL=218, eLCb=219,
                          expD=88, w1f=992)}
            PBK = {0: dict(b1=(PA, 0, RPA[0]), b2=(PA, 512, RPA[1]),
                           b3=(PA, 1024, RPA[2]), pt=(PT, RPT)),
                   1: dict(b1=(PA, 1536, RPA[3]), b2=(PBp, 0, RPB[0]),
                           b3=(PBp, 512, RPB[1]), pt=(PT2, RPT2))}

            for hf in range(2):
                b.op("vector", lambda e: e.memset(D1[:, :], 0.0), writes=RD1h)
                for c in range(16):
                    d = c % 2
                    base = d * 4096
                    qr = RQ[d]
                    for nm, lo, hi, rows in (("q", 0, 1024, 1024),
                                             ("k", 1024, 2048, 1024),
                                             ("v", 2048, 4096, 2048)):
                        scr = {"q": q_s, "k": k_s, "v": v_s}[nm]
                        b.dma("sync", lambda e, hf=hf, c=c, base=base, lo=lo, scr=scr, rows=rows: e.dma_start(
                            Q(0, base + lo, base + lo + rows)
                            .rearrange("p (h f) -> p h f", f=128),
                            scr[hf * rows:(hf + 1) * rows, c * 128:(c + 1) * 128]
                            .rearrange("(h p) f -> p h f", p=128)),
                            writes=[qr])

                    def qf(h, base=base):
                        return Q(0, base + h * 128, base + (h + 1) * 128)

                    def kf(h, base=base):
                        return Q(0, base + 1024 + h * 128, base + 1024 + (h + 1) * 128)

                    def vf(h, half, base=base):
                        return Q(0, base + 2048 + (2 * h + half) * 128,
                                 base + 2048 + (2 * h + half + 1) * 128)

                    b.op("tensor", lambda e, c=c: e.matmul(
                        PA[:, 1024:1040], muift[:],
                        statx[:, 320 + c * 16:320 + (c + 1) * 16],
                        start=True, stop=True),
                        reads=[Rstx, Rconst], writes=[RPA[2]])
                    b.op("vector", lambda e: e.tensor_copy(
                        statx[:, 576:592], PA[:, 1024:1040]),
                        reads=[RPA[2]], writes=[Rstx])

                    def head_steps(h, kf=kf, qf=qf, vf=vf, qr=qr):
                        par = h % 2
                        hg = hf * 8 + h
                        F = FS[par]
                        BK = PBK[par]
                        pb1, o1_, rb1 = BK["b1"]
                        pb2, o2_, rb2 = BK["b2"]
                        pb3, o3_, rb3 = BK["b3"]
                        ptp, rpt = BK["pt"]
                        Lcol = statx[:, 576 + hg:577 + hg]
                        bcol = bT[:, c * 16 + hg:c * 16 + hg + 1]
                        sst = D1[:, h * 256:(h + 1) * 256]
                        rs = RD1h[h]
                        rsx = Rstx0 if par == 0 else Rstx1
                        lrow = statx[0:1, 608:736] if par == 0 else rw[0:1, 0:128]
                        lrr = rsx if par == 0 else Rrw
                        s0b, s0r = SL(par, 1152, 256)
                        vtm, vtr = SL(par, 1408, 256)
                        ntc, ntr = SL(par, 0, 128)
                        y0, _ = SL(par, 640, 256)
                        y1s, _ = SL(par, 896, 256)
                        ktm, _ = SL(par, 1664, 128)
                        kp, _ = SL(par, 1792, 128)
                        pta, _ = SL(par, 128, 128)
                        ptb, _ = SL(par, 256, 128)
                        pna, _ = SL(par, 384, 128)
                        pnb, _ = SL(par, 512, 128)
                        sres = ntr
                        S = []
                        A_ = S.append
                        A_(lambda: b.op("vector", lambda e: e.tensor_copy(s0b, sst),
                                        reads=[rs], writes=[sres]))
                        for half in range(2):
                            A_(lambda half=half: b.op("tensor", lambda e: e.transpose(
                                ptp[:, 0:128], vf(h, half), idb[:]),
                                reads=[qr, Rconst], writes=[rpt]))
                            A_(lambda half=half: b.op("vector", lambda e: e.tensor_copy(
                                vtm[:, half * 128:(half + 1) * 128], ptp[:, 0:128]),
                                reads=[rpt], writes=[sres]))
                        A_(lambda: b.op("scalar", lambda e: e.activation(
                            statx[:, F["eL"]:F["eL"] + 1], Lcol, AF.Exp),
                            reads=[Rstx], writes=[rsx]))
                        A_(lambda: b.op("vector", lambda e: e.tensor_scalar(
                            statx[:, F["eLn"]:F["eLn"] + 1],
                            statx[:, F["eL"]:F["eL"] + 1], -1.0, None, ALU.mult),
                            reads=[rsx], writes=[rsx]))
                        A_(lambda: b.op("vector", lambda e: e.tensor_scalar(
                            statx[:, F["negL"]:F["negL"] + 1], Lcol, -1.0,
                            None, ALU.mult),
                            reads=[Rstx], writes=[rsx]))
                        A_(lambda: b.op("tensor", lambda e: e.transpose(
                            pb3[0:1, o3_:o3_ + 128], Lcol, idf[:]),
                            reads=[Rstx, Rconst], writes=[rb3]))
                        A_(lambda: b.op("vector", lambda e: e.tensor_copy(
                            lrow, pb3[0:1, o3_:o3_ + 128]),
                            reads=[rb3], writes=[lrr]))
                        A_(lambda: b.op("tensor", lambda e: e.matmul(
                            pb3[:, o3_:o3_ + 128], onesr[:], lrow,
                            start=True, stop=True),
                            reads=[lrr, Rconst], writes=[rb3]))
                        A_(lambda: b.op("vector", lambda e: e.tensor_scalar(
                            pb3[:, o3_ + 128:o3_ + 256],
                            pb3[:, o3_:o3_ + 128],
                            statx[:, F["negL"]:F["negL"] + 1], 0.0,
                            ALU.add, ALU.min),
                            reads=[rb3, rsx], writes=[rb3]))
                        A_(lambda: b.op("scalar", lambda e: e.activation(
                            statx[:, F["expD"]:F["expD"] + 128],
                            pb3[:, o3_ + 128:o3_ + 256], AF.Exp),
                            reads=[rb3], writes=[rsx]))
                        A_(lambda: b.op("tensor", lambda e: e.matmul(
                            pb1[:, o1_:o1_ + 256], kf(h), s0b,
                            start=True, stop=True),
                            reads=[qr, sres], writes=[rb1]))
                        A_(lambda: b.op("vector", lambda e: e.scalar_tensor_tensor(
                            y0, pb1[:, o1_:o1_ + 256],
                            statx[:, F["eLn"]:F["eLn"] + 1], vtm,
                            ALU.mult, ALU.add),
                            reads=[rb1, rsx, sres], writes=[sres]))
                        A_(lambda: b.op("tensor", lambda e: e.matmul(
                            pb2[:, o2_:o2_ + 128], kf(h), kf(h),
                            start=True, stop=True),
                            reads=[qr], writes=[rb2]))
                        A_(lambda: b.op("vector", lambda e: e.tensor_tensor(
                            statx[:, F["w1f"]:F["w1f"] + 128],
                            statx[:, F["expD"]:F["expD"] + 128],
                            pb2[:, o2_:o2_ + 128], ALU.mult),
                            reads=[rsx, rb2], writes=[rsx]))
                        A_(lambda: b.op("vector", lambda e: e.scalar_tensor_tensor(
                            ntc, statx[:, F["w1f"]:F["w1f"] + 128], bcol,
                            msut[:], ALU.mult, ALU.mult),
                            reads=[rsx, RbT, Rconst], writes=[sres]))
                        A_(lambda: b.op("tensor", lambda e: e.transpose(
                            ptp[:, 0:128], ntc, idb[:]),
                            reads=[sres, Rconst], writes=[rpt]))
                        A_(lambda: b.op("vector", lambda e: e.tensor_copy(
                            pna, ptp[:, 0:128]),
                            reads=[rpt], writes=[sres]))
                        A_(lambda: b.op("vector", lambda e: e.tensor_copy(
                            pta, ntc),
                            reads=[sres], writes=[sres]))
                        ycur, ynext = y0, y1s
                        pt_cur, pt_nxt = pta, ptb
                        pn_cur, pn_nxt = pna, pnb
                        for j in range(7):
                            A_(lambda pt_cur=pt_cur, ycur=ycur: b.op("tensor", lambda e: e.matmul(
                                pb1[:, o1_:o1_ + 256], pt_cur, ycur,
                                start=True, stop=True),
                                reads=[sres], writes=[rb1]))
                            A_(lambda ycur=ycur, ynext=ynext: b.op("vector", lambda e: e.tensor_tensor(
                                ynext, pb1[:, o1_:o1_ + 256], ycur, ALU.add),
                                reads=[rb1, sres], writes=[sres]))
                            ycur, ynext = ynext, ycur
                            if j < 6:
                                A_(lambda pn_cur=pn_cur, pt_cur=pt_cur: b.op("tensor", lambda e: e.matmul(
                                    pb2[:, o2_:o2_ + 128], pn_cur, pt_cur,
                                    start=True, stop=True),
                                    reads=[sres], writes=[rb2]))
                                A_(lambda pt_nxt=pt_nxt: b.op("vector", lambda e: e.tensor_copy(
                                    pt_nxt, pb2[:, o2_:o2_ + 128]),
                                    reads=[rb2], writes=[sres]))
                                if j < 5:
                                    A_(lambda pt_cur=pt_cur, pn_cur=pn_cur: b.op("tensor", lambda e: e.matmul(
                                        pb2[:, o2_:o2_ + 128], pt_cur, pn_cur,
                                        start=True, stop=True),
                                        reads=[sres], writes=[rb2]))
                                    A_(lambda pn_nxt=pn_nxt: b.op("vector", lambda e: e.tensor_copy(
                                        pn_nxt, pb2[:, o2_:o2_ + 128]),
                                        reads=[rb2], writes=[sres]))
                                pt_cur, pt_nxt = pt_nxt, pt_cur
                                pn_cur, pn_nxt = pn_nxt, pn_cur
                        R_ = ycur
                        Rtmp = y1s if R_ is y0 else y0
                        A_(lambda: b.op("tensor", lambda e: e.matmul(
                            pb2[:, o2_:o2_ + 128], kf(h), qf(h),
                            start=True, stop=True),
                            reads=[qr], writes=[rb2]))
                        A_(lambda: b.op("vector", lambda e: e.tensor_tensor(
                            statx[:, F["w1f"]:F["w1f"] + 128],
                            statx[:, F["expD"]:F["expD"] + 128],
                            pb2[:, o2_:o2_ + 128], ALU.mult),
                            reads=[rsx, rb2], writes=[rsx]))
                        A_(lambda: b.op("vector", lambda e: e.scalar_tensor_tensor(
                            ntc, statx[:, F["w1f"]:F["w1f"] + 128], bcol,
                            muibt[:], ALU.mult, ALU.mult),
                            reads=[rsx, RbT, Rconst], writes=[sres]))
                        A_(lambda R_=R_: b.op("tensor", lambda e: e.matmul(
                            pb1[:, o1_ + 256:o1_ + 512], ntc, R_,
                            start=True, stop=True, skip_group_check=True),
                            reads=[sres], writes=[rb1]))
                        A_(lambda: b.op("tensor", lambda e: e.matmul(
                            pb1[:, o1_:o1_ + 256], qf(h), s0b,
                            start=True, stop=True, skip_group_check=True),
                            reads=[qr, sres], writes=[rb1]))
                        A_(lambda Rtmp=Rtmp: b.op("vector", lambda e: e.tensor_scalar(
                            Rtmp, pb1[:, o1_:o1_ + 256],
                            statx[:, F["eL"]:F["eL"] + 1], None, ALU.mult),
                            reads=[rb1, rsx], writes=[sres]))
                        A_(lambda Rtmp=Rtmp: b.op("vector", lambda e: e.tensor_tensor(
                            Rtmp, Rtmp, pb1[:, o1_ + 256:o1_ + 512], ALU.add),
                            reads=[sres, rb1], writes=[sres]))
                        for half in range(2):
                            A_(lambda half=half, Rtmp=Rtmp: b.op("tensor", lambda e: e.transpose(
                                ptp[:, 0:128], Rtmp[:, half * 128:(half + 1) * 128],
                                idb[:]),
                                reads=[sres, Rconst], writes=[rpt]))
                            A_(lambda half=half: b.op("vector", lambda e: e.tensor_copy(
                                D2[:, h * 256 + half * 128:h * 256 + (half + 1) * 128],
                                ptp[:, 0:128]),
                                reads=[rpt], writes=[RD2]))
                        A_(lambda: b.op("tensor", lambda e: e.transpose(
                            ptp[:, 0:128], kf(h), idb[:]),
                            reads=[qr, Rconst], writes=[rpt]))
                        A_(lambda: b.op("vector", lambda e: e.tensor_copy(
                            ktm, ptp[:, 0:128]),
                            reads=[rpt], writes=[sres]))
                        A_(lambda: b.op("vector", lambda e: e.tensor_tensor(
                            statx[:, F["negL"]:F["negL"] + 1],
                            statx[:, F["expD"] + 127:F["expD"] + 128], bcol,
                            ALU.mult),
                            reads=[rsx, RbT], writes=[rsx]))
                        A_(lambda: b.op("vector", lambda e: e.tensor_scalar(
                            kp, ktm, statx[:, F["negL"]:F["negL"] + 1],
                            None, ALU.mult),
                            reads=[sres, rsx], writes=[sres]))
                        A_(lambda R_=R_: b.op("tensor", lambda e: e.matmul(
                            pb3[:, o3_ + 256:o3_ + 512], kp, R_,
                            start=True, stop=True, skip_group_check=True),
                            reads=[sres], writes=[rb3]))
                        A_(lambda: b.op("vector", lambda e: e.tensor_tensor(
                            statx[:, F["eLCb"]:F["eLCb"] + 1],
                            statx[:, F["expD"] + 127:F["expD"] + 128],
                            statx[:, F["eL"]:F["eL"] + 1], ALU.mult),
                            reads=[rsx], writes=[rsx]))
                        A_(lambda: b.op("vector", lambda e: e.scalar_tensor_tensor(
                            sst, sst, statx[:, F["eLCb"]:F["eLCb"] + 1],
                            pb3[:, o3_ + 256:o3_ + 512], ALU.mult, ALU.add),
                            reads=[rs, rsx, rb3], writes=[rs]))
                        return S

                    for hp in range(4):
                        s_even = head_steps(2 * hp)
                        s_odd = head_steps(2 * hp + 1)
                        n = max(len(s_even), len(s_odd))
                        for idx in range(n):
                            if idx < len(s_even):
                                s_even[idx]()
                            if idx < len(s_odd):
                                s_odd[idx]()
                    b.dma("sync", lambda e, hf=hf, c=c: e.dma_start(
                        outs_s[hf * 2048:(hf + 1) * 2048, c * 128:(c + 1) * 128]
                        .rearrange("(h p) f -> p h f", p=128),
                        D2[:, 0:2048].rearrange("p (h f) -> p h f", f=128)),
                        reads=[RD2], writes=[Rscr["outs_s"]])

        def phase_l1_wo():
            for kq in range(4):
                wr = [RQ[0], RQ[1]]
                wload(C1[:, 0:16384].rearrange("p (c f) -> p c f", f=2048),
                      dr["l1_wo"][kq * 1024:(kq + 1) * 1024, :]
                      .rearrange("(c p) f -> p c f", p=128), wr)
                for i in range(NT):
                    ob = RO[i % 2]
                    ocol = 16384 + (i % 2) * 1024
                    b.dma("sync", lambda e, i=i, kq=kq, ocol=ocol: e.dma_start(
                        C1[:, ocol:ocol + 1024].rearrange("p (c f) -> p c f", f=128),
                        outs_s[kq * 1024:(kq + 1) * 1024, i * 128:(i + 1) * 128]
                        .rearrange("(c p) f -> p c f", p=128)),
                        reads=[Rscr["outs_s"]], writes=[ob])
                    if kq == 3:
                        b.dma("gpsimd", lambda e, i=i: e.dma_start(
                            D1[:, :], x1_s[i * 128:(i + 1) * 128, :]),
                            reads=[Rscr["x1_s"], RD1], writes=[RD1])
                    for fb in range(4):
                        bank = RPA[fb]
                        bap = PAb(fb)
                        for cc in range(8):
                            b.op("tensor", lambda e, cc=cc, fb=fb, bap=bap, ocol=ocol: e.matmul(
                                bap, C1[:, ocol + cc * 128:ocol + (cc + 1) * 128],
                                C1[:, cc * 2048 + fb * 512:cc * 2048 + (fb + 1) * 512],
                                start=(cc == 0), stop=(cc == 7)),
                                reads=[ob] + wr, writes=[bank])
                        if kq == 0:
                            b.op("vector", lambda e, i=i, fb=fb, bap=bap: e.tensor_copy(
                                Bacc[:, i * 2048 + fb * 512:i * 2048 + (fb + 1) * 512],
                                bap),
                                reads=[bank, RB[i]], writes=[RB[i]])
                        elif kq < 3:
                            b.op("vector", lambda e, i=i, fb=fb, bap=bap: e.tensor_tensor(
                                Bacc[:, i * 2048 + fb * 512:i * 2048 + (fb + 1) * 512],
                                bap,
                                Bacc[:, i * 2048 + fb * 512:i * 2048 + (fb + 1) * 512],
                                ALU.add),
                                reads=[bank, RB[i]], writes=[RB[i]])
                        else:
                            b.op("vector", lambda e, i=i, fb=fb, bap=bap: e.tensor_tensor(
                                scanb[:, 0:512], bap,
                                Bacc[:, i * 2048 + fb * 512:i * 2048 + (fb + 1) * 512],
                                ALU.add),
                                reads=[bank, RB[i]], writes=[Rscb])
                            st_ = 704 if fb % 2 == 0 else 192
                            b.op("vector", lambda e, fb=fb, st_=st_: e.tensor_tensor(
                                statx[:, st_:st_ + 512], scanb[:, 0:512],
                                D1[:, fb * 512:(fb + 1) * 512], ALU.add),
                                reads=[Rscb, RD1], writes=[Rstx])
                            b.dma("sync", lambda e, i=i, fb=fb, st_=st_: e.dma_start(
                                xa_s[i * 128:(i + 1) * 128, fb * 512:(fb + 1) * 512],
                                statx[:, st_:st_ + 512]),
                                reads=[Rstx], writes=[Rscr["xa_s"]])

        def round_min_sign(dst_bf_ap, dst_res, clipv, sg_ap, sg_res, n, par=0):
            ao = 704 if par == 0 else 88
            io = par * 512
            av = statx[:, ao:ao + n]
            it = ITL[:, io:io + n]
            rit = RIT if par == 0 else RIT1
            b.op("scalar", lambda e: e.activation(it, av, AF.Identity),
                 reads=[Rstx], writes=[rit])
            b.op("vector", lambda e: e.scalar_tensor_tensor(
                av, it, -1.0, av, ALU.mult, ALU.add),
                reads=[rit, Rstx], writes=[Rstx])
            b.op("vector", lambda e: e.scalar_tensor_tensor(
                av, av, 0.5, it, ALU.is_ge, ALU.add),
                reads=[Rstx, rit], writes=[Rstx])
            b.op("vector", lambda e: e.scalar_tensor_tensor(
                dst_bf_ap, av, float(clipv), sg_ap,
                ALU.min, ALU.mult),
                reads=[Rstx, sg_res], writes=[dst_res])

        def quant_chain(dst_bf_ap, src_f32_ap, inv_scale_ap, clipv, sg_ap,
                        n, par=0, rsrc_extra=()):
            ao = 704 if par == 0 else 88
            b.op("scalar", lambda e: e.activation(
                statx[:, ao:ao + n], src_f32_ap, AF.Abs, scale=inv_scale_ap),
                reads=[RD1, Rstx] + list(rsrc_extra), writes=[Rstx])
            b.op("scalar", lambda e: e.activation(
                sg_ap, src_f32_ap, AF.Sign),
                reads=[RD1] + list(rsrc_extra), writes=[RD2])
            round_min_sign(dst_bf_ap, RD2, clipv, sg_ap, RD2, n, par=par)

        def phase_prep_wq():
            for hc in range(64):
                b.dma("sync", lambda e, hc=hc: e.dma_start(
                    D1[:, :], dr["l1_w1"][hc * 128:(hc + 1) * 128, :]),
                    writes=[RD1])
                b.op("vector", lambda e: e.reduce_sum(
                    statx[:, 640:641], D1[:, :], AXX, apply_absolute_value=True),
                    reads=[RD1], writes=[Rstx])
                b.op("vector", lambda e, hc=hc: e.tensor_scalar(
                    statx[:, hc:hc + 1], statx[:, 640:641], 1.0 / DM, None, ALU.mult),
                    reads=[Rstx], writes=[Rstx])
                b.op("vector", lambda e: e.reciprocal(
                    statx[:, 641:642], statx[:, 640:641]),
                    reads=[Rstx], writes=[Rstx])
                b.op("vector", lambda e: e.tensor_scalar(
                    statx[:, 641:642], statx[:, 641:642], float(DM), None, ALU.mult),
                    reads=[Rstx], writes=[Rstx])
                for pc in range(4):
                    quant_chain(D2[:, pc * 512:(pc + 1) * 512],
                                D1[:, pc * 512:(pc + 1) * 512],
                                statx[:, 641:642], 1.0,
                                D2[:, 2048 + (pc % 2) * 512:
                                      2048 + (pc % 2) * 512 + 512], 512,
                                par=pc % 2)
                for t in range(16):
                    transpose(D2[:, 2048 + t * 128:2048 + (t + 1) * 128],
                              D2[:, t * 128:(t + 1) * 128],
                              RD2, RD2, use2=(t % 2 == 1))
                b.dma("sync", lambda e, hc=hc: e.dma_start(
                    wq1t_s[:, hc * 128:(hc + 1) * 128]
                    .rearrange("(c p) f -> p c f", p=128),
                    D2[:, 2048:4096].rearrange("p (c f) -> p c f", f=128)),
                    reads=[RD2], writes=[Rscr["wq1t_s"]])
            for dc in range(16):
                for pc in range(4):
                    b.dma("sync", lambda e, dc=dc, pc=pc: e.dma_start(
                        D1[:, :], dr["l1_w2"][dc * 128:(dc + 1) * 128,
                                              pc * 2048:(pc + 1) * 2048]),
                        writes=[RD1])
                    b.op("vector", lambda e, pc=pc: e.reduce_sum(
                        statx[:, 644 + pc:645 + pc], D1[:, :], AXX,
                        apply_absolute_value=True),
                        reads=[RD1], writes=[Rstx])
                b.op("vector", lambda e: e.tensor_tensor(
                    statx[:, 644:645], statx[:, 644:645], statx[:, 645:646], ALU.add),
                    reads=[Rstx], writes=[Rstx])
                b.op("vector", lambda e: e.tensor_tensor(
                    statx[:, 646:647], statx[:, 646:647], statx[:, 647:648], ALU.add),
                    reads=[Rstx], writes=[Rstx])
                b.op("vector", lambda e: e.tensor_tensor(
                    statx[:, 640:641], statx[:, 644:645], statx[:, 646:647], ALU.add),
                    reads=[Rstx], writes=[Rstx])
                b.op("vector", lambda e, dc=dc: e.tensor_scalar(
                    statx[:, 64 + dc:65 + dc], statx[:, 640:641], 1.0 / 8192,
                    None, ALU.mult),
                    reads=[Rstx], writes=[Rstx])
                b.op("vector", lambda e: e.reciprocal(
                    statx[:, 641:642], statx[:, 640:641]),
                    reads=[Rstx], writes=[Rstx])
                b.op("vector", lambda e: e.tensor_scalar(
                    statx[:, 641:642], statx[:, 641:642], 8192.0, None, ALU.mult),
                    reads=[Rstx], writes=[Rstx])
                for pc in range(4):
                    b.dma("sync", lambda e, dc=dc, pc=pc: e.dma_start(
                        D1[:, :], dr["l1_w2"][dc * 128:(dc + 1) * 128,
                                              pc * 2048:(pc + 1) * 2048]),
                        writes=[RD1])
                    for sp in range(4):
                        quant_chain(D2[:, sp * 512:(sp + 1) * 512],
                                    D1[:, sp * 512:(sp + 1) * 512],
                                    statx[:, 641:642], 1.0,
                                    D2[:, 2048 + (sp % 2) * 512:
                                          2048 + (sp % 2) * 512 + 512], 512,
                                    par=sp % 2)
                    for t in range(16):
                        transpose(D2[:, 2048 + t * 128:2048 + (t + 1) * 128],
                                  D2[:, t * 128:(t + 1) * 128],
                                  RD2, RD2, use2=(t % 2 == 1))
                    b.dma("sync", lambda e, dc=dc, pc=pc: e.dma_start(
                        wq2t_s[pc * 2048:(pc + 1) * 2048,
                               dc * 128:(dc + 1) * 128]
                        .rearrange("(c p) f -> p c f", p=128),
                        D2[:, 2048:4096].rearrange("p (c f) -> p c f", f=128)),
                        reads=[RD2], writes=[Rscr["wq2t_s"]])

        def phase_l1_ffn():
            for i in range(NT):
                b.op("vector", lambda e, i=i: e.tensor_scalar_max(
                    stat[:, 48 + i:49 + i], stat[:, 32 + i:33 + i], 1e-5),
                    reads=[Rstat], writes=[Rstat])
                b.op("vector", lambda e, i=i: e.tensor_scalar(
                    stat[:, 64 + i:65 + i], stat[:, 48 + i:49 + i], 1.0 / 127.0,
                    None, ALU.mult),
                    reads=[Rstat], writes=[Rstat])
                b.op("vector", lambda e, i=i: e.reciprocal(
                    stat[:, 48 + i:49 + i], stat[:, 48 + i:49 + i]),
                    reads=[Rstat], writes=[Rstat])
                b.op("vector", lambda e, i=i: e.tensor_scalar(
                    stat[:, 48 + i:49 + i], stat[:, 48 + i:49 + i], 127.0,
                    None, ALU.mult),
                    reads=[Rstat], writes=[Rstat])

            def build_bcast(colbase, dst_ap, dst_res, nda=True):
                for i in range(NT):
                    b.op("tensor", lambda e, i=i, colbase=colbase: e.transpose(
                        PA[0:1, 1024:1152], stat[:, colbase + i:colbase + i + 1],
                        idf[:]),
                        reads=[Rstat, Rconst], writes=[RPA[2]])
                    b.op("vector", lambda e, i=i: e.tensor_copy(
                        D1[0:1, i * 128:(i + 1) * 128], PA[0:1, 1024:1152]),
                        reads=[RPA[2]], writes=[RD1])
                for fb in range(4):
                    b.op("tensor", lambda e, fb=fb: e.matmul(
                        PA[:, 1024:1536], onesr[:], D1[0:1, fb * 512:(fb + 1) * 512],
                        start=True, stop=True),
                        reads=[RD1, Rconst], writes=[RPA[2]])
                    b.op("vector", lambda e, fb=fb, dst_ap=dst_ap: e.tensor_copy(
                        dst_ap[:, fb * 512:(fb + 1) * 512], PA[:, 1024:1536]),
                        reads=[RPA[2]], writes=[dst_res])

            build_bcast(48, D2[:, 0:2048], RD2)
            build_bcast(64, D2[:, 2048:4096], RD2)
            for i in range(NT):
                b.op("tensor", lambda e, i=i: e.transpose(
                    PA[0:1, 1024:1152], statx[:, 64 + i:65 + i], idf[:]),
                    reads=[Rstx, Rconst], writes=[RPA[2]])
                b.op("vector", lambda e, i=i: e.tensor_copy(
                    D1[0:1, i * 128:(i + 1) * 128], PA[0:1, 1024:1152]),
                    reads=[RPA[2]], writes=[RD1])
            for fb in range(4):
                b.op("tensor", lambda e, fb=fb: e.matmul(
                    PA[:, 1024:1536], onesr[:], D1[0:1, fb * 512:(fb + 1) * 512],
                    start=True, stop=True),
                    reads=[RD1, Rconst], writes=[RPA[2]])
                b.op("vector", lambda e, fb=fb: e.tensor_copy(
                    C1[:, 8192 + fb * 512:8192 + (fb + 1) * 512], PA[:, 1024:1536]),
                    reads=[RPA[2]], writes=[RQ[1]])

            for cc in range(16):
                for pc in range(4):
                    src = A[:, cc * 2048 + pc * 512: cc * 2048 + (pc + 1) * 512]
                    par = pc % 2
                    ao = 704 if par == 0 else 88
                    sgs = scanb[:, par * 512:par * 512 + 512]
                    b.op("vector", lambda e, src=src, pc=pc, ao=ao: e.tensor_tensor(
                        statx[:, ao:ao + 512], src, D2[:, pc * 512:(pc + 1) * 512],
                        ALU.mult),
                        reads=[RA[cc], RD2], writes=[Rstx])
                    b.op("scalar", lambda e, src=src, sgs=sgs: e.activation(
                        sgs, src, AF.Sign),
                        reads=[RA[cc]], writes=[Rscb])
                    b.op("scalar", lambda e, ao=ao: e.activation(
                        statx[:, ao:ao + 512], statx[:, ao:ao + 512], AF.Abs),
                        reads=[Rstx], writes=[Rstx])
                    round_min_sign(src, RA[cc], 127.0, sgs, Rscb, 512, par=par)

            RMX, RMN, S2B, SG2 = 512, 1024, 0, 512
            for g in range(4):
                gcol = g * 512
                for hc in range(64):
                    wk = hc % 2
                    wcol = 16384 + wk * 2048
                    wload(C1[:, wcol:wcol + 2048].rearrange("p (c f) -> p c f", f=128),
                          wq1t_s[:, hc * 128:(hc + 1) * 128]
                          .rearrange("(c p) f -> p c f", p=128), RO[wk])
                    bank = RPA[hc % 2]
                    bap = PAb(hc % 2)
                    for kc in range(16):
                        b.op("tensor", lambda e, hc=hc, kc=kc, gcol=gcol, wcol=wcol, bap=bap: e.matmul(
                            bap,
                            C1[:, wcol + kc * 128:wcol + (kc + 1) * 128],
                            A[:, kc * 2048 + gcol: kc * 2048 + gcol + 512],
                            start=(kc == 0), stop=(kc == 15)),
                            reads=[RA[kc], RO[hc % 2]], writes=[bank])
                    b.op("vector", lambda e, gcol=gcol, bap=bap: e.tensor_tensor(
                        statx[:, 704:1216], bap, D2[:, 2048 + gcol:2048 + gcol + 512],
                        ALU.mult),
                        reads=[bank, RD2], writes=[Rstx])
                    b.op("scalar", lambda e, hc=hc: e.activation(
                        Bacc[:, hc * 512:(hc + 1) * 512], statx[:, 704:1216],
                        AF.Gelu_apprx_tanh, scale=statx[:, hc:hc + 1]),
                        reads=[Rstx], writes=[RB[0]])
                    if hc == 0:
                        b.op("vector", lambda e: e.tensor_copy(
                            scanb[:, RMX:RMX + 512], Bacc[:, 0:512]),
                            reads=[RB[0]], writes=[Rscb])
                        b.op("vector", lambda e: e.tensor_copy(
                            scanb[:, RMN:RMN + 512], Bacc[:, 0:512]),
                            reads=[RB[0]], writes=[Rscb])
                    else:
                        b.op("vector", lambda e, hc=hc: e.tensor_tensor(
                            scanb[:, RMX:RMX + 512], scanb[:, RMX:RMX + 512],
                            Bacc[:, hc * 512:(hc + 1) * 512], ALU.max),
                            reads=[RB[0], Rscb], writes=[Rscb])
                        b.op("vector", lambda e, hc=hc: e.tensor_tensor(
                            scanb[:, RMN:RMN + 512], scanb[:, RMN:RMN + 512],
                            Bacc[:, hc * 512:(hc + 1) * 512], ALU.min),
                            reads=[RB[0], Rscb], writes=[Rscb])
                b.op("vector", lambda e: e.tensor_scalar(
                    scanb[:, RMN:RMN + 512], scanb[:, RMN:RMN + 512], -1.0,
                    None, ALU.mult),
                    reads=[Rscb], writes=[Rscb])
                b.op("vector", lambda e: e.tensor_tensor(
                    scanb[:, RMX:RMX + 512], scanb[:, RMX:RMX + 512],
                    scanb[:, RMN:RMN + 512], ALU.max),
                    reads=[Rscb], writes=[Rscb])
                for ts in range(4):
                    b.op("tensor", lambda e, ts=ts: e.transpose(
                        PT[:, 0:128], scanb[:, RMX + ts * 128:RMX + (ts + 1) * 128],
                        idb[:]),
                        reads=[Rscb, Rconst], writes=[RPT])
                    b.op("vector", lambda e, ts=ts: e.reduce_max(
                        stat[:, 80 + ts:81 + ts], PT[:, 0:128], AXX),
                        reads=[RPT], writes=[Rstat])
                    b.op("vector", lambda e, ts=ts: e.tensor_scalar_max(
                        stat[:, 80 + ts:81 + ts], stat[:, 80 + ts:81 + ts], 1e-5),
                        reads=[Rstat], writes=[Rstat])
                    b.op("vector", lambda e, ts=ts: e.tensor_scalar(
                        stat[:, 84 + ts:85 + ts], stat[:, 80 + ts:81 + ts],
                        1.0 / 127.0, None, ALU.mult),
                        reads=[Rstat], writes=[Rstat])
                    b.op("vector", lambda e, ts=ts: e.reciprocal(
                        stat[:, 80 + ts:81 + ts], stat[:, 80 + ts:81 + ts]),
                        reads=[Rstat], writes=[Rstat])
                    b.op("vector", lambda e, ts=ts: e.tensor_scalar(
                        stat[:, 80 + ts:81 + ts], stat[:, 80 + ts:81 + ts], 127.0,
                        None, ALU.mult),
                        reads=[Rstat], writes=[Rstat])
                    b.op("tensor", lambda e, ts=ts: e.transpose(
                        PA[0:1, 1024:1152], stat[:, 80 + ts:81 + ts], idf[:]),
                        reads=[Rstat, Rconst], writes=[RPA[2]])
                    b.op("vector", lambda e, ts=ts: e.tensor_copy(
                        D1[0:1, ts * 128:(ts + 1) * 128], PA[0:1, 1024:1152]),
                        reads=[RPA[2]], writes=[RD1])
                b.op("tensor", lambda e: e.matmul(
                    PA[:, 1024:1536], onesr[:], D1[0:1, 0:512],
                    start=True, stop=True),
                    reads=[RD1, Rconst], writes=[RPA[2]])
                b.op("vector", lambda e: e.tensor_copy(
                    scanb[:, S2B:S2B + 512], PA[:, 1024:1536]),
                    reads=[RPA[2]], writes=[Rscb])
                for hc in range(64):
                    hb = Bacc[:, hc * 512:(hc + 1) * 512]
                    par = hc % 2
                    ao = 704 if par == 0 else 88
                    sgs = scanb[:, SG2 + par * 512:SG2 + par * 512 + 512]
                    b.op("vector", lambda e, hb=hb, ao=ao: e.tensor_tensor(
                        statx[:, ao:ao + 512], hb, scanb[:, S2B:S2B + 512], ALU.mult),
                        reads=[RB[0], Rscb], writes=[Rstx])
                    b.op("scalar", lambda e, hb=hb, sgs=sgs: e.activation(
                        sgs, hb, AF.Sign),
                        reads=[RB[0]], writes=[Rscb])
                    b.op("scalar", lambda e, ao=ao: e.activation(
                        statx[:, ao:ao + 512], statx[:, ao:ao + 512], AF.Abs),
                        reads=[Rstx], writes=[Rstx])
                    round_min_sign(hb, RB[0], 127.0, sgs, Rscb, 512, par=par)
                for fq in range(4):
                    for kc2 in range(64):
                        wk = kc2 % 2
                        wload(Q(0, wk * 512, wk * 512 + 512)
                              .rearrange("p (o f) -> p o f", f=512),
                              wq2t_s[kc2 * 128:(kc2 + 1) * 128,
                                     fq * 512:(fq + 1) * 512]
                              .rearrange("(o p) f -> p o f", p=128), RQ[wk])
                        for ts in range(4):
                            b.op("tensor", lambda e, kc2=kc2, ts=ts, wk=wk: e.matmul(
                                PAb(ts),
                                Bacc[:, kc2 * 512 + ts * 128: kc2 * 512 + (ts + 1) * 128],
                                Q(0, wk * 512, wk * 512 + 512),
                                start=(kc2 == 0), stop=(kc2 == 63)),
                                reads=[RB[0], RQ[wk]], writes=[RPA[ts]])
                    for ts in range(4):
                        i = g * 4 + ts
                        b.dma("gpsimd", lambda e, i=i, fq=fq, ts=ts: e.dma_start(
                            D1[:, ts * 512:(ts + 1) * 512],
                            xa_s[i * 128:(i + 1) * 128, fq * 512:(fq + 1) * 512]),
                            writes=[RD1])
                        st_ = 704 if ts % 2 == 0 else 192
                        b.op("vector", lambda e, ts=ts, fq=fq, st_=st_: e.tensor_tensor(
                            statx[:, st_:st_ + 512], PAb(ts),
                            C1[:, 8192 + fq * 512:8192 + (fq + 1) * 512], ALU.mult),
                            reads=[RPA[ts], RQ[1]], writes=[Rstx])
                        b.op("vector", lambda e, ts=ts, st_=st_: e.scalar_tensor_tensor(
                            statx[:, st_:st_ + 512], statx[:, st_:st_ + 512],
                            stat[:, 84 + ts:85 + ts], D1[:, ts * 512:(ts + 1) * 512],
                            ALU.mult, ALU.add),
                            reads=[Rstx, Rstat, RD1], writes=[Rstx])
                        b.dma("sync", lambda e, i=i, fq=fq, st_=st_: e.dma_start(
                            out[i * 128:(i + 1) * 128, fq * 512:(fq + 1) * 512],
                            statx[:, st_:st_ + 512]),
                            reads=[Rstx], writes=[Rout])

        PH = {"ln": 1, "kv": 2, "qo1": 3, "qo2": 4, "qo": 5, "rt": 6,
              "moe1": 7, "moe": 8, "x1": 9, "l1p": 10, "l1s": 11, "l1a": 12,
              "full": 99}
        lim = PH.get(upto, 99)
        load_consts()
        phase_ln(dr["x0"], s1_track=False)
        if lim >= 2:
            b.phase_barrier()
            phase_l0_kv()
        if lim >= 3:
            b.phase_barrier()
            phase_l0_qo(sub=min(lim - 2, 3))
        if lim >= 6:
            b.phase_barrier()
            phase_router()
        if lim == 7:
            b.phase_barrier()
            phase_l0_moe(nexp=1)
        elif lim >= 8:
            b.phase_barrier()
            phase_l0_moe()
        if lim >= 9:
            b.phase_barrier()
            phase_x1_ln()
        if lim >= 10:
            b.phase_barrier()
            phase_l1_proj()
        if lim >= 11:
            b.phase_barrier()
            phase_l1_scan()
        if lim >= 12:
            b.phase_barrier()
            phase_l1_wo()
        if lim >= 13:
            b.phase_barrier()
            phase_prep_wq()
            b.phase_barrier()
            phase_l1_ffn()

        if dump == "wq1":
            for i in range(NT):
                b.dma("gpsimd", lambda e, i=i: e.dma_start(
                    out[i * 128:(i + 1) * 128, :],
                    wq1t_s[i * 128:(i + 1) * 128, 0:2048]),
                    reads=[Rscr["wq1t_s"]], writes=[Rout])
        elif dump == "wq2":
            for i in range(NT):
                b.dma("gpsimd", lambda e, i=i: e.dma_start(
                    out[i * 128:(i + 1) * 128, :],
                    wq2t_s[i * 128:(i + 1) * 128, 0:2048]),
                    reads=[Rscr["wq2t_s"]], writes=[Rout])
        elif dump == "y1d":
            for c in range(16):
                b.op("vector", lambda e, c=c: e.tensor_copy(
                    D1[:, :], A[:, c * 2048:(c + 1) * 2048]),
                    reads=[RA[c]], writes=[RD1])
                b.dma("gpsimd", lambda e, c=c: e.dma_start(
                    out[c * 128:(c + 1) * 128, :], D1[:, :]),
                    reads=[RD1], writes=[Rout])
        elif dump == "x1":
            for i in range(NT):
                b.dma("gpsimd", lambda e, i=i: e.dma_start(
                    out[i * 128:(i + 1) * 128, :], x1_s[i * 128:(i + 1) * 128, :]),
                    reads=[Rscr["x1_s"]], writes=[Rout])
        elif dump == "l1a":
            for i in range(NT):
                b.dma("gpsimd", lambda e, i=i: e.dma_start(
                    out[i * 128:(i + 1) * 128, :], xa_s[i * 128:(i + 1) * 128, :]),
                    reads=[Rscr["xa_s"]], writes=[Rout])
        elif dump in ("acc", "moe"):
            for i in range(NT):
                b.op("vector", lambda e, i=i: e.tensor_copy(
                    D1[:, :], Bacc[:, i * 2048:(i + 1) * 2048]),
                    reads=[RB[i]], writes=[RD1])
                b.dma("gpsimd", lambda e, i=i: e.dma_start(
                    out[i * 128:(i + 1) * 128, :], D1[:, :]),
                    reads=[RD1], writes=[Rout])
        elif dump == "xn":
            for c in range(16):
                b.op("vector", lambda e, c=c: e.tensor_copy(
                    D1[:, :], A[:, c * 2048:(c + 1) * 2048]),
                    reads=[RA[c]], writes=[RD1])
                b.dma("gpsimd", lambda e, c=c: e.dma_start(
                    out[c * 128:(c + 1) * 128, :], D1[:, :]),
                    reads=[RD1], writes=[Rout])

        tok = b.barrier_token()
        b.emit("sync", lambda e: e.nop(), extra_waits=tok)
        b.run(sems)
    return nc



def make_in_maps(inputs):
    x = np.ascontiguousarray(inputs["L0_attn_hidden_states"])
    maps = []
    common = {
        "l0_wq": np.ascontiguousarray(inputs["L0_attn_Wq"]),
        "l0_wk": np.ascontiguousarray(inputs["L0_attn_Wk"]),
        "l0_wv": np.ascontiguousarray(inputs["L0_attn_Wv"]),
        "l0_wo": np.ascontiguousarray(inputs["L0_attn_Wo"]),
        "l0_bo": np.ascontiguousarray(inputs["L0_attn_bo"]).reshape(1, DM),
        "l0_router": np.ascontiguousarray(inputs["L0_ffn_router_w"]),
        "l0_gate_up": np.ascontiguousarray(inputs["L0_ffn_gate_up_w"]),
        "l0_down": np.ascontiguousarray(inputs["L0_ffn_down_w"]),
        "l0_sh_gate": np.ascontiguousarray(inputs["L0_ffn_shared_gate_w"]),
        "l0_sh_up": np.ascontiguousarray(inputs["L0_ffn_shared_up_w"]),
        "l0_sh_down": np.ascontiguousarray(inputs["L0_ffn_shared_down_w"]),
        "l1_wq": np.ascontiguousarray(inputs["L1_attn_Wq"]),
        "l1_wk": np.ascontiguousarray(inputs["L1_attn_Wk"]),
        "l1_wv": np.ascontiguousarray(inputs["L1_attn_Wv"]),
        "l1_wo": np.ascontiguousarray(inputs["L1_attn_Wo"]),
        "l1_wa": np.ascontiguousarray(inputs["L1_attn_Wa"]),
        "l1_wb": np.ascontiguousarray(inputs["L1_attn_Wb"]),
        "l1_qcw": np.ascontiguousarray(inputs["L1_attn_qcw"]),
        "l1_qcb": np.ascontiguousarray(inputs["L1_attn_qcb"]).reshape(1, DM),
        "l1_kcw": np.ascontiguousarray(inputs["L1_attn_kcw"]),
        "l1_kcb": np.ascontiguousarray(inputs["L1_attn_kcb"]).reshape(1, DM),
        "l1_vcw": np.ascontiguousarray(inputs["L1_attn_vcw"]),
        "l1_vcb": np.ascontiguousarray(inputs["L1_attn_vcb"]).reshape(1, 4096),
        "l1_w1": np.ascontiguousarray(inputs["L1_ffn_w1"]),
        "l1_w2": np.ascontiguousarray(inputs["L1_ffn_w2"]),
    }
    for i in range(8):
        m = dict(common)
        m["x0"] = np.ascontiguousarray(x[i])
        maps.append(m)
    return maps


def kernel(**inputs):
    nc = build_nc()
    maps = make_in_maps(inputs)
    res = run_bass_kernel_spmd(nc, maps, core_ids=list(range(8)))
    return np.stack([np.asarray(res.results[i]["out"], dtype=np.float32)
                     for i in range(8)], axis=0)


# version 04
# speedup vs baseline: 1.1775x; 1.1775x over previous
import contextlib
import numpy as np
import ml_dtypes
import concourse.bass as bass
import concourse.mybir as mybir
from concourse.bass_utils import run_bass_kernel_spmd

F32 = mybir.dt.float32
BF16 = mybir.dt.bfloat16
I32 = mybir.dt.int32
I16 = mybir.dt.int16
AF = mybir.ActivationFunctionType
ALU = mybir.AluOpType
AXX = mybir.AxisListType.X

T_SEQ = 2048
DM = 2048
NT = T_SEQ // 128

ENGINES = ("tensor", "vector", "scalar", "sync", "gpsimd")


class Res:
    __slots__ = ("name", "last_w", "readers")

    def __init__(self, name):
        self.name = name
        self.last_w = None
        self.readers = []


class B:

    def __init__(self, nc):
        self.nc = nc
        self.q = {e: [] for e in ENGINES}
        self.cnt = {}
        self.watermark = {e: {} for e in ENGINES}
        self.dma_idx = {"sync": 0, "gpsimd": 0, "scalar": 0}
        self.pending = {e: None for e in ENGINES}

    def sem(self, name):
        self.cnt[name] = 0

    def _waits(self, eng, toks):
        out = {}
        for t in toks:
            if t is None:
                continue
            s, v = t
            if v <= self.watermark[eng].get(s, 0):
                continue
            out[s] = max(out.get(s, 0), v)
        for s, v in out.items():
            self.watermark[eng][s] = v
        return list(out.items())

    def phase_barrier(self):
        toks = self.barrier_token()
        for e in ENGINES:
            self.pending[e] = toks

    def emit(self, eng, fn, reads=(), writes=(), sem=None, inc=0, extra_waits=()):
        toks = list(extra_waits)
        if self.pending[eng]:
            toks.extend(self.pending[eng])
            self.pending[eng] = None
        for r in reads:
            toks.append(r.last_w)
        for r in writes:
            toks.append(r.last_w)
            toks.extend(r.readers)
        if eng == "tensor":
            toks = [t for t in toks if t is None or t[0] != "t_tensor"]
        waits = self._waits(eng, toks)
        tok = None
        if inc:
            self.cnt[sem] += inc
            tok = (sem, self.cnt[sem])
        self.q[eng].append((fn, waits, sem if inc else None, inc))
        for r in reads:
            if tok is not None:
                r.readers.append(tok)
        for r in writes:
            r.last_w = tok
            r.readers = []
        return tok

    def op(self, eng, fn, reads=(), writes=(), extra_waits=()):
        return self.emit(eng, fn, reads, writes, sem="t_" + eng, inc=1,
                         extra_waits=extra_waits)

    NDMA = 8

    def dma(self, queue, fn, reads=(), writes=(), extra_waits=()):
        idx = self.dma_idx[queue]
        self.dma_idx[queue] = (idx + 1) % self.NDMA
        sem = f"d_{queue}{idx}"
        prev = (sem, self.cnt[sem]) if self.cnt[sem] > 0 else None
        ew = list(extra_waits) + ([prev] if prev else [])
        return self.emit(queue, fn, reads, writes, sem=sem, inc=16,
                         extra_waits=ew)

    def barrier_token(self):
        return [(s, v) for s, v in self.cnt.items() if v > 0]

    def run(self, sems):
        nc = self.nc
        with nc.Block() as blk:
            for eng in ENGINES:
                stream = self.q[eng]
                if not stream:
                    continue

                def make(eng, stream):
                    def body(e):
                        for fn, waits, sname, inc in stream:
                            for ws, wv in waits:
                                e.wait_ge(sems[ws], wv)
                            inst = fn(e)
                            if sname is not None:
                                inst.then_inc(sems[sname], inc)
                    return body

                getattr(blk, eng)(make(eng, stream))


SEM_NAMES = (["t_tensor", "t_vector", "t_scalar", "t_sync", "t_gpsimd"]
             + [f"d_sync{i}" for i in range(B.NDMA)]
             + [f"d_gpsimd{i}" for i in range(B.NDMA)]
             + [f"d_scalar{i}" for i in range(B.NDMA)])


def build_nc(upto="full", dump=None):
    nc = bass.Bass("TRN2", target_bir_lowering=False)

    dr = {}

    def din(name, shape):
        dr[name] = nc.dram_tensor(name, list(shape), F32, kind="ExternalInput").ap()

    din("x0", (T_SEQ, DM))
    din("l0_wq", (DM, DM)); din("l0_wk", (DM, DM)); din("l0_wv", (DM, DM))
    din("l0_wo", (DM, DM)); din("l0_bo", (1, DM))
    din("l0_router", (DM, 8))
    din("l0_gate_up", (8, DM, DM)); din("l0_down", (8, 1024, DM))
    din("l0_sh_gate", (DM, 1024)); din("l0_sh_up", (DM, 1024))
    din("l0_sh_down", (1024, DM))
    din("l1_wq", (DM, DM)); din("l1_wk", (DM, DM)); din("l1_wv", (DM, 4096))
    din("l1_wo", (4096, DM)); din("l1_wa", (DM, 16)); din("l1_wb", (DM, 16))
    din("l1_qcw", (DM, 4)); din("l1_qcb", (1, DM))
    din("l1_kcw", (DM, 4)); din("l1_kcb", (1, DM))
    din("l1_vcw", (4096, 4)); din("l1_vcb", (1, 4096))
    din("l1_w1", (8192, DM)); din("l1_w2", (DM, 8192))

    out = nc.dram_tensor("out", [T_SEQ, DM], F32, kind="ExternalOutput").ap()

    dr["ident"] = nc.inline_tensor(np.eye(128, dtype=ml_dtypes.bfloat16), "identb").ap()
    dr["identf"] = nc.inline_tensor(np.eye(128, dtype=np.float32), "identf").ap()
    msu = -np.triu(np.ones((128, 128), dtype=np.float32), 1)
    mui = np.triu(np.ones((128, 128), dtype=np.float32))
    dr["mask_su"] = nc.inline_tensor(msu.astype(ml_dtypes.bfloat16), "mask_su").ap()
    dr["mask_uif"] = nc.inline_tensor(mui.astype(np.float32), "mask_uif").ap()
    dr["mask_uib"] = nc.inline_tensor(mui.astype(ml_dtypes.bfloat16), "mask_uib").ap()
    dr["ones_row"] = nc.inline_tensor(np.ones((1, 128), dtype=np.float32), "ones_row").ap()
    dr["ones_rowb"] = nc.inline_tensor(np.ones((1, 128), dtype=ml_dtypes.bfloat16), "ones_rowb").ap()
    dr["ones_colb"] = nc.inline_tensor(np.ones((128, 1), dtype=ml_dtypes.bfloat16), "ones_colb").ap()

    x1_s = nc.dram_tensor("x1_s", [T_SEQ, DM], F32, kind="Internal").ap()
    q_s = nc.dram_tensor("q_s", [DM, T_SEQ], BF16, kind="Internal").ap()
    k_s = nc.dram_tensor("k_s", [DM, T_SEQ], BF16, kind="Internal").ap()
    v_s = nc.dram_tensor("v_s", [4096, T_SEQ], BF16, kind="Internal").ap()
    outs_s = nc.dram_tensor("outs_s", [4096, T_SEQ], BF16, kind="Internal").ap()
    o1_s = nc.dram_tensor("o1_s", [DM, T_SEQ], BF16, kind="Internal").ap()
    wq1t_s = nc.dram_tensor("wq1t_s", [DM, 8192], BF16, kind="Internal").ap()
    wq2t_s = nc.dram_tensor("wq2t_s", [8192, DM], BF16, kind="Internal").ap()
    xa_s = nc.dram_tensor("xa_s", [T_SEQ, DM], F32, kind="Internal").ap()

    es = contextlib.ExitStack()
    with es:
        A = es.enter_context(nc.sbuf_tensor("A", [128, NT * DM], BF16))
        Bacc = es.enter_context(nc.sbuf_tensor("Bacc", [128, NT * DM], BF16))
        C1 = es.enter_context(nc.sbuf_tensor("C1", [128, 24576], BF16))
        D1 = es.enter_context(nc.sbuf_tensor("D1", [128, 2048], F32))
        D2 = es.enter_context(nc.sbuf_tensor("D2", [128, 4096], BF16))
        ITL = es.enter_context(nc.sbuf_tensor("ITL", [128, 1024], I16))
        idb = es.enter_context(nc.sbuf_tensor("idb", [128, 128], BF16))
        idf = es.enter_context(nc.sbuf_tensor("idf", [128, 128], F32))
        msut = es.enter_context(nc.sbuf_tensor("msut", [128, 128], BF16))
        muift = es.enter_context(nc.sbuf_tensor("muift", [128, 128], F32))
        onesr = es.enter_context(nc.sbuf_tensor("onesr", [1, 128], F32))
        onesrb = es.enter_context(nc.sbuf_tensor("onesrb", [1, 128], BF16))
        stat = es.enter_context(nc.sbuf_tensor("stat", [128, 96], F32))
        wsm = es.enter_context(nc.sbuf_tensor("wsm", [128, 128], BF16))
        rw = es.enter_context(nc.sbuf_tensor("rw", [128, 128], F32))
        smallf = es.enter_context(nc.sbuf_tensor("smallf", [128, 64], F32))
        cst = es.enter_context(nc.sbuf_tensor("cst", [128, 8], F32))
        hwrk = es.enter_context(nc.sbuf_tensor("hwrk", [128, 256], BF16))
        muibt = es.enter_context(nc.sbuf_tensor("muibt", [128, 128], BF16))
        onescb = es.enter_context(nc.sbuf_tensor("onescb", [128, 1], BF16))
        statx = es.enter_context(nc.sbuf_tensor("statx", [128, 1216], F32))
        scanb = es.enter_context(nc.sbuf_tensor("scanb", [128, 1920], BF16))
        bT = es.enter_context(nc.sbuf_tensor("bT", [128, 256], BF16))
        PA = es.enter_context(nc.psum_tensor("PA", [128, 2048], F32))
        PBp = es.enter_context(nc.psum_tensor("PBp", [128, 1024], F32))
        PT = es.enter_context(nc.psum_tensor("PT", [128, 1024], BF16))
        PT2 = es.enter_context(nc.psum_tensor("PT2", [128, 1024], BF16))
        sems = {n: es.enter_context(nc.semaphore(n)) for n in SEM_NAMES}

        b = B(nc)
        for n in SEM_NAMES:
            b.sem(n)

        RA = [Res(f"A{i}") for i in range(NT)]
        RB = [Res(f"B{i}") for i in range(NT)]
        RQ = [Res(f"Q{i}") for i in range(3)]
        RO = [Res(f"O{i}") for i in range(2)]
        RD1, RD2, RIT = Res("D1"), Res("D2"), Res("ITL")
        RIT1 = Res("ITL1")
        Rstat, Rrw = Res("stat"), Res("rw")
        Rsm, Rbob, Rwsm = Res("smallf"), Res("bob"), Res("wsm")
        Rhw = Res("hwrk")
        Rhw1 = Res("hwrk1")
        RHF = [Res(f"hf{i}") for i in range(8)]
        Rconst = Res("const")
        Rout = Res("out")
        Rscr = {n: Res(n) for n in
                ("x1_s", "q_s", "k_s", "v_s", "outs_s", "wq1t_s", "wq2t_s",
                 "o1_s", "xa_s")}
        RPA = [Res(f"PA{i}") for i in range(4)]
        RPB = [Res(f"PB{i}") for i in range(2)]
        RPT, RPT2 = Res("PT"), Res("PT2")

        def PAb(i):
            return PA[:, i * 512:(i + 1) * 512]

        def Q(k, lo, hi):
            return C1[:, k * 8192 + lo: k * 8192 + hi]

        Rcst = Res("cst")
        Rstx, Rscb, RbT = Res("statx"), Res("scanb"), Res("bT")
        CEPS, CHALF, CNEG1, CONE = (lambda c: None), None, None, None

        def cstc(i):
            return cst[:, i:i + 1]

        def load_consts():
            b.op("vector", lambda e: e.memset(cst[:, 0:1], 1e-5), writes=[Rcst])
            b.op("vector", lambda e: e.memset(cst[:, 1:2], 0.5), writes=[Rcst])
            b.op("vector", lambda e: e.memset(cst[:, 2:3], -1.0), writes=[Rcst])
            b.op("vector", lambda e: e.memset(cst[:, 3:4], 1.0), writes=[Rcst])
            b.dma("sync", lambda e: e.dma_start(muibt[:], dr["mask_uib"][:, :]), writes=[Rconst])
            b.dma("sync", lambda e: e.dma_start(onescb[:], dr["ones_colb"][:, :]), writes=[Rconst])
            b.dma("sync", lambda e: e.dma_start(idb[:], dr["ident"][:, :]), writes=[Rconst])
            b.dma("sync", lambda e: e.dma_start(idf[:], dr["identf"][:, :]), writes=[Rconst])
            b.dma("sync", lambda e: e.dma_start(msut[:], dr["mask_su"][:, :]), writes=[Rconst])
            b.dma("sync", lambda e: e.dma_start(muift[:], dr["mask_uif"][:, :]), writes=[Rconst])
            b.dma("sync", lambda e: e.dma_start(onesr[:], dr["ones_row"][:, :]), writes=[Rconst])
            b.dma("sync", lambda e: e.dma_start(onesrb[:], dr["ones_rowb"][:, :]), writes=[Rconst])
            b.dma("gpsimd", lambda e: e.dma_start(
                wsm[:, 0:128].rearrange("p (c e) -> p c e", e=8),
                dr["l0_router"].rearrange("(c p) e -> p c e", p=128)),
                writes=[Rwsm])

        def transpose(dst_ap, src_ap, rdst, rsrc, use2=False):
            pr = RPT2 if use2 else RPT
            pap = (PT2 if use2 else PT)[:, 0:128]
            b.op("tensor", lambda e: e.transpose(pap, src_ap, idb[:]),
                 reads=[rsrc, Rconst], writes=[pr])
            b.op("vector", lambda e: e.tensor_copy(dst_ap, pap),
                 reads=[pr], writes=[rdst])

        def wload(dst_ap, src_ap, wres, extra=()):
            if not isinstance(wres, (list, tuple)):
                wres = [wres]
            return b.dma("gpsimd", lambda e: e.dma_start(dst_ap, src_ap),
                         writes=wres, extra_waits=extra)

        def phase_ln(xin_ap, s1_track):
            for i in range(NT):
                b.dma("sync", lambda e, i=i: e.dma_start(
                    D1[:, :], xin_ap[i * 128:(i + 1) * 128, :]), writes=[RD1])
                b.op("vector", lambda e: e.reduce_sum(stat[:, 0:1], D1[:, :], AXX),
                     reads=[RD1], writes=[Rstat])
                b.op("scalar", lambda e: e.activation(
                    D2[:, 0:2048], D1[:, :], AF.Square, accum_out=stat[:, 1:2]),
                    reads=[RD1], writes=[RD2, Rstat])
                b.op("vector", lambda e: e.tensor_scalar(
                    stat[:, 2:3], stat[:, 0:1], 1.0 / DM, None, ALU.mult),
                    reads=[Rstat], writes=[Rstat])
                b.op("vector", lambda e: e.tensor_scalar(
                    stat[:, 3:4], stat[:, 1:2], 1.0 / DM, None, ALU.mult),
                    reads=[Rstat], writes=[Rstat])
                b.op("vector", lambda e: e.scalar_tensor_tensor(
                    stat[:, 4:5], stat[:, 2:3], stat[:, 2:3], stat[:, 3:4],
                    ALU.mult, ALU.subtract),
                    reads=[Rstat], writes=[Rstat])
                b.op("scalar", lambda e: e.activation(
                    stat[:, 5:6], stat[:, 4:5], AF.Sqrt, bias=cstc(0), scale=cstc(2)),
                    reads=[Rstat, Rcst], writes=[Rstat])
                b.op("vector", lambda e: e.reciprocal(stat[:, 6:7], stat[:, 5:6]),
                     reads=[Rstat], writes=[Rstat])
                b.op("vector", lambda e: e.tensor_scalar(
                    D2[:, 0:2048], D1[:, :], stat[:, 2:3], stat[:, 6:7],
                    ALU.subtract, ALU.mult),
                    reads=[RD1, Rstat], writes=[RD2])
                if s1_track:
                    b.op("vector", lambda e, i=i: e.reduce_max(
                        stat[:, 32 + i:33 + i], D2[:, 0:2048], AXX,
                        apply_absolute_value=True),
                        reads=[RD2], writes=[Rstat])
                for c in range(16):
                    transpose(A[:, c * 2048 + i * 128: c * 2048 + (i + 1) * 128],
                              D2[:, c * 128:(c + 1) * 128],
                              RA[c], RD2, use2=(c % 2 == 1))

        def proj_modeT(wdram, fdim, evac):
            nq = fdim // 512
            for fq in range(nq):
                k = fq % 2
                wr = RQ[k]
                wload(Q(k, 0, 8192).rearrange("p (c f) -> p c f", f=512),
                      wdram[:, fq * 512:(fq + 1) * 512]
                      .rearrange("(c p) f -> p c f", p=128), wr)
                for i in range(NT):
                    bank = RPA[i % 2]
                    bap = PAb(i % 2)
                    for kc in range(16):
                        b.op("tensor", lambda e, i=i, kc=kc, k=k, bap=bap: e.matmul(
                            bap,
                            A[:, kc * 2048 + i * 128: kc * 2048 + (i + 1) * 128],
                            Q(k, kc * 512, (kc + 1) * 512),
                            start=(kc == 0), stop=(kc == 15)),
                            reads=[RA[kc], wr], writes=[bank])
                    evac(i, fq, bap, bank)

        def phase_l0_kv():
            def evk(i, fq, bap, bank):
                b.op("scalar", lambda e, i=i, fq=fq, bap=bap: e.activation(
                    Bacc[:, i * 2048 + fq * 512: i * 2048 + (fq + 1) * 512],
                    bap, AF.Relu),
                    reads=[bank], writes=[RB[i]])
            proj_modeT(dr["l0_wk"], DM, evk)

            b.op("vector", lambda e: e.memset(
                D2[:, 0:520].rearrange("p (h c) -> p h c", c=65)[:, :, 64:65], 1.0),
                writes=[RD2])

            kv_first = {}

            def kvslot(j):
                if j < 7:
                    return RPB[0], PBp[:, j * 65:(j + 1) * 65]
                if j < 14:
                    return RPB[1], PBp[:, 512 + (j - 7) * 65: 512 + (j - 6) * 65]
                return RPA[3], PA[:, 1536 + (j - 14) * 65: 1536 + (j - 13) * 65]

            def evv(i, fq, bap, bank):
                b.op("vector", lambda e, bap=bap: e.tensor_copy(
                    D2[:, 0:520].rearrange("p (h c) -> p h c", c=65)[:, :, 0:64],
                    bap.rearrange("p (h c) -> p h c", c=64)),
                    reads=[bank], writes=[RD2])
                for hh in range(4):
                    h0 = 8 * fq + 2 * hh
                    j = h0 // 2
                    kr, kap = kvslot(j)
                    for half in range(2):
                        h = h0 + half
                        po = 64 * half
                        first = (id(kr), po) not in kv_first
                        kv_first[(id(kr), po)] = True
                        b.op("tensor",
                             lambda e, i=i, h=h, hh=hh, half=half, kap=kap, po=po, first=first: e.matmul(
                                 kap[po:po + 64, :],
                                 Bacc[:, i * 2048 + h * 64: i * 2048 + (h + 1) * 64],
                                 D2[:, (2 * hh + half) * 65:(2 * hh + half + 1) * 65],
                                 start=first, stop=(i == NT - 1),
                                 skip_group_check=True,
                                 tile_position=(0, po)),
                             reads=[RB[i], RD2], writes=[kr])
            proj_modeT(dr["l0_wv"], DM, evv)
            for j in range(16):
                kr, kap = kvslot(j)
                b.op("vector", lambda e, kap=kap, j=j: e.tensor_copy(
                    scanb[:, j * 65:(j + 1) * 65], kap),
                    reads=[kr], writes=[Rscb])

        def phase_l0_qo(sub=3):
            b.dma("gpsimd", lambda e: e.dma_start(D2[0:1, 0:2048], dr["l0_bo"][0:1, :]),
                  writes=[RD2])
            for fb in range(4):
                b.op("tensor", lambda e, fb=fb: e.matmul(
                    PAb(0), onesrb[:, :], D2[0:1, fb * 512:(fb + 1) * 512],
                    start=True, stop=True),
                    reads=[RD2, Rconst], writes=[RPA[0]])
                b.op("vector", lambda e, fb=fb: e.tensor_copy(
                    C1[:, 22528:24576][:, fb * 512:(fb + 1) * 512], PAb(0)),
                    reads=[RPA[0]], writes=[Rbob])

            def oslot(ts, half):
                lo = half * 512 + ts * 65
                return RPB[half], PBp[:, lo:lo + 65]

            if sub < 2:
                return
            for fq in range(4):
                k = fq % 2
                wr = RQ[k]
                wload(Q(k, 0, 8192).rearrange("p (c f) -> p c f", f=512),
                      dr["l0_wq"][:, fq * 512:(fq + 1) * 512]
                      .rearrange("(c p) f -> p c f", p=128), wr)
                for fc in range(4):
                    h0 = fq * 8 + fc * 2
                    j = h0 // 2
                    for tg in range(4):
                        bank = RPA[tg % 2]
                        bap = PAb(tg % 2)
                        for kc in range(16):
                            b.op("tensor", lambda e, fc=fc, tg=tg, kc=kc, k=k, bap=bap: e.matmul(
                                bap,
                                Q(k, kc * 512 + fc * 128, kc * 512 + (fc + 1) * 128),
                                A[:, kc * 2048 + tg * 512:kc * 2048 + (tg + 1) * 512],
                                start=(kc == 0), stop=(kc == 15)),
                                reads=[RA[kc], wr], writes=[bank])
                        b.op("scalar", lambda e, bap=bap: e.activation(
                            D2[:, 1024:1536], bap, AF.Relu),
                            reads=[bank], writes=[RD2])
                        for half in range(2):
                            po = 64 * half
                            for ts in range(4):
                                sr, sap = oslot(ts, half)
                                b.op("tensor", lambda e, ts=ts, j=j, po=po, sap=sap: e.matmul(
                                    sap,
                                    D2[po:po + 64, 1024 + ts * 128:1024 + (ts + 1) * 128],
                                    scanb[po:po + 64, j * 65:(j + 1) * 65],
                                    start=True, stop=True, skip_group_check=True),
                                    reads=[RD2, Rscb], writes=[sr])
                        for g in range(2):
                            b.op("vector", lambda e, g=g: e.tensor_scalar_max(
                                smallf[:, g * 4:(g + 1) * 4],
                                PBp[:, g * 512 + 64: g * 512 + 64 + 260]
                                .rearrange("p (s c) -> p s c", c=65)[:, :, 0:1],
                                1e-6),
                                reads=[RPB[g]], writes=[Rsm])
                        b.op("vector", lambda e: e.reciprocal(
                            smallf[:, 8:16], smallf[:, 0:8]),
                            reads=[Rsm], writes=[Rsm])
                        for ts in range(4):
                            for half in range(2):
                                sr, sap = oslot(ts, half)
                                b.op("vector", lambda e, ts=ts, half=half, sap=sap: e.tensor_scalar(
                                    D2[:, 2048 + ts * 128 + half * 64:
                                        2048 + ts * 128 + half * 64 + 64],
                                    sap[:, 0:64],
                                    smallf[:, 8 + half * 4 + ts:9 + half * 4 + ts],
                                    None, ALU.mult),
                                    reads=[sr, Rsm], writes=[RD2])
                        for ts in range(4):
                            transpose(D2[:, 2560 + ts * 128:2560 + (ts + 1) * 128],
                                      D2[:, 2048 + ts * 128:2048 + (ts + 1) * 128],
                                      RD2, RD2, use2=(ts % 2 == 1))
                        b.dma("sync", lambda e, h0=h0, tg=tg: e.dma_start(
                            o1_s[h0 * 64:h0 * 64 + 128,
                                 tg * 512:(tg + 1) * 512],
                            D2[:, 2560:3072]),
                            reads=[RD2], writes=[Rscr["o1_s"]])
            if sub < 3:
                return
            for fh in range(2):
                wr = [RQ[0], RQ[1]]
                wload(C1[:, 0:16384].rearrange("p (c f) -> p c f", f=1024),
                      dr["l0_wo"][:, fh * 1024:(fh + 1) * 1024]
                      .rearrange("(c p) f -> p c f", p=128), wr)
                for i in range(NT):
                    ob = RO[i % 2]
                    ocol = 16384 + (i % 2) * 2048
                    b.dma("sync", lambda e, i=i, ocol=ocol: e.dma_start(
                        C1[:, ocol:ocol + 2048].rearrange("p (c f) -> p c f", f=128),
                        o1_s[:, i * 128:(i + 1) * 128]
                        .rearrange("(c p) f -> p c f", p=128)),
                        reads=[Rscr["o1_s"]], writes=[ob])
                    for fb in range(2):
                        bank = RPA[2 * (i % 2) + fb]
                        bap = PAb(2 * (i % 2) + fb)
                        for c in range(16):
                            b.op("tensor", lambda e, c=c, fb=fb, bap=bap, ocol=ocol: e.matmul(
                                bap, C1[:, ocol + c * 128:ocol + (c + 1) * 128],
                                C1[:, c * 1024 + fb * 512:c * 1024 + (fb + 1) * 512],
                                start=(c == 0), stop=(c == 15)),
                                reads=[ob] + wr, writes=[bank])
                        b.op("vector", lambda e, i=i, fh=fh, fb=fb, bap=bap: e.tensor_tensor(
                            Bacc[:, i * 2048 + fh * 1024 + fb * 512:
                                 i * 2048 + fh * 1024 + (fb + 1) * 512],
                            bap, C1[:, 22528:24576][:, fh * 1024 + fb * 512:
                                      fh * 1024 + (fb + 1) * 512],
                            ALU.add),
                            reads=[bank, Rbob], writes=[RB[i]])

        def phase_router():
            for i in range(NT):
                bap = PAb(i % 2)
                bank = RPA[i % 2]
                for kc in range(16):
                    b.op("tensor", lambda e, i=i, kc=kc, bap=bap: e.matmul(
                        bap[:, 0:8],
                        A[:, kc * 2048 + i * 128: kc * 2048 + (i + 1) * 128],
                        wsm[:, kc * 8:(kc + 1) * 8],
                        start=(kc == 0), stop=(kc == 15)),
                        reads=[RA[kc], Rwsm], writes=[bank])
                lg = smallf[:, 16:24]
                b.op("vector", lambda e, bap=bap: e.tensor_copy(lg, bap[:, 0:8]),
                     reads=[bank], writes=[Rsm])
                b.op("vector", lambda e: e.reduce_max(smallf[:, 24:25], lg, AXX),
                     reads=[Rsm], writes=[Rsm])
                b.op("vector", lambda e: e.tensor_scalar(
                    smallf[:, 32:40], lg, smallf[:, 24:25], None, ALU.is_ge),
                    reads=[Rsm], writes=[Rsm])
                b.op("vector", lambda e: e.scalar_tensor_tensor(
                    smallf[:, 40:48], smallf[:, 32:40], -1e30, lg,
                    ALU.mult, ALU.add),
                    reads=[Rsm], writes=[Rsm])
                b.op("vector", lambda e: e.reduce_max(
                    smallf[:, 25:26], smallf[:, 40:48], AXX),
                    reads=[Rsm], writes=[Rsm])
                b.op("vector", lambda e: e.tensor_scalar(
                    smallf[:, 32:40], lg, smallf[:, 25:26], None, ALU.is_ge),
                    reads=[Rsm], writes=[Rsm])
                b.op("scalar", lambda e: e.activation(
                    smallf[:, 40:48], lg, AF.Sigmoid),
                    reads=[Rsm], writes=[Rsm])
                b.op("vector", lambda e, i=i: e.tensor_tensor(
                    rw[:, i * 8:(i + 1) * 8], smallf[:, 40:48], smallf[:, 32:40],
                    ALU.mult),
                    reads=[Rsm], writes=[Rrw])

        def phase_l0_moe(nexp=9):
            RQD = [Res(f"qd{i}") for i in range(8)]
            RHW = [Res(f"hws{i}") for i in range(4)]
            for e in list(range(8))[:max(0, nexp - 1)] + [8]:
                shared = (e == 8)
                for q in range(4):
                    dsrc = (dr["l0_sh_down"] if shared else dr["l0_down"][e])
                    for fh in range(2):
                        for pc in range(8):
                            b.dma("gpsimd", lambda e_, fh=fh, pc=pc, dsrc=dsrc: e_.dma_start(
                                Q(2, pc * 1024, (pc + 1) * 1024),
                                dsrc[pc * 128:(pc + 1) * 128,
                                     fh * 1024:(fh + 1) * 1024]),
                                writes=[RQD[pc]])
                        if fh == 0:
                            pend_gu = []
                            for fb in range(4):
                                k = fb % 2
                                gud = Q(k, 0, 8192).rearrange(
                                    "p (c f) -> p c f", f=512)
                                if shared:
                                    wload(gud[:, :, 0:256],
                                          dr["l0_sh_gate"][:, fb * 256:(fb + 1) * 256]
                                          .rearrange("(c p) f -> p c f", p=128), RQ[k])
                                    wload(gud[:, :, 256:512],
                                          dr["l0_sh_up"][:, fb * 256:(fb + 1) * 256]
                                          .rearrange("(c p) f -> p c f", p=128), RQ[k])
                                else:
                                    wload(gud[:, :, 0:256],
                                          dr["l0_gate_up"][e][:, fb * 256:(fb + 1) * 256]
                                          .rearrange("(c p) f -> p c f", p=128), RQ[k])
                                    wload(gud[:, :, 256:512],
                                          dr["l0_gate_up"][e][:, 1024 + fb * 256:1024 + (fb + 1) * 256]
                                          .rearrange("(c p) f -> p c f", p=128), RQ[k])
                                for t4 in range(4):
                                    i = q * 4 + t4
                                    bap = PBp[:, (t4 % 2) * 512:(t4 % 2) * 512 + 512]
                                    bank = RPB[t4 % 2]
                                    for kc in range(16):
                                        b.op("tensor", lambda e_, i=i, kc=kc, k=k, bap=bap: e_.matmul(
                                            bap,
                                            A[:, kc * 2048 + i * 128: kc * 2048 + (i + 1) * 128],
                                            Q(k, kc * 512, (kc + 1) * 512),
                                            start=(kc == 0), stop=(kc == 15)),
                                            reads=[RA[kc], RQ[k]], writes=[bank])
                                    if t4 < 3:
                                        hwg = scanb[:, t4 * 512:t4 * 512 + 256]
                                        hwh = scanb[:, t4 * 512 + 256:t4 * 512 + 512]
                                    else:
                                        hwg = hwrk[:, 0:256]
                                        hwh = scanb[:, 1536:1792]
                                    rhw = RHW[t4]
                                    if shared:
                                        b.op("scalar", lambda e_, bap=bap, hwg=hwg: e_.activation(
                                            hwg, bap[:, 0:256], AF.Silu),
                                            reads=[bank], writes=[rhw])
                                        b.op("vector", lambda e_, bap=bap, hwg=hwg, hwh=hwh: e_.tensor_tensor(
                                            hwh, bap[:, 256:512],
                                            hwg, ALU.mult),
                                            reads=[bank, rhw], writes=[rhw])
                                    else:
                                        b.op("scalar", lambda e_, i=i, e=e, bap=bap, hwg=hwg: e_.activation(
                                            hwg, bap[:, 0:256], AF.Silu,
                                            scale=rw[:, i * 8 + e:i * 8 + e + 1]),
                                            reads=[bank, Rrw], writes=[rhw])
                                        b.op("vector", lambda e_, i=i, e=e, bap=bap, hwg=hwg, hwh=hwh: e_.scalar_tensor_tensor(
                                            hwh, bap[:, 256:512],
                                            rw[:, i * 8 + e:i * 8 + e + 1],
                                            hwg, ALU.mult, ALU.mult),
                                            reads=[bank, Rrw, rhw], writes=[rhw])
                                    def mk_tr(fb=fb, t4=t4, hwh=hwh, rhw=rhw):
                                        def tr():
                                            for half in range(2):
                                                ptx = PT if half == 0 else PT2
                                                rptx = RPT if half == 0 else RPT2
                                                b.op("tensor", lambda e_, half=half, hwh=hwh, ptx=ptx: e_.transpose(
                                                    ptx[:, 0:128],
                                                    hwh[:, half * 128:(half + 1) * 128],
                                                    idb[:]),
                                                    reads=[rhw, Rconst], writes=[rptx])
                                                b.op("vector", lambda e_, fb=fb, t4=t4, half=half, ptx=ptx: e_.tensor_copy(
                                                    D2[:, (2 * fb + half) * 512 + t4 * 128:
                                                          (2 * fb + half) * 512 + (t4 + 1) * 128],
                                                    ptx[:, 0:128]),
                                                    reads=[rptx], writes=[RHF[2 * fb + half]])
                                        return tr
                                    pend_gu.append(mk_tr())
                                    if len(pend_gu) > 1:
                                        pend_gu.pop(0)()
                            while pend_gu:
                                pend_gu.pop(0)()
                        for t4 in range(4):
                            i = q * 4 + t4
                            bap = PA[:, (t4 % 2) * 1024:(t4 % 2) * 1024 + 1024]
                            banks = [RPA[(t4 % 2) * 2], RPA[(t4 % 2) * 2 + 1]]
                            for ic in range(8):
                                for fb2 in range(2):
                                    b.op("tensor", lambda e_, ic=ic, fb2=fb2, t4=t4, bap=bap: e_.matmul(
                                        bap[:, fb2 * 512:(fb2 + 1) * 512],
                                        D2[:, ic * 512 + t4 * 128: ic * 512 + (t4 + 1) * 128],
                                        Q(2, ic * 1024 + fb2 * 512, ic * 1024 + (fb2 + 1) * 512),
                                        start=(ic == 0), stop=(ic == 7)),
                                        reads=[RHF[ic], RQD[ic]], writes=[banks[fb2]])
                            b.op("vector", lambda e_, i=i, fh=fh, bap=bap: e_.tensor_tensor(
                                Bacc[:, i * 2048 + fh * 1024: i * 2048 + (fh + 1) * 1024],
                                bap,
                                Bacc[:, i * 2048 + fh * 1024: i * 2048 + (fh + 1) * 1024],
                                ALU.add),
                                reads=banks + [RB[i]], writes=[RB[i]])

        def phase_x1_ln():
            for i in range(NT):
                xcol = (i % 2) * 2048
                xres = RQ[i % 2]
                b.dma("gpsimd", lambda e, i=i, xcol=xcol: e.dma_start(
                    Q(0, xcol, xcol + 2048), dr["x0"][i * 128:(i + 1) * 128, :]),
                    writes=[xres])
                b.op("vector", lambda e, i=i, xcol=xcol: e.tensor_tensor(
                    D1[:, :], Q(0, xcol, xcol + 2048),
                    Bacc[:, i * 2048:(i + 1) * 2048], ALU.add),
                    reads=[xres, RB[i]], writes=[RD1])
                b.dma("sync", lambda e, i=i: e.dma_start(
                    x1_s[i * 128:(i + 1) * 128, :], D1[:, :]),
                    reads=[RD1], writes=[Rscr["x1_s"]])
                b.op("vector", lambda e: e.reduce_sum(stat[:, 0:1], D1[:, :], AXX),
                     reads=[RD1], writes=[Rstat])
                b.op("scalar", lambda e: e.activation(
                    D2[:, 0:2048], D1[:, :], AF.Square, accum_out=stat[:, 1:2]),
                    reads=[RD1], writes=[RD2, Rstat])
                b.op("vector", lambda e: e.tensor_scalar(
                    stat[:, 2:3], stat[:, 0:1], 1.0 / DM, None, ALU.mult),
                    reads=[Rstat], writes=[Rstat])
                b.op("vector", lambda e: e.tensor_scalar(
                    stat[:, 3:4], stat[:, 1:2], 1.0 / DM, None, ALU.mult),
                    reads=[Rstat], writes=[Rstat])
                b.op("vector", lambda e: e.scalar_tensor_tensor(
                    stat[:, 4:5], stat[:, 2:3], stat[:, 2:3], stat[:, 3:4],
                    ALU.mult, ALU.subtract),
                    reads=[Rstat], writes=[Rstat])
                b.op("scalar", lambda e: e.activation(
                    stat[:, 5:6], stat[:, 4:5], AF.Sqrt, bias=cstc(0), scale=cstc(2)),
                    reads=[Rstat, Rcst], writes=[Rstat])
                b.op("vector", lambda e: e.reciprocal(stat[:, 6:7], stat[:, 5:6]),
                     reads=[Rstat], writes=[Rstat])
                b.op("vector", lambda e: e.tensor_scalar(
                    D2[:, 0:2048], D1[:, :], stat[:, 2:3], stat[:, 6:7],
                    ALU.subtract, ALU.mult),
                    reads=[RD1, Rstat], writes=[RD2])
                b.op("vector", lambda e, i=i: e.reduce_max(
                    stat[:, 32 + i:33 + i], D2[:, 0:2048], AXX,
                    apply_absolute_value=True),
                    reads=[RD2], writes=[Rstat])
                for c in range(16):
                    transpose(A[:, c * 2048 + i * 128: c * 2048 + (i + 1) * 128],
                              D2[:, c * 128:(c + 1) * 128],
                              RA[c], RD2, use2=(c % 2 == 1))

        def phase_l1_proj():
            b.dma("sync", lambda e: e.dma_start(
                statx[:, 0:64].rearrange("p (c j) -> p c j", j=4),
                dr["l1_qcw"].rearrange("(c p) j -> p c j", p=128)), writes=[Rstx])
            b.dma("sync", lambda e: e.dma_start(
                statx[:, 64:128].rearrange("p (c j) -> p c j", j=4),
                dr["l1_kcw"].rearrange("(c p) j -> p c j", p=128)), writes=[Rstx])
            b.dma("sync", lambda e: e.dma_start(
                statx[:, 128:256].rearrange("p (c j) -> p c j", j=4),
                dr["l1_vcw"].rearrange("(c p) j -> p c j", p=128)), writes=[Rstx])
            b.dma("sync", lambda e: e.dma_start(
                statx[:, 256:272].rearrange("p (c o) -> p c o", o=1),
                dr["l1_qcb"].rearrange("o (c p) -> p c o", p=128),
                allow_slow_non_contiguous=True), writes=[Rstx])
            b.dma("sync", lambda e: e.dma_start(
                statx[:, 272:288].rearrange("p (c o) -> p c o", o=1),
                dr["l1_kcb"].rearrange("o (c p) -> p c o", p=128),
                allow_slow_non_contiguous=True), writes=[Rstx])
            b.dma("sync", lambda e: e.dma_start(
                statx[:, 288:320].rearrange("p (c o) -> p c o", o=1),
                dr["l1_vcb"].rearrange("o (c p) -> p c o", p=128),
                allow_slow_non_contiguous=True), writes=[Rstx])

            plans = [
                (dr["l1_wq"], q_s, 4, True, 0, 256),
                (dr["l1_wk"], k_s, 4, True, 64, 272),
                (dr["l1_wv"], v_s, 8, False, 128, 288),
            ]
            ZC, T1 = 0, 516
            for wd, scr, nq, l2, cwo, cbo in plans:
                for fq in range(nq):
                    k_ = fq % 2
                    wload(Q(k_, 0, 8192).rearrange("p (c f) -> p c f", f=512),
                          wd[:, fq * 512:(fq + 1) * 512]
                          .rearrange("(c p) f -> p c f", p=128), RQ[k_])
                    for fc in range(4):
                        ch = fq * 4 + fc
                        b.op("vector", lambda e: e.memset(
                            scanb[:, ZC:ZC + 3], 0.0), writes=[Rscb])
                        for tg in range(4):
                            bank = RPA[tg % 2]
                            bap = PAb(tg % 2)
                            for kc in range(16):
                                b.op("tensor", lambda e, fc=fc, tg=tg, kc=kc, k_=k_, bap=bap: e.matmul(
                                    bap,
                                    Q(k_, kc * 512 + fc * 128, kc * 512 + (fc + 1) * 128),
                                    A[:, kc * 2048 + tg * 512:kc * 2048 + (tg + 1) * 512],
                                    start=(kc == 0), stop=(kc == 15)),
                                    reads=[RA[kc], RQ[k_]], writes=[bank])
                            b.op("vector", lambda e, bap=bap: e.tensor_copy(
                                scanb[:, ZC + 3:ZC + 515], bap),
                                reads=[bank], writes=[Rscb])
                            b.op("vector", lambda e, ch=ch, cwo=cwo: e.tensor_scalar(
                                scanb[:, T1:T1 + 512], scanb[:, ZC:ZC + 512],
                                statx[:, cwo + ch * 4:cwo + ch * 4 + 1], None, ALU.mult),
                                reads=[Rscb, Rstx], writes=[Rscb])
                            for j in (1, 2, 3):
                                b.op("vector", lambda e, j=j, ch=ch, cwo=cwo: e.scalar_tensor_tensor(
                                    scanb[:, T1:T1 + 512], scanb[:, ZC + j:ZC + j + 512],
                                    statx[:, cwo + ch * 4 + j:cwo + ch * 4 + j + 1],
                                    scanb[:, T1:T1 + 512], ALU.mult, ALU.add),
                                    reads=[Rscb, Rstx], writes=[Rscb])
                            b.op("vector", lambda e: e.tensor_copy(
                                scanb[:, ZC:ZC + 3], scanb[:, ZC + 512:ZC + 515]),
                                reads=[Rscb], writes=[Rscb])
                            b.op("scalar", lambda e, tg=tg, ch=ch, cbo=cbo: e.activation(
                                D2[:, tg * 512:(tg + 1) * 512], scanb[:, T1:T1 + 512],
                                AF.Silu, bias=statx[:, cbo + ch:cbo + ch + 1]),
                                reads=[Rscb, Rstx], writes=[RD2])
                            if l2:
                                b.op("vector", lambda e, tg=tg: e.tensor_tensor(
                                    scanb[:, T1:T1 + 512],
                                    D2[:, tg * 512:(tg + 1) * 512],
                                    D2[:, tg * 512:(tg + 1) * 512], ALU.mult),
                                    reads=[RD2], writes=[Rscb])
                                b.op("tensor", lambda e: e.matmul(
                                    PBp[0:1, 0:512], onescb[:], scanb[:, T1:T1 + 512],
                                    start=True, stop=True),
                                    reads=[Rscb, Rconst], writes=[RPB[0]])
                                b.op("scalar", lambda e: e.activation(
                                    D1[0:1, 0:512], PBp[0:1, 0:512], AF.Sqrt),
                                    reads=[RPB[0]], writes=[RD1])
                                b.op("vector", lambda e: e.tensor_scalar_max(
                                    D1[0:1, 0:512], D1[0:1, 0:512], 1e-12),
                                    reads=[RD1], writes=[RD1])
                                b.op("vector", lambda e: e.reciprocal(
                                    D1[0:1, 512:1024], D1[0:1, 0:512]),
                                    reads=[RD1], writes=[RD1])
                                b.op("tensor", lambda e: e.matmul(
                                    PBp[:, 512:1024], onesr[:], D1[0:1, 512:1024],
                                    start=True, stop=True),
                                    reads=[RD1, Rconst], writes=[RPB[1]])
                                b.op("vector", lambda e, tg=tg: e.tensor_tensor(
                                    D2[:, tg * 512:(tg + 1) * 512],
                                    D2[:, tg * 512:(tg + 1) * 512],
                                    PBp[:, 512:1024], ALU.mult),
                                    reads=[RD2, RPB[1]], writes=[RD2])
                        b.dma("sync", lambda e, scr=scr, ch=ch: e.dma_start(
                            scr[ch * 128:(ch + 1) * 128, :], D2[:, 0:2048]),
                            reads=[RD2], writes=[Rscr[{id(q_s): "q_s", id(k_s): "k_s", id(v_s): "v_s"}[id(scr)]]])
            b.dma("gpsimd", lambda e: e.dma_start(
                C1[:, 22528:24576][:, 0:256].rearrange("p (c f) -> p c f", f=16),
                dr["l1_wa"].rearrange("(c p) f -> p c f", p=128)), writes=[Rbob])
            b.dma("gpsimd", lambda e: e.dma_start(
                C1[:, 22528:24576][:, 256:512].rearrange("p (c f) -> p c f", f=16),
                dr["l1_wb"].rearrange("(c p) f -> p c f", p=128)), writes=[Rbob])
            for tg in range(4):
                for which in range(2):
                    bap = PBp[0:16, which * 512:which * 512 + 512]
                    bank = RPB[which]
                    for kc in range(16):
                        b.op("tensor", lambda e, tg=tg, kc=kc, which=which, bap=bap: e.matmul(
                            bap,
                            C1[:, 22528:24576][:, which * 256 + kc * 16:which * 256 + (kc + 1) * 16],
                            A[:, kc * 2048 + tg * 512:kc * 2048 + (tg + 1) * 512],
                            start=(kc == 0), stop=(kc == 15)),
                            reads=[RA[kc], Rbob], writes=[bank])
                b.op("scalar", lambda e: e.activation(
                    statx[0:16, 704:1216], PBp[0:16, 0:512], AF.Sigmoid),
                    reads=[RPB[0]], writes=[Rstx])
                b.op("scalar", lambda e: e.activation(
                    D2[0:16, 0:512], PBp[0:16, 512:1024], AF.Sigmoid),
                    reads=[RPB[1]], writes=[RD2])
                b.op("scalar", lambda e: e.activation(
                    statx[0:16, 704:1216], statx[0:16, 704:1216], AF.Ln),
                    reads=[Rstx], writes=[Rstx])
                for blk in range(4):
                    cidx = tg * 4 + blk
                    b.op("tensor", lambda e, blk=blk: e.transpose(
                        PA[:, 0:16], statx[0:16, 704 + blk * 128:704 + (blk + 1) * 128],
                        idf[0:16, 0:16]),
                        reads=[Rstx, Rconst], writes=[RPA[0]])
                    b.op("vector", lambda e, cidx=cidx: e.tensor_copy(
                        statx[:, 320 + cidx * 16:320 + (cidx + 1) * 16], PA[:, 0:16]),
                        reads=[RPA[0]], writes=[Rstx])
                    b.op("tensor", lambda e, blk=blk: e.transpose(
                        PT[:, 0:16], D2[0:16, blk * 128:(blk + 1) * 128], idb[0:16, 0:16]),
                        reads=[RD2, Rconst], writes=[RPT])
                    b.op("vector", lambda e, cidx=cidx: e.tensor_copy(
                        bT[:, cidx * 16:(cidx + 1) * 16], PT[:, 0:16]),
                        reads=[RPT], writes=[RbT])

        def phase_l1_scan():
            RD2o = Res("d2odd")
            RD1h = [Res(f"st{h}") for h in range(8)]
            Rstx0, Rstx1 = Res("sx0"), Res("sx1")

            def SL(par, off, w):
                if par == 0:
                    return scanb[:, off:off + w], Rscb
                return D2[:, 2048 + off:2048 + off + w], RD2o
            FS = {0: dict(eL=592, eLn=593, negL=594, eLCb=595,
                          expD=736, w1f=864),
                  1: dict(eL=216, eLn=217, negL=218, eLCb=219,
                          expD=88, w1f=992)}
            PBK = {0: dict(b1=(PA, 0, RPA[0]), b2=(PA, 512, RPA[1]),
                           b3=(PA, 1024, RPA[2]), pt=(PT, RPT)),
                   1: dict(b1=(PA, 1536, RPA[3]), b2=(PBp, 0, RPB[0]),
                           b3=(PBp, 512, RPB[1]), pt=(PT2, RPT2))}

            for hf in range(2):
                b.op("vector", lambda e: e.memset(D1[:, :], 0.0), writes=RD1h)
                for c in range(16):
                    d = c % 2
                    base = d * 4096
                    qr = RQ[d]
                    for nm, lo, hi, rows in (("q", 0, 1024, 1024),
                                             ("k", 1024, 2048, 1024),
                                             ("v", 2048, 4096, 2048)):
                        scr = {"q": q_s, "k": k_s, "v": v_s}[nm]
                        b.dma("sync", lambda e, hf=hf, c=c, base=base, lo=lo, scr=scr, rows=rows: e.dma_start(
                            Q(0, base + lo, base + lo + rows)
                            .rearrange("p (h f) -> p h f", f=128),
                            scr[hf * rows:(hf + 1) * rows, c * 128:(c + 1) * 128]
                            .rearrange("(h p) f -> p h f", p=128)),
                            writes=[qr])

                    def qf(h, base=base):
                        return Q(0, base + h * 128, base + (h + 1) * 128)

                    def kf(h, base=base):
                        return Q(0, base + 1024 + h * 128, base + 1024 + (h + 1) * 128)

                    def vf(h, half, base=base):
                        return Q(0, base + 2048 + (2 * h + half) * 128,
                                 base + 2048 + (2 * h + half + 1) * 128)

                    b.op("tensor", lambda e, c=c: e.matmul(
                        PA[:, 1024:1040], muift[:],
                        statx[:, 320 + c * 16:320 + (c + 1) * 16],
                        start=True, stop=True),
                        reads=[Rstx, Rconst], writes=[RPA[2]])
                    b.op("vector", lambda e: e.tensor_copy(
                        statx[:, 576:592], PA[:, 1024:1040]),
                        reads=[RPA[2]], writes=[Rstx])

                    def head_steps(h, kf=kf, qf=qf, vf=vf, qr=qr):
                        par = h % 2
                        hg = hf * 8 + h
                        F = FS[par]
                        BK = PBK[par]
                        pb1, o1_, rb1 = BK["b1"]
                        pb2, o2_, rb2 = BK["b2"]
                        pb3, o3_, rb3 = BK["b3"]
                        ptp, rpt = BK["pt"]
                        Lcol = statx[:, 576 + hg:577 + hg]
                        bcol = bT[:, c * 16 + hg:c * 16 + hg + 1]
                        sst = D1[:, h * 256:(h + 1) * 256]
                        rs = RD1h[h]
                        rsx = Rstx0 if par == 0 else Rstx1
                        lrow = statx[0:1, 608:736] if par == 0 else rw[0:1, 0:128]
                        lrr = rsx if par == 0 else Rrw
                        s0b, s0r = SL(par, 1152, 256)
                        vtm, vtr = SL(par, 1408, 256)
                        ntc, ntr = SL(par, 0, 128)
                        y0, _ = SL(par, 640, 256)
                        y1s, _ = SL(par, 896, 256)
                        ktm, _ = SL(par, 1664, 128)
                        kp, _ = SL(par, 1792, 128)
                        pta, _ = SL(par, 128, 128)
                        ptb, _ = SL(par, 256, 128)
                        pna, _ = SL(par, 384, 128)
                        pnb, _ = SL(par, 512, 128)
                        sres = ntr
                        S = []
                        A_ = S.append
                        A_(lambda: b.op("vector", lambda e: e.tensor_copy(s0b, sst),
                                        reads=[rs], writes=[sres]))
                        for half in range(2):
                            A_(lambda half=half: b.op("tensor", lambda e: e.transpose(
                                ptp[:, 0:128], vf(h, half), idb[:]),
                                reads=[qr, Rconst], writes=[rpt]))
                            A_(lambda half=half: b.op("vector", lambda e: e.tensor_copy(
                                vtm[:, half * 128:(half + 1) * 128], ptp[:, 0:128]),
                                reads=[rpt], writes=[sres]))
                        A_(lambda: b.op("scalar", lambda e: e.activation(
                            statx[:, F["eL"]:F["eL"] + 1], Lcol, AF.Exp),
                            reads=[Rstx], writes=[rsx]))
                        A_(lambda: b.op("vector", lambda e: e.tensor_scalar(
                            statx[:, F["eLn"]:F["eLn"] + 1],
                            statx[:, F["eL"]:F["eL"] + 1], -1.0, None, ALU.mult),
                            reads=[rsx], writes=[rsx]))
                        A_(lambda: b.op("vector", lambda e: e.tensor_scalar(
                            statx[:, F["negL"]:F["negL"] + 1], Lcol, -1.0,
                            None, ALU.mult),
                            reads=[Rstx], writes=[rsx]))
                        A_(lambda: b.op("tensor", lambda e: e.transpose(
                            pb3[0:1, o3_:o3_ + 128], Lcol, idf[:]),
                            reads=[Rstx, Rconst], writes=[rb3]))
                        A_(lambda: b.op("vector", lambda e: e.tensor_copy(
                            lrow, pb3[0:1, o3_:o3_ + 128]),
                            reads=[rb3], writes=[lrr]))
                        A_(lambda: b.op("tensor", lambda e: e.matmul(
                            pb3[:, o3_:o3_ + 128], onesr[:], lrow,
                            start=True, stop=True),
                            reads=[lrr, Rconst], writes=[rb3]))
                        A_(lambda: b.op("vector", lambda e: e.tensor_scalar(
                            pb3[:, o3_ + 128:o3_ + 256],
                            pb3[:, o3_:o3_ + 128],
                            statx[:, F["negL"]:F["negL"] + 1], 0.0,
                            ALU.add, ALU.min),
                            reads=[rb3, rsx], writes=[rb3]))
                        A_(lambda: b.op("scalar", lambda e: e.activation(
                            statx[:, F["expD"]:F["expD"] + 128],
                            pb3[:, o3_ + 128:o3_ + 256], AF.Exp),
                            reads=[rb3], writes=[rsx]))
                        A_(lambda: b.op("tensor", lambda e: e.matmul(
                            pb1[:, o1_:o1_ + 256], kf(h), s0b,
                            start=True, stop=True),
                            reads=[qr, sres], writes=[rb1]))
                        A_(lambda: b.op("vector", lambda e: e.scalar_tensor_tensor(
                            y0, pb1[:, o1_:o1_ + 256],
                            statx[:, F["eLn"]:F["eLn"] + 1], vtm,
                            ALU.mult, ALU.add),
                            reads=[rb1, rsx, sres], writes=[sres]))
                        A_(lambda: b.op("tensor", lambda e: e.matmul(
                            pb2[:, o2_:o2_ + 128], kf(h), kf(h),
                            start=True, stop=True),
                            reads=[qr], writes=[rb2]))
                        A_(lambda: b.op("vector", lambda e: e.tensor_tensor(
                            statx[:, F["w1f"]:F["w1f"] + 128],
                            statx[:, F["expD"]:F["expD"] + 128],
                            pb2[:, o2_:o2_ + 128], ALU.mult),
                            reads=[rsx, rb2], writes=[rsx]))
                        A_(lambda: b.op("vector", lambda e: e.scalar_tensor_tensor(
                            ntc, statx[:, F["w1f"]:F["w1f"] + 128], bcol,
                            msut[:], ALU.mult, ALU.mult),
                            reads=[rsx, RbT, Rconst], writes=[sres]))
                        A_(lambda: b.op("tensor", lambda e: e.transpose(
                            ptp[:, 0:128], ntc, idb[:]),
                            reads=[sres, Rconst], writes=[rpt]))
                        A_(lambda: b.op("vector", lambda e: e.tensor_copy(
                            pna, ptp[:, 0:128]),
                            reads=[rpt], writes=[sres]))
                        A_(lambda: b.op("vector", lambda e: e.tensor_copy(
                            pta, ntc),
                            reads=[sres], writes=[sres]))
                        ycur, ynext = y0, y1s
                        pt_cur, pt_nxt = pta, ptb
                        pn_cur, pn_nxt = pna, pnb
                        for j in range(7):
                            A_(lambda pt_cur=pt_cur, ycur=ycur: b.op("tensor", lambda e: e.matmul(
                                pb1[:, o1_:o1_ + 256], pt_cur, ycur,
                                start=True, stop=True),
                                reads=[sres], writes=[rb1]))
                            A_(lambda ycur=ycur, ynext=ynext: b.op("vector", lambda e: e.tensor_tensor(
                                ynext, pb1[:, o1_:o1_ + 256], ycur, ALU.add),
                                reads=[rb1, sres], writes=[sres]))
                            ycur, ynext = ynext, ycur
                            if j < 6:
                                A_(lambda pn_cur=pn_cur, pt_cur=pt_cur: b.op("tensor", lambda e: e.matmul(
                                    pb2[:, o2_:o2_ + 128], pn_cur, pt_cur,
                                    start=True, stop=True),
                                    reads=[sres], writes=[rb2]))
                                A_(lambda pt_nxt=pt_nxt: b.op("vector", lambda e: e.tensor_copy(
                                    pt_nxt, pb2[:, o2_:o2_ + 128]),
                                    reads=[rb2], writes=[sres]))
                                if j < 5:
                                    A_(lambda pt_cur=pt_cur, pn_cur=pn_cur: b.op("tensor", lambda e: e.matmul(
                                        pb2[:, o2_:o2_ + 128], pt_cur, pn_cur,
                                        start=True, stop=True),
                                        reads=[sres], writes=[rb2]))
                                    A_(lambda pn_nxt=pn_nxt: b.op("vector", lambda e: e.tensor_copy(
                                        pn_nxt, pb2[:, o2_:o2_ + 128]),
                                        reads=[rb2], writes=[sres]))
                                pt_cur, pt_nxt = pt_nxt, pt_cur
                                pn_cur, pn_nxt = pn_nxt, pn_cur
                        R_ = ycur
                        Rtmp = y1s if R_ is y0 else y0
                        A_(lambda: b.op("tensor", lambda e: e.matmul(
                            pb2[:, o2_:o2_ + 128], kf(h), qf(h),
                            start=True, stop=True),
                            reads=[qr], writes=[rb2]))
                        A_(lambda: b.op("vector", lambda e: e.tensor_tensor(
                            statx[:, F["w1f"]:F["w1f"] + 128],
                            statx[:, F["expD"]:F["expD"] + 128],
                            pb2[:, o2_:o2_ + 128], ALU.mult),
                            reads=[rsx, rb2], writes=[rsx]))
                        A_(lambda: b.op("vector", lambda e: e.scalar_tensor_tensor(
                            ntc, statx[:, F["w1f"]:F["w1f"] + 128], bcol,
                            muibt[:], ALU.mult, ALU.mult),
                            reads=[rsx, RbT, Rconst], writes=[sres]))
                        A_(lambda R_=R_: b.op("tensor", lambda e: e.matmul(
                            pb1[:, o1_ + 256:o1_ + 512], ntc, R_,
                            start=True, stop=True, skip_group_check=True),
                            reads=[sres], writes=[rb1]))
                        A_(lambda: b.op("tensor", lambda e: e.matmul(
                            pb1[:, o1_:o1_ + 256], qf(h), s0b,
                            start=True, stop=True, skip_group_check=True),
                            reads=[qr, sres], writes=[rb1]))
                        A_(lambda Rtmp=Rtmp: b.op("vector", lambda e: e.tensor_scalar(
                            Rtmp, pb1[:, o1_:o1_ + 256],
                            statx[:, F["eL"]:F["eL"] + 1], None, ALU.mult),
                            reads=[rb1, rsx], writes=[sres]))
                        A_(lambda Rtmp=Rtmp: b.op("vector", lambda e: e.tensor_tensor(
                            Rtmp, Rtmp, pb1[:, o1_ + 256:o1_ + 512], ALU.add),
                            reads=[sres, rb1], writes=[sres]))
                        for half in range(2):
                            A_(lambda half=half, Rtmp=Rtmp: b.op("tensor", lambda e: e.transpose(
                                ptp[:, 0:128], Rtmp[:, half * 128:(half + 1) * 128],
                                idb[:]),
                                reads=[sres, Rconst], writes=[rpt]))
                            A_(lambda half=half: b.op("vector", lambda e: e.tensor_copy(
                                D2[:, h * 256 + half * 128:h * 256 + (half + 1) * 128],
                                ptp[:, 0:128]),
                                reads=[rpt], writes=[RD2]))
                        A_(lambda: b.op("tensor", lambda e: e.transpose(
                            ptp[:, 0:128], kf(h), idb[:]),
                            reads=[qr, Rconst], writes=[rpt]))
                        A_(lambda: b.op("vector", lambda e: e.tensor_copy(
                            ktm, ptp[:, 0:128]),
                            reads=[rpt], writes=[sres]))
                        A_(lambda: b.op("vector", lambda e: e.tensor_tensor(
                            statx[:, F["negL"]:F["negL"] + 1],
                            statx[:, F["expD"] + 127:F["expD"] + 128], bcol,
                            ALU.mult),
                            reads=[rsx, RbT], writes=[rsx]))
                        A_(lambda: b.op("vector", lambda e: e.tensor_scalar(
                            kp, ktm, statx[:, F["negL"]:F["negL"] + 1],
                            None, ALU.mult),
                            reads=[sres, rsx], writes=[sres]))
                        A_(lambda R_=R_: b.op("tensor", lambda e: e.matmul(
                            pb3[:, o3_ + 256:o3_ + 512], kp, R_,
                            start=True, stop=True, skip_group_check=True),
                            reads=[sres], writes=[rb3]))
                        A_(lambda: b.op("vector", lambda e: e.tensor_tensor(
                            statx[:, F["eLCb"]:F["eLCb"] + 1],
                            statx[:, F["expD"] + 127:F["expD"] + 128],
                            statx[:, F["eL"]:F["eL"] + 1], ALU.mult),
                            reads=[rsx], writes=[rsx]))
                        A_(lambda: b.op("vector", lambda e: e.scalar_tensor_tensor(
                            sst, sst, statx[:, F["eLCb"]:F["eLCb"] + 1],
                            pb3[:, o3_ + 256:o3_ + 512], ALU.mult, ALU.add),
                            reads=[rs, rsx, rb3], writes=[rs]))
                        return S

                    for hp in range(4):
                        s_even = head_steps(2 * hp)
                        s_odd = head_steps(2 * hp + 1)
                        n = max(len(s_even), len(s_odd))
                        for idx in range(n):
                            if idx < len(s_even):
                                s_even[idx]()
                            if idx < len(s_odd):
                                s_odd[idx]()
                    b.dma("sync", lambda e, hf=hf, c=c: e.dma_start(
                        outs_s[hf * 2048:(hf + 1) * 2048, c * 128:(c + 1) * 128]
                        .rearrange("(h p) f -> p h f", p=128),
                        D2[:, 0:2048].rearrange("p (h f) -> p h f", f=128)),
                        reads=[RD2], writes=[Rscr["outs_s"]])

        def phase_l1_wo():
            for kq in range(4):
                wr = [RQ[0], RQ[1]]
                wload(C1[:, 0:16384].rearrange("p (c f) -> p c f", f=2048),
                      dr["l1_wo"][kq * 1024:(kq + 1) * 1024, :]
                      .rearrange("(c p) f -> p c f", p=128), wr)
                for i in range(NT):
                    ob = RO[i % 2]
                    ocol = 16384 + (i % 2) * 1024
                    b.dma("sync", lambda e, i=i, kq=kq, ocol=ocol: e.dma_start(
                        C1[:, ocol:ocol + 1024].rearrange("p (c f) -> p c f", f=128),
                        outs_s[kq * 1024:(kq + 1) * 1024, i * 128:(i + 1) * 128]
                        .rearrange("(c p) f -> p c f", p=128)),
                        reads=[Rscr["outs_s"]], writes=[ob])
                    if kq == 3:
                        b.dma("gpsimd", lambda e, i=i: e.dma_start(
                            D1[:, :], x1_s[i * 128:(i + 1) * 128, :]),
                            reads=[Rscr["x1_s"], RD1], writes=[RD1])
                    for fb in range(4):
                        bank = RPA[fb]
                        bap = PAb(fb)
                        for cc in range(8):
                            b.op("tensor", lambda e, cc=cc, fb=fb, bap=bap, ocol=ocol: e.matmul(
                                bap, C1[:, ocol + cc * 128:ocol + (cc + 1) * 128],
                                C1[:, cc * 2048 + fb * 512:cc * 2048 + (fb + 1) * 512],
                                start=(cc == 0), stop=(cc == 7)),
                                reads=[ob] + wr, writes=[bank])
                        if kq == 0:
                            b.op("vector", lambda e, i=i, fb=fb, bap=bap: e.tensor_copy(
                                Bacc[:, i * 2048 + fb * 512:i * 2048 + (fb + 1) * 512],
                                bap),
                                reads=[bank, RB[i]], writes=[RB[i]])
                        elif kq < 3:
                            b.op("vector", lambda e, i=i, fb=fb, bap=bap: e.tensor_tensor(
                                Bacc[:, i * 2048 + fb * 512:i * 2048 + (fb + 1) * 512],
                                bap,
                                Bacc[:, i * 2048 + fb * 512:i * 2048 + (fb + 1) * 512],
                                ALU.add),
                                reads=[bank, RB[i]], writes=[RB[i]])
                        else:
                            b.op("vector", lambda e, i=i, fb=fb, bap=bap: e.tensor_tensor(
                                scanb[:, 0:512], bap,
                                Bacc[:, i * 2048 + fb * 512:i * 2048 + (fb + 1) * 512],
                                ALU.add),
                                reads=[bank, RB[i]], writes=[Rscb])
                            st_ = 704 if fb % 2 == 0 else 192
                            b.op("vector", lambda e, fb=fb, st_=st_: e.tensor_tensor(
                                statx[:, st_:st_ + 512], scanb[:, 0:512],
                                D1[:, fb * 512:(fb + 1) * 512], ALU.add),
                                reads=[Rscb, RD1], writes=[Rstx])
                            b.dma("sync", lambda e, i=i, fb=fb, st_=st_: e.dma_start(
                                xa_s[i * 128:(i + 1) * 128, fb * 512:(fb + 1) * 512],
                                statx[:, st_:st_ + 512]),
                                reads=[Rstx], writes=[Rscr["xa_s"]])

        def round_min_sign(dst_bf_ap, dst_res, clipv, sg_ap, sg_res, n, par=0):
            ao = 704 if par == 0 else 88
            io = par * 512
            av = statx[:, ao:ao + n]
            it = ITL[:, io:io + n]
            rit = RIT if par == 0 else RIT1
            b.op("scalar", lambda e: e.activation(it, av, AF.Identity),
                 reads=[Rstx], writes=[rit])
            b.op("vector", lambda e: e.scalar_tensor_tensor(
                av, it, -1.0, av, ALU.mult, ALU.add),
                reads=[rit, Rstx], writes=[Rstx])
            b.op("vector", lambda e: e.scalar_tensor_tensor(
                av, av, 0.5, it, ALU.is_ge, ALU.add),
                reads=[Rstx, rit], writes=[Rstx])
            b.op("vector", lambda e: e.scalar_tensor_tensor(
                dst_bf_ap, av, float(clipv), sg_ap,
                ALU.min, ALU.mult),
                reads=[Rstx, sg_res], writes=[dst_res])

        def quant_chain(dst_bf_ap, src_f32_ap, inv_scale_ap, clipv, sg_ap,
                        n, par=0, rsrc_extra=()):
            ao = 704 if par == 0 else 88
            b.op("scalar", lambda e: e.activation(
                statx[:, ao:ao + n], src_f32_ap, AF.Abs, scale=inv_scale_ap),
                reads=[RD1, Rstx] + list(rsrc_extra), writes=[Rstx])
            b.op("scalar", lambda e: e.activation(
                sg_ap, src_f32_ap, AF.Sign),
                reads=[RD1] + list(rsrc_extra), writes=[RD2])
            round_min_sign(dst_bf_ap, RD2, clipv, sg_ap, RD2, n, par=par)

        def phase_prep_wq():
            for hc in range(64):
                b.dma("sync", lambda e, hc=hc: e.dma_start(
                    D1[:, :], dr["l1_w1"][hc * 128:(hc + 1) * 128, :]),
                    writes=[RD1])
                b.op("vector", lambda e: e.reduce_sum(
                    statx[:, 640:641], D1[:, :], AXX, apply_absolute_value=True),
                    reads=[RD1], writes=[Rstx])
                b.op("vector", lambda e, hc=hc: e.tensor_scalar(
                    statx[:, hc:hc + 1], statx[:, 640:641], 1.0 / DM, None, ALU.mult),
                    reads=[Rstx], writes=[Rstx])
                b.op("vector", lambda e: e.reciprocal(
                    statx[:, 641:642], statx[:, 640:641]),
                    reads=[Rstx], writes=[Rstx])
                b.op("vector", lambda e: e.tensor_scalar(
                    statx[:, 641:642], statx[:, 641:642], float(DM), None, ALU.mult),
                    reads=[Rstx], writes=[Rstx])
                for pc in range(4):
                    quant_chain(D2[:, pc * 512:(pc + 1) * 512],
                                D1[:, pc * 512:(pc + 1) * 512],
                                statx[:, 641:642], 1.0,
                                D2[:, 2048 + (pc % 2) * 512:
                                      2048 + (pc % 2) * 512 + 512], 512,
                                par=pc % 2)
                for t in range(16):
                    transpose(D2[:, 2048 + t * 128:2048 + (t + 1) * 128],
                              D2[:, t * 128:(t + 1) * 128],
                              RD2, RD2, use2=(t % 2 == 1))
                b.dma("sync", lambda e, hc=hc: e.dma_start(
                    wq1t_s[:, hc * 128:(hc + 1) * 128]
                    .rearrange("(c p) f -> p c f", p=128),
                    D2[:, 2048:4096].rearrange("p (c f) -> p c f", f=128)),
                    reads=[RD2], writes=[Rscr["wq1t_s"]])
            for dc in range(16):
                for pc in range(4):
                    b.dma("sync", lambda e, dc=dc, pc=pc: e.dma_start(
                        D1[:, :], dr["l1_w2"][dc * 128:(dc + 1) * 128,
                                              pc * 2048:(pc + 1) * 2048]),
                        writes=[RD1])
                    b.op("vector", lambda e, pc=pc: e.reduce_sum(
                        statx[:, 644 + pc:645 + pc], D1[:, :], AXX,
                        apply_absolute_value=True),
                        reads=[RD1], writes=[Rstx])
                b.op("vector", lambda e: e.tensor_tensor(
                    statx[:, 644:645], statx[:, 644:645], statx[:, 645:646], ALU.add),
                    reads=[Rstx], writes=[Rstx])
                b.op("vector", lambda e: e.tensor_tensor(
                    statx[:, 646:647], statx[:, 646:647], statx[:, 647:648], ALU.add),
                    reads=[Rstx], writes=[Rstx])
                b.op("vector", lambda e: e.tensor_tensor(
                    statx[:, 640:641], statx[:, 644:645], statx[:, 646:647], ALU.add),
                    reads=[Rstx], writes=[Rstx])
                b.op("vector", lambda e, dc=dc: e.tensor_scalar(
                    statx[:, 64 + dc:65 + dc], statx[:, 640:641], 1.0 / 8192,
                    None, ALU.mult),
                    reads=[Rstx], writes=[Rstx])
                b.op("vector", lambda e: e.reciprocal(
                    statx[:, 641:642], statx[:, 640:641]),
                    reads=[Rstx], writes=[Rstx])
                b.op("vector", lambda e: e.tensor_scalar(
                    statx[:, 641:642], statx[:, 641:642], 8192.0, None, ALU.mult),
                    reads=[Rstx], writes=[Rstx])
                for pc in range(4):
                    b.dma("sync", lambda e, dc=dc, pc=pc: e.dma_start(
                        D1[:, :], dr["l1_w2"][dc * 128:(dc + 1) * 128,
                                              pc * 2048:(pc + 1) * 2048]),
                        writes=[RD1])
                    for sp in range(4):
                        quant_chain(D2[:, sp * 512:(sp + 1) * 512],
                                    D1[:, sp * 512:(sp + 1) * 512],
                                    statx[:, 641:642], 1.0,
                                    D2[:, 2048 + (sp % 2) * 512:
                                          2048 + (sp % 2) * 512 + 512], 512,
                                    par=sp % 2)
                    for t in range(16):
                        transpose(D2[:, 2048 + t * 128:2048 + (t + 1) * 128],
                                  D2[:, t * 128:(t + 1) * 128],
                                  RD2, RD2, use2=(t % 2 == 1))
                    b.dma("sync", lambda e, dc=dc, pc=pc: e.dma_start(
                        wq2t_s[pc * 2048:(pc + 1) * 2048,
                               dc * 128:(dc + 1) * 128]
                        .rearrange("(c p) f -> p c f", p=128),
                        D2[:, 2048:4096].rearrange("p (c f) -> p c f", f=128)),
                        reads=[RD2], writes=[Rscr["wq2t_s"]])

        def phase_l1_ffn():
            for i in range(NT):
                b.op("vector", lambda e, i=i: e.tensor_scalar_max(
                    stat[:, 48 + i:49 + i], stat[:, 32 + i:33 + i], 1e-5),
                    reads=[Rstat], writes=[Rstat])
                b.op("vector", lambda e, i=i: e.tensor_scalar(
                    stat[:, 64 + i:65 + i], stat[:, 48 + i:49 + i], 1.0 / 127.0,
                    None, ALU.mult),
                    reads=[Rstat], writes=[Rstat])
                b.op("vector", lambda e, i=i: e.reciprocal(
                    stat[:, 48 + i:49 + i], stat[:, 48 + i:49 + i]),
                    reads=[Rstat], writes=[Rstat])
                b.op("vector", lambda e, i=i: e.tensor_scalar(
                    stat[:, 48 + i:49 + i], stat[:, 48 + i:49 + i], 127.0,
                    None, ALU.mult),
                    reads=[Rstat], writes=[Rstat])

            def build_bcast(colbase, dst_ap, dst_res, nda=True):
                for i in range(NT):
                    b.op("tensor", lambda e, i=i, colbase=colbase: e.transpose(
                        PA[0:1, 1024:1152], stat[:, colbase + i:colbase + i + 1],
                        idf[:]),
                        reads=[Rstat, Rconst], writes=[RPA[2]])
                    b.op("vector", lambda e, i=i: e.tensor_copy(
                        D1[0:1, i * 128:(i + 1) * 128], PA[0:1, 1024:1152]),
                        reads=[RPA[2]], writes=[RD1])
                for fb in range(4):
                    b.op("tensor", lambda e, fb=fb: e.matmul(
                        PA[:, 1024:1536], onesr[:], D1[0:1, fb * 512:(fb + 1) * 512],
                        start=True, stop=True),
                        reads=[RD1, Rconst], writes=[RPA[2]])
                    b.op("vector", lambda e, fb=fb, dst_ap=dst_ap: e.tensor_copy(
                        dst_ap[:, fb * 512:(fb + 1) * 512], PA[:, 1024:1536]),
                        reads=[RPA[2]], writes=[dst_res])

            build_bcast(48, D2[:, 0:2048], RD2)
            build_bcast(64, D2[:, 2048:4096], RD2)
            for i in range(NT):
                b.op("tensor", lambda e, i=i: e.transpose(
                    PA[0:1, 1024:1152], statx[:, 64 + i:65 + i], idf[:]),
                    reads=[Rstx, Rconst], writes=[RPA[2]])
                b.op("vector", lambda e, i=i: e.tensor_copy(
                    D1[0:1, i * 128:(i + 1) * 128], PA[0:1, 1024:1152]),
                    reads=[RPA[2]], writes=[RD1])
            for fb in range(4):
                b.op("tensor", lambda e, fb=fb: e.matmul(
                    PA[:, 1024:1536], onesr[:], D1[0:1, fb * 512:(fb + 1) * 512],
                    start=True, stop=True),
                    reads=[RD1, Rconst], writes=[RPA[2]])
                b.op("vector", lambda e, fb=fb: e.tensor_copy(
                    C1[:, 8192 + fb * 512:8192 + (fb + 1) * 512], PA[:, 1024:1536]),
                    reads=[RPA[2]], writes=[RQ[1]])

            for cc in range(16):
                for pc in range(4):
                    src = A[:, cc * 2048 + pc * 512: cc * 2048 + (pc + 1) * 512]
                    par = pc % 2
                    ao = 704 if par == 0 else 88
                    sgs = scanb[:, par * 512:par * 512 + 512]
                    b.op("vector", lambda e, src=src, pc=pc, ao=ao: e.tensor_tensor(
                        statx[:, ao:ao + 512], src, D2[:, pc * 512:(pc + 1) * 512],
                        ALU.mult),
                        reads=[RA[cc], RD2], writes=[Rstx])
                    b.op("scalar", lambda e, src=src, sgs=sgs: e.activation(
                        sgs, src, AF.Sign),
                        reads=[RA[cc]], writes=[Rscb])
                    b.op("scalar", lambda e, ao=ao: e.activation(
                        statx[:, ao:ao + 512], statx[:, ao:ao + 512], AF.Abs),
                        reads=[Rstx], writes=[Rstx])
                    round_min_sign(src, RA[cc], 127.0, sgs, Rscb, 512, par=par)

            RMX, RMN, S2B, SG2 = 512, 1024, 0, 512
            for g in range(4):
                gcol = g * 512
                for hc in range(64):
                    wk = hc % 2
                    wcol = 16384 + wk * 2048
                    wload(C1[:, wcol:wcol + 2048].rearrange("p (c f) -> p c f", f=128),
                          wq1t_s[:, hc * 128:(hc + 1) * 128]
                          .rearrange("(c p) f -> p c f", p=128), RO[wk])
                    bank = RPA[hc % 2]
                    bap = PAb(hc % 2)
                    for kc in range(16):
                        b.op("tensor", lambda e, hc=hc, kc=kc, gcol=gcol, wcol=wcol, bap=bap: e.matmul(
                            bap,
                            C1[:, wcol + kc * 128:wcol + (kc + 1) * 128],
                            A[:, kc * 2048 + gcol: kc * 2048 + gcol + 512],
                            start=(kc == 0), stop=(kc == 15)),
                            reads=[RA[kc], RO[hc % 2]], writes=[bank])
                    b.op("vector", lambda e, gcol=gcol, bap=bap: e.tensor_tensor(
                        statx[:, 704:1216], bap, D2[:, 2048 + gcol:2048 + gcol + 512],
                        ALU.mult),
                        reads=[bank, RD2], writes=[Rstx])
                    b.op("scalar", lambda e, hc=hc: e.activation(
                        Bacc[:, hc * 512:(hc + 1) * 512], statx[:, 704:1216],
                        AF.Gelu_apprx_tanh, scale=statx[:, hc:hc + 1]),
                        reads=[Rstx], writes=[RB[0]])
                    if hc == 0:
                        b.op("vector", lambda e: e.tensor_copy(
                            scanb[:, RMX:RMX + 512], Bacc[:, 0:512]),
                            reads=[RB[0]], writes=[Rscb])
                        b.op("vector", lambda e: e.tensor_copy(
                            scanb[:, RMN:RMN + 512], Bacc[:, 0:512]),
                            reads=[RB[0]], writes=[Rscb])
                    else:
                        b.op("vector", lambda e, hc=hc: e.tensor_tensor(
                            scanb[:, RMX:RMX + 512], scanb[:, RMX:RMX + 512],
                            Bacc[:, hc * 512:(hc + 1) * 512], ALU.max),
                            reads=[RB[0], Rscb], writes=[Rscb])
                        b.op("vector", lambda e, hc=hc: e.tensor_tensor(
                            scanb[:, RMN:RMN + 512], scanb[:, RMN:RMN + 512],
                            Bacc[:, hc * 512:(hc + 1) * 512], ALU.min),
                            reads=[RB[0], Rscb], writes=[Rscb])
                b.op("vector", lambda e: e.tensor_scalar(
                    scanb[:, RMN:RMN + 512], scanb[:, RMN:RMN + 512], -1.0,
                    None, ALU.mult),
                    reads=[Rscb], writes=[Rscb])
                b.op("vector", lambda e: e.tensor_tensor(
                    scanb[:, RMX:RMX + 512], scanb[:, RMX:RMX + 512],
                    scanb[:, RMN:RMN + 512], ALU.max),
                    reads=[Rscb], writes=[Rscb])
                for ts in range(4):
                    b.op("tensor", lambda e, ts=ts: e.transpose(
                        PT[:, 0:128], scanb[:, RMX + ts * 128:RMX + (ts + 1) * 128],
                        idb[:]),
                        reads=[Rscb, Rconst], writes=[RPT])
                    b.op("vector", lambda e, ts=ts: e.reduce_max(
                        stat[:, 80 + ts:81 + ts], PT[:, 0:128], AXX),
                        reads=[RPT], writes=[Rstat])
                    b.op("vector", lambda e, ts=ts: e.tensor_scalar_max(
                        stat[:, 80 + ts:81 + ts], stat[:, 80 + ts:81 + ts], 1e-5),
                        reads=[Rstat], writes=[Rstat])
                    b.op("vector", lambda e, ts=ts: e.tensor_scalar(
                        stat[:, 84 + ts:85 + ts], stat[:, 80 + ts:81 + ts],
                        1.0 / 127.0, None, ALU.mult),
                        reads=[Rstat], writes=[Rstat])
                    b.op("vector", lambda e, ts=ts: e.reciprocal(
                        stat[:, 80 + ts:81 + ts], stat[:, 80 + ts:81 + ts]),
                        reads=[Rstat], writes=[Rstat])
                    b.op("vector", lambda e, ts=ts: e.tensor_scalar(
                        stat[:, 80 + ts:81 + ts], stat[:, 80 + ts:81 + ts], 127.0,
                        None, ALU.mult),
                        reads=[Rstat], writes=[Rstat])
                    b.op("tensor", lambda e, ts=ts: e.transpose(
                        PA[0:1, 1024:1152], stat[:, 80 + ts:81 + ts], idf[:]),
                        reads=[Rstat, Rconst], writes=[RPA[2]])
                    b.op("vector", lambda e, ts=ts: e.tensor_copy(
                        D1[0:1, ts * 128:(ts + 1) * 128], PA[0:1, 1024:1152]),
                        reads=[RPA[2]], writes=[RD1])
                b.op("tensor", lambda e: e.matmul(
                    PA[:, 1024:1536], onesr[:], D1[0:1, 0:512],
                    start=True, stop=True),
                    reads=[RD1, Rconst], writes=[RPA[2]])
                b.op("vector", lambda e: e.tensor_copy(
                    scanb[:, S2B:S2B + 512], PA[:, 1024:1536]),
                    reads=[RPA[2]], writes=[Rscb])
                for hc in range(64):
                    hb = Bacc[:, hc * 512:(hc + 1) * 512]
                    par = hc % 2
                    ao = 704 if par == 0 else 88
                    sgs = scanb[:, SG2 + par * 512:SG2 + par * 512 + 512]
                    b.op("vector", lambda e, hb=hb, ao=ao: e.tensor_tensor(
                        statx[:, ao:ao + 512], hb, scanb[:, S2B:S2B + 512], ALU.mult),
                        reads=[RB[0], Rscb], writes=[Rstx])
                    b.op("scalar", lambda e, hb=hb, sgs=sgs: e.activation(
                        sgs, hb, AF.Sign),
                        reads=[RB[0]], writes=[Rscb])
                    b.op("scalar", lambda e, ao=ao: e.activation(
                        statx[:, ao:ao + 512], statx[:, ao:ao + 512], AF.Abs),
                        reads=[Rstx], writes=[Rstx])
                    round_min_sign(hb, RB[0], 127.0, sgs, Rscb, 512, par=par)
                for fq in range(4):
                    for kc2 in range(64):
                        wk = kc2 % 2
                        wload(Q(0, wk * 512, wk * 512 + 512)
                              .rearrange("p (o f) -> p o f", f=512),
                              wq2t_s[kc2 * 128:(kc2 + 1) * 128,
                                     fq * 512:(fq + 1) * 512]
                              .rearrange("(o p) f -> p o f", p=128), RQ[wk])
                        for ts in range(4):
                            b.op("tensor", lambda e, kc2=kc2, ts=ts, wk=wk: e.matmul(
                                PAb(ts),
                                Bacc[:, kc2 * 512 + ts * 128: kc2 * 512 + (ts + 1) * 128],
                                Q(0, wk * 512, wk * 512 + 512),
                                start=(kc2 == 0), stop=(kc2 == 63)),
                                reads=[RB[0], RQ[wk]], writes=[RPA[ts]])
                    for ts in range(4):
                        i = g * 4 + ts
                        b.dma("gpsimd", lambda e, i=i, fq=fq, ts=ts: e.dma_start(
                            D1[:, ts * 512:(ts + 1) * 512],
                            xa_s[i * 128:(i + 1) * 128, fq * 512:(fq + 1) * 512]),
                            writes=[RD1])
                        st_ = 704 if ts % 2 == 0 else 192
                        b.op("vector", lambda e, ts=ts, fq=fq, st_=st_: e.tensor_tensor(
                            statx[:, st_:st_ + 512], PAb(ts),
                            C1[:, 8192 + fq * 512:8192 + (fq + 1) * 512], ALU.mult),
                            reads=[RPA[ts], RQ[1]], writes=[Rstx])
                        b.op("vector", lambda e, ts=ts, st_=st_: e.scalar_tensor_tensor(
                            statx[:, st_:st_ + 512], statx[:, st_:st_ + 512],
                            stat[:, 84 + ts:85 + ts], D1[:, ts * 512:(ts + 1) * 512],
                            ALU.mult, ALU.add),
                            reads=[Rstx, Rstat, RD1], writes=[Rstx])
                        b.dma("sync", lambda e, i=i, fq=fq, st_=st_: e.dma_start(
                            out[i * 128:(i + 1) * 128, fq * 512:(fq + 1) * 512],
                            statx[:, st_:st_ + 512]),
                            reads=[Rstx], writes=[Rout])

        PH = {"ln": 1, "kv": 2, "qo1": 3, "qo2": 4, "qo": 5, "rt": 6,
              "moe1": 7, "moe": 8, "x1": 9, "l1p": 10, "l1s": 11, "l1a": 12,
              "full": 99}
        lim = PH.get(upto, 99)
        load_consts()
        phase_ln(dr["x0"], s1_track=False)
        if lim >= 2:
            b.phase_barrier()
            phase_l0_kv()
        if lim >= 3:
            b.phase_barrier()
            phase_l0_qo(sub=min(lim - 2, 3))
        if lim >= 6:
            b.phase_barrier()
            phase_router()
        if lim == 7:
            b.phase_barrier()
            phase_l0_moe(nexp=1)
        elif lim >= 8:
            b.phase_barrier()
            phase_l0_moe()
        if lim >= 9:
            b.phase_barrier()
            phase_x1_ln()
        if lim >= 10:
            b.phase_barrier()
            phase_l1_proj()
        if lim >= 11:
            b.phase_barrier()
            phase_l1_scan()
        if lim >= 12:
            b.phase_barrier()
            phase_l1_wo()
        if lim >= 13:
            b.phase_barrier()
            phase_prep_wq()
            b.phase_barrier()
            phase_l1_ffn()

        if dump == "wq1":
            for i in range(NT):
                b.dma("gpsimd", lambda e, i=i: e.dma_start(
                    out[i * 128:(i + 1) * 128, :],
                    wq1t_s[i * 128:(i + 1) * 128, 0:2048]),
                    reads=[Rscr["wq1t_s"]], writes=[Rout])
        elif dump == "wq2":
            for i in range(NT):
                b.dma("gpsimd", lambda e, i=i: e.dma_start(
                    out[i * 128:(i + 1) * 128, :],
                    wq2t_s[i * 128:(i + 1) * 128, 0:2048]),
                    reads=[Rscr["wq2t_s"]], writes=[Rout])
        elif dump == "y1d":
            for c in range(16):
                b.op("vector", lambda e, c=c: e.tensor_copy(
                    D1[:, :], A[:, c * 2048:(c + 1) * 2048]),
                    reads=[RA[c]], writes=[RD1])
                b.dma("gpsimd", lambda e, c=c: e.dma_start(
                    out[c * 128:(c + 1) * 128, :], D1[:, :]),
                    reads=[RD1], writes=[Rout])
        elif dump == "x1":
            for i in range(NT):
                b.dma("gpsimd", lambda e, i=i: e.dma_start(
                    out[i * 128:(i + 1) * 128, :], x1_s[i * 128:(i + 1) * 128, :]),
                    reads=[Rscr["x1_s"]], writes=[Rout])
        elif dump == "l1a":
            for i in range(NT):
                b.dma("gpsimd", lambda e, i=i: e.dma_start(
                    out[i * 128:(i + 1) * 128, :], xa_s[i * 128:(i + 1) * 128, :]),
                    reads=[Rscr["xa_s"]], writes=[Rout])
        elif dump in ("acc", "moe"):
            for i in range(NT):
                b.op("vector", lambda e, i=i: e.tensor_copy(
                    D1[:, :], Bacc[:, i * 2048:(i + 1) * 2048]),
                    reads=[RB[i]], writes=[RD1])
                b.dma("gpsimd", lambda e, i=i: e.dma_start(
                    out[i * 128:(i + 1) * 128, :], D1[:, :]),
                    reads=[RD1], writes=[Rout])
        elif dump == "xn":
            for c in range(16):
                b.op("vector", lambda e, c=c: e.tensor_copy(
                    D1[:, :], A[:, c * 2048:(c + 1) * 2048]),
                    reads=[RA[c]], writes=[RD1])
                b.dma("gpsimd", lambda e, c=c: e.dma_start(
                    out[c * 128:(c + 1) * 128, :], D1[:, :]),
                    reads=[RD1], writes=[Rout])

        tok = b.barrier_token()
        b.emit("sync", lambda e: e.nop(), extra_waits=tok)
        b.run(sems)
    return nc



def make_in_maps(inputs):
    x = np.ascontiguousarray(inputs["L0_attn_hidden_states"])
    maps = []
    common = {
        "l0_wq": np.ascontiguousarray(inputs["L0_attn_Wq"]),
        "l0_wk": np.ascontiguousarray(inputs["L0_attn_Wk"]),
        "l0_wv": np.ascontiguousarray(inputs["L0_attn_Wv"]),
        "l0_wo": np.ascontiguousarray(inputs["L0_attn_Wo"]),
        "l0_bo": np.ascontiguousarray(inputs["L0_attn_bo"]).reshape(1, DM),
        "l0_router": np.ascontiguousarray(inputs["L0_ffn_router_w"]),
        "l0_gate_up": np.ascontiguousarray(inputs["L0_ffn_gate_up_w"]),
        "l0_down": np.ascontiguousarray(inputs["L0_ffn_down_w"]),
        "l0_sh_gate": np.ascontiguousarray(inputs["L0_ffn_shared_gate_w"]),
        "l0_sh_up": np.ascontiguousarray(inputs["L0_ffn_shared_up_w"]),
        "l0_sh_down": np.ascontiguousarray(inputs["L0_ffn_shared_down_w"]),
        "l1_wq": np.ascontiguousarray(inputs["L1_attn_Wq"]),
        "l1_wk": np.ascontiguousarray(inputs["L1_attn_Wk"]),
        "l1_wv": np.ascontiguousarray(inputs["L1_attn_Wv"]),
        "l1_wo": np.ascontiguousarray(inputs["L1_attn_Wo"]),
        "l1_wa": np.ascontiguousarray(inputs["L1_attn_Wa"]),
        "l1_wb": np.ascontiguousarray(inputs["L1_attn_Wb"]),
        "l1_qcw": np.ascontiguousarray(inputs["L1_attn_qcw"]),
        "l1_qcb": np.ascontiguousarray(inputs["L1_attn_qcb"]).reshape(1, DM),
        "l1_kcw": np.ascontiguousarray(inputs["L1_attn_kcw"]),
        "l1_kcb": np.ascontiguousarray(inputs["L1_attn_kcb"]).reshape(1, DM),
        "l1_vcw": np.ascontiguousarray(inputs["L1_attn_vcw"]),
        "l1_vcb": np.ascontiguousarray(inputs["L1_attn_vcb"]).reshape(1, 4096),
        "l1_w1": np.ascontiguousarray(inputs["L1_ffn_w1"]),
        "l1_w2": np.ascontiguousarray(inputs["L1_ffn_w2"]),
    }
    for i in range(8):
        m = dict(common)
        m["x0"] = np.ascontiguousarray(x[i])
        maps.append(m)
    return maps


def kernel(**inputs):
    nc = build_nc()
    maps = make_in_maps(inputs)
    res = run_bass_kernel_spmd(nc, maps, core_ids=list(range(8)))
    return np.stack([np.asarray(res.results[i]["out"], dtype=np.float32)
                     for i in range(8)], axis=0)


# version 05
# speedup vs baseline: 1.1966x; 1.0162x over previous
import contextlib
import numpy as np
import ml_dtypes
import concourse.bass as bass
import concourse.mybir as mybir
from concourse.bass_utils import run_bass_kernel_spmd

F32 = mybir.dt.float32
BF16 = mybir.dt.bfloat16
I32 = mybir.dt.int32
I16 = mybir.dt.int16
AF = mybir.ActivationFunctionType
ALU = mybir.AluOpType
AXX = mybir.AxisListType.X

T_SEQ = 2048
DM = 2048
NT = T_SEQ // 128

ENGINES = ("tensor", "vector", "scalar", "sync", "gpsimd")


class Res:
    __slots__ = ("name", "last_w", "readers")

    def __init__(self, name):
        self.name = name
        self.last_w = None
        self.readers = []


class B:

    def __init__(self, nc):
        self.nc = nc
        self.q = {e: [] for e in ENGINES}
        self.cnt = {}
        self.watermark = {e: {} for e in ENGINES}
        self.dma_idx = {"sync": 0, "gpsimd": 0, "scalar": 0}
        self.pending = {e: None for e in ENGINES}

    def sem(self, name):
        self.cnt[name] = 0

    def _waits(self, eng, toks):
        out = {}
        for t in toks:
            if t is None:
                continue
            s, v = t
            if v <= self.watermark[eng].get(s, 0):
                continue
            out[s] = max(out.get(s, 0), v)
        for s, v in out.items():
            self.watermark[eng][s] = v
        return list(out.items())

    def phase_barrier(self):
        toks = self.barrier_token()
        for e in ENGINES:
            self.pending[e] = toks

    def emit(self, eng, fn, reads=(), writes=(), sem=None, inc=0, extra_waits=()):
        toks = list(extra_waits)
        if self.pending[eng]:
            toks.extend(self.pending[eng])
            self.pending[eng] = None
        for r in reads:
            toks.append(r.last_w)
        for r in writes:
            toks.append(r.last_w)
            toks.extend(r.readers)
        if eng == "tensor":
            toks = [t for t in toks if t is None or t[0] != "t_tensor"]
        waits = self._waits(eng, toks)
        tok = None
        if inc:
            self.cnt[sem] += inc
            tok = (sem, self.cnt[sem])
        self.q[eng].append((fn, waits, sem if inc else None, inc))
        for r in reads:
            if tok is not None:
                r.readers.append(tok)
        for r in writes:
            r.last_w = tok
            r.readers = []
        return tok

    def op(self, eng, fn, reads=(), writes=(), extra_waits=()):
        return self.emit(eng, fn, reads, writes, sem="t_" + eng, inc=1,
                         extra_waits=extra_waits)

    NDMA = 8

    def dma(self, queue, fn, reads=(), writes=(), extra_waits=()):
        idx = self.dma_idx[queue]
        self.dma_idx[queue] = (idx + 1) % self.NDMA
        sem = f"d_{queue}{idx}"
        prev = (sem, self.cnt[sem]) if self.cnt[sem] > 0 else None
        ew = list(extra_waits) + ([prev] if prev else [])
        return self.emit(queue, fn, reads, writes, sem=sem, inc=16,
                         extra_waits=ew)

    def barrier_token(self):
        return [(s, v) for s, v in self.cnt.items() if v > 0]

    def run(self, sems):
        nc = self.nc
        with nc.Block() as blk:
            for eng in ENGINES:
                stream = self.q[eng]
                if not stream:
                    continue

                def make(eng, stream):
                    def body(e):
                        for fn, waits, sname, inc in stream:
                            for ws, wv in waits:
                                e.wait_ge(sems[ws], wv)
                            inst = fn(e)
                            if sname is not None:
                                inst.then_inc(sems[sname], inc)
                    return body

                getattr(blk, eng)(make(eng, stream))


SEM_NAMES = (["t_tensor", "t_vector", "t_scalar", "t_sync", "t_gpsimd"]
             + [f"d_sync{i}" for i in range(B.NDMA)]
             + [f"d_gpsimd{i}" for i in range(B.NDMA)]
             + [f"d_scalar{i}" for i in range(B.NDMA)])


def build_nc(upto="full", dump=None):
    nc = bass.Bass("TRN2", target_bir_lowering=False)

    dr = {}

    def din(name, shape):
        dr[name] = nc.dram_tensor(name, list(shape), F32, kind="ExternalInput").ap()

    din("x0", (T_SEQ, DM))
    din("l0_wq", (DM, DM)); din("l0_wk", (DM, DM)); din("l0_wv", (DM, DM))
    din("l0_wo", (DM, DM)); din("l0_bo", (1, DM))
    din("l0_router", (DM, 8))
    din("l0_gate_up", (8, DM, DM)); din("l0_down", (8, 1024, DM))
    din("l0_sh_gate", (DM, 1024)); din("l0_sh_up", (DM, 1024))
    din("l0_sh_down", (1024, DM))
    din("l1_wq", (DM, DM)); din("l1_wk", (DM, DM)); din("l1_wv", (DM, 4096))
    din("l1_wo", (4096, DM)); din("l1_wa", (DM, 16)); din("l1_wb", (DM, 16))
    din("l1_qcw", (DM, 4)); din("l1_qcb", (1, DM))
    din("l1_kcw", (DM, 4)); din("l1_kcb", (1, DM))
    din("l1_vcw", (4096, 4)); din("l1_vcb", (1, 4096))
    din("l1_w1", (8192, DM)); din("l1_w2", (DM, 8192))

    out = nc.dram_tensor("out", [T_SEQ, DM], F32, kind="ExternalOutput").ap()

    dr["ident"] = nc.inline_tensor(np.eye(128, dtype=ml_dtypes.bfloat16), "identb").ap()
    dr["identf"] = nc.inline_tensor(np.eye(128, dtype=np.float32), "identf").ap()
    msu = -np.triu(np.ones((128, 128), dtype=np.float32), 1)
    mui = np.triu(np.ones((128, 128), dtype=np.float32))
    dr["mask_su"] = nc.inline_tensor(msu.astype(ml_dtypes.bfloat16), "mask_su").ap()
    dr["mask_uif"] = nc.inline_tensor(mui.astype(np.float32), "mask_uif").ap()
    dr["mask_uib"] = nc.inline_tensor(mui.astype(ml_dtypes.bfloat16), "mask_uib").ap()
    dr["ones_row"] = nc.inline_tensor(np.ones((1, 128), dtype=np.float32), "ones_row").ap()
    dr["ones_rowb"] = nc.inline_tensor(np.ones((1, 128), dtype=ml_dtypes.bfloat16), "ones_rowb").ap()
    dr["ones_colb"] = nc.inline_tensor(np.ones((128, 1), dtype=ml_dtypes.bfloat16), "ones_colb").ap()

    x1_s = nc.dram_tensor("x1_s", [T_SEQ, DM], F32, kind="Internal").ap()
    q_s = nc.dram_tensor("q_s", [DM, T_SEQ], BF16, kind="Internal").ap()
    k_s = nc.dram_tensor("k_s", [DM, T_SEQ], BF16, kind="Internal").ap()
    v_s = nc.dram_tensor("v_s", [4096, T_SEQ], BF16, kind="Internal").ap()
    outs_s = nc.dram_tensor("outs_s", [4096, T_SEQ], BF16, kind="Internal").ap()
    o1_s = nc.dram_tensor("o1_s", [DM, T_SEQ], BF16, kind="Internal").ap()
    wq1t_s = nc.dram_tensor("wq1t_s", [DM, 8192], BF16, kind="Internal").ap()
    wq2t_s = nc.dram_tensor("wq2t_s", [8192, DM], BF16, kind="Internal").ap()
    xa_s = nc.dram_tensor("xa_s", [T_SEQ, DM], F32, kind="Internal").ap()

    es = contextlib.ExitStack()
    with es:
        A = es.enter_context(nc.sbuf_tensor("A", [128, NT * DM], BF16))
        Bacc = es.enter_context(nc.sbuf_tensor("Bacc", [128, NT * DM], BF16))
        C1 = es.enter_context(nc.sbuf_tensor("C1", [128, 24576], BF16))
        D1 = es.enter_context(nc.sbuf_tensor("D1", [128, 2048], F32))
        D2 = es.enter_context(nc.sbuf_tensor("D2", [128, 4096], BF16))
        ITL = es.enter_context(nc.sbuf_tensor("ITL", [128, 1024], I16))
        idb = es.enter_context(nc.sbuf_tensor("idb", [128, 128], BF16))
        idf = es.enter_context(nc.sbuf_tensor("idf", [128, 128], F32))
        msut = es.enter_context(nc.sbuf_tensor("msut", [128, 128], BF16))
        muift = es.enter_context(nc.sbuf_tensor("muift", [128, 128], F32))
        onesr = es.enter_context(nc.sbuf_tensor("onesr", [1, 128], F32))
        onesrb = es.enter_context(nc.sbuf_tensor("onesrb", [1, 128], BF16))
        stat = es.enter_context(nc.sbuf_tensor("stat", [128, 96], F32))
        wsm = es.enter_context(nc.sbuf_tensor("wsm", [128, 128], BF16))
        rw = es.enter_context(nc.sbuf_tensor("rw", [128, 128], F32))
        smallf = es.enter_context(nc.sbuf_tensor("smallf", [128, 64], F32))
        cst = es.enter_context(nc.sbuf_tensor("cst", [128, 8], F32))
        hwrk = es.enter_context(nc.sbuf_tensor("hwrk", [128, 256], BF16))
        muibt = es.enter_context(nc.sbuf_tensor("muibt", [128, 128], BF16))
        onescb = es.enter_context(nc.sbuf_tensor("onescb", [128, 1], BF16))
        statx = es.enter_context(nc.sbuf_tensor("statx", [128, 1216], F32))
        scanb = es.enter_context(nc.sbuf_tensor("scanb", [128, 1920], BF16))
        bT = es.enter_context(nc.sbuf_tensor("bT", [128, 256], BF16))
        PA = es.enter_context(nc.psum_tensor("PA", [128, 2048], F32))
        PBp = es.enter_context(nc.psum_tensor("PBp", [128, 1024], F32))
        PT = es.enter_context(nc.psum_tensor("PT", [128, 1024], BF16))
        PT2 = es.enter_context(nc.psum_tensor("PT2", [128, 1024], BF16))
        sems = {n: es.enter_context(nc.semaphore(n)) for n in SEM_NAMES}

        b = B(nc)
        for n in SEM_NAMES:
            b.sem(n)

        RA = [Res(f"A{i}") for i in range(NT)]
        RB = [Res(f"B{i}") for i in range(NT)]
        RQ = [Res(f"Q{i}") for i in range(3)]
        RO = [Res(f"O{i}") for i in range(2)]
        RD1, RD2, RIT = Res("D1"), Res("D2"), Res("ITL")
        RIT1 = Res("ITL1")
        Rstat, Rrw = Res("stat"), Res("rw")
        Rsm, Rbob, Rwsm = Res("smallf"), Res("bob"), Res("wsm")
        Rhw = Res("hwrk")
        Rhw1 = Res("hwrk1")
        RHF = [Res(f"hf{i}") for i in range(8)]
        Rconst = Res("const")
        Rout = Res("out")
        Rscr = {n: Res(n) for n in
                ("x1_s", "q_s", "k_s", "v_s", "outs_s", "wq1t_s", "wq2t_s",
                 "o1_s", "xa_s")}
        RPA = [Res(f"PA{i}") for i in range(4)]
        RPB = [Res(f"PB{i}") for i in range(2)]
        RPT, RPT2 = Res("PT"), Res("PT2")

        def PAb(i):
            return PA[:, i * 512:(i + 1) * 512]

        def Q(k, lo, hi):
            return C1[:, k * 8192 + lo: k * 8192 + hi]

        Rcst = Res("cst")
        Rstx, Rscb, RbT = Res("statx"), Res("scanb"), Res("bT")
        CEPS, CHALF, CNEG1, CONE = (lambda c: None), None, None, None

        def cstc(i):
            return cst[:, i:i + 1]

        def load_consts():
            b.op("vector", lambda e: e.memset(cst[:, 0:1], 1e-5), writes=[Rcst])
            b.op("vector", lambda e: e.memset(cst[:, 1:2], 0.5), writes=[Rcst])
            b.op("vector", lambda e: e.memset(cst[:, 2:3], -1.0), writes=[Rcst])
            b.op("vector", lambda e: e.memset(cst[:, 3:4], 1.0), writes=[Rcst])
            b.dma("sync", lambda e: e.dma_start(muibt[:], dr["mask_uib"][:, :]), writes=[Rconst])
            b.dma("sync", lambda e: e.dma_start(onescb[:], dr["ones_colb"][:, :]), writes=[Rconst])
            b.dma("sync", lambda e: e.dma_start(idb[:], dr["ident"][:, :]), writes=[Rconst])
            b.dma("sync", lambda e: e.dma_start(idf[:], dr["identf"][:, :]), writes=[Rconst])
            b.dma("sync", lambda e: e.dma_start(msut[:], dr["mask_su"][:, :]), writes=[Rconst])
            b.dma("sync", lambda e: e.dma_start(muift[:], dr["mask_uif"][:, :]), writes=[Rconst])
            b.dma("sync", lambda e: e.dma_start(onesr[:], dr["ones_row"][:, :]), writes=[Rconst])
            b.dma("sync", lambda e: e.dma_start(onesrb[:], dr["ones_rowb"][:, :]), writes=[Rconst])
            b.dma("gpsimd", lambda e: e.dma_start(
                wsm[:, 0:128].rearrange("p (c e) -> p c e", e=8),
                dr["l0_router"].rearrange("(c p) e -> p c e", p=128)),
                writes=[Rwsm])

        def transpose(dst_ap, src_ap, rdst, rsrc, use2=False):
            pr = RPT2 if use2 else RPT
            pap = (PT2 if use2 else PT)[:, 0:128]
            b.op("tensor", lambda e: e.transpose(pap, src_ap, idb[:]),
                 reads=[rsrc, Rconst], writes=[pr])
            b.op("vector", lambda e: e.tensor_copy(dst_ap, pap),
                 reads=[pr], writes=[rdst])

        def wload(dst_ap, src_ap, wres, extra=()):
            if not isinstance(wres, (list, tuple)):
                wres = [wres]
            return b.dma("gpsimd", lambda e: e.dma_start(dst_ap, src_ap),
                         writes=wres, extra_waits=extra)

        def phase_ln(xin_ap, s1_track):
            for i in range(NT):
                b.dma("sync", lambda e, i=i: e.dma_start(
                    D1[:, :], xin_ap[i * 128:(i + 1) * 128, :]), writes=[RD1])
                b.op("vector", lambda e: e.reduce_sum(stat[:, 0:1], D1[:, :], AXX),
                     reads=[RD1], writes=[Rstat])
                b.op("scalar", lambda e: e.activation(
                    D2[:, 0:2048], D1[:, :], AF.Square, accum_out=stat[:, 1:2]),
                    reads=[RD1], writes=[RD2, Rstat])
                b.op("vector", lambda e: e.tensor_scalar(
                    stat[:, 2:3], stat[:, 0:1], 1.0 / DM, None, ALU.mult),
                    reads=[Rstat], writes=[Rstat])
                b.op("vector", lambda e: e.tensor_scalar(
                    stat[:, 3:4], stat[:, 1:2], 1.0 / DM, None, ALU.mult),
                    reads=[Rstat], writes=[Rstat])
                b.op("vector", lambda e: e.scalar_tensor_tensor(
                    stat[:, 4:5], stat[:, 2:3], stat[:, 2:3], stat[:, 3:4],
                    ALU.mult, ALU.subtract),
                    reads=[Rstat], writes=[Rstat])
                b.op("scalar", lambda e: e.activation(
                    stat[:, 5:6], stat[:, 4:5], AF.Sqrt, bias=cstc(0), scale=cstc(2)),
                    reads=[Rstat, Rcst], writes=[Rstat])
                b.op("vector", lambda e: e.reciprocal(stat[:, 6:7], stat[:, 5:6]),
                     reads=[Rstat], writes=[Rstat])
                b.op("vector", lambda e: e.tensor_scalar(
                    D2[:, 0:2048], D1[:, :], stat[:, 2:3], stat[:, 6:7],
                    ALU.subtract, ALU.mult),
                    reads=[RD1, Rstat], writes=[RD2])
                if s1_track:
                    b.op("vector", lambda e, i=i: e.reduce_max(
                        stat[:, 32 + i:33 + i], D2[:, 0:2048], AXX,
                        apply_absolute_value=True),
                        reads=[RD2], writes=[Rstat])
                for c in range(16):
                    transpose(A[:, c * 2048 + i * 128: c * 2048 + (i + 1) * 128],
                              D2[:, c * 128:(c + 1) * 128],
                              RA[c], RD2, use2=(c % 2 == 1))

        def proj_modeT(wdram, fdim, evac):
            nq = fdim // 512
            for fq in range(nq):
                k = fq % 2
                wr = RQ[k]
                wload(Q(k, 0, 8192).rearrange("p (c f) -> p c f", f=512),
                      wdram[:, fq * 512:(fq + 1) * 512]
                      .rearrange("(c p) f -> p c f", p=128), wr)
                for i in range(NT):
                    bank = RPA[i % 2]
                    bap = PAb(i % 2)
                    for kc in range(16):
                        b.op("tensor", lambda e, i=i, kc=kc, k=k, bap=bap: e.matmul(
                            bap,
                            A[:, kc * 2048 + i * 128: kc * 2048 + (i + 1) * 128],
                            Q(k, kc * 512, (kc + 1) * 512),
                            start=(kc == 0), stop=(kc == 15)),
                            reads=[RA[kc], wr], writes=[bank])
                    evac(i, fq, bap, bank)

        def phase_l0_kv():
            def evk(i, fq, bap, bank):
                b.op("scalar", lambda e, i=i, fq=fq, bap=bap: e.activation(
                    Bacc[:, i * 2048 + fq * 512: i * 2048 + (fq + 1) * 512],
                    bap, AF.Relu),
                    reads=[bank], writes=[RB[i]])
            proj_modeT(dr["l0_wk"], DM, evk)

            b.op("vector", lambda e: e.memset(
                D2[:, 0:520].rearrange("p (h c) -> p h c", c=65)[:, :, 64:65], 1.0),
                writes=[RD2])

            kv_first = {}

            def kvslot(j):
                if j < 7:
                    return RPB[0], PBp[:, j * 65:(j + 1) * 65]
                if j < 14:
                    return RPB[1], PBp[:, 512 + (j - 7) * 65: 512 + (j - 6) * 65]
                return RPA[3], PA[:, 1536 + (j - 14) * 65: 1536 + (j - 13) * 65]

            def evv(i, fq, bap, bank):
                b.op("vector", lambda e, bap=bap: e.tensor_copy(
                    D2[:, 0:520].rearrange("p (h c) -> p h c", c=65)[:, :, 0:64],
                    bap.rearrange("p (h c) -> p h c", c=64)),
                    reads=[bank], writes=[RD2])
                for hh in range(4):
                    h0 = 8 * fq + 2 * hh
                    j = h0 // 2
                    kr, kap = kvslot(j)
                    for half in range(2):
                        h = h0 + half
                        po = 64 * half
                        first = (id(kr), po) not in kv_first
                        kv_first[(id(kr), po)] = True
                        b.op("tensor",
                             lambda e, i=i, h=h, hh=hh, half=half, kap=kap, po=po, first=first: e.matmul(
                                 kap[po:po + 64, :],
                                 Bacc[:, i * 2048 + h * 64: i * 2048 + (h + 1) * 64],
                                 D2[:, (2 * hh + half) * 65:(2 * hh + half + 1) * 65],
                                 start=first, stop=(i == NT - 1),
                                 skip_group_check=True,
                                 tile_position=(0, po)),
                             reads=[RB[i], RD2], writes=[kr])
            proj_modeT(dr["l0_wv"], DM, evv)
            for j in range(16):
                kr, kap = kvslot(j)
                b.op("vector", lambda e, kap=kap, j=j: e.tensor_copy(
                    scanb[:, j * 65:(j + 1) * 65], kap),
                    reads=[kr], writes=[Rscb])

        def phase_l0_qo(sub=3):
            b.dma("gpsimd", lambda e: e.dma_start(D2[0:1, 0:2048], dr["l0_bo"][0:1, :]),
                  writes=[RD2])
            for fb in range(4):
                b.op("tensor", lambda e, fb=fb: e.matmul(
                    PAb(0), onesrb[:, :], D2[0:1, fb * 512:(fb + 1) * 512],
                    start=True, stop=True),
                    reads=[RD2, Rconst], writes=[RPA[0]])
                b.op("vector", lambda e, fb=fb: e.tensor_copy(
                    C1[:, 22528:24576][:, fb * 512:(fb + 1) * 512], PAb(0)),
                    reads=[RPA[0]], writes=[Rbob])

            def oslot(ts, half):
                lo = half * 512 + ts * 65
                return RPB[half], PBp[:, lo:lo + 65]

            if sub < 2:
                return
            for fq in range(4):
                k = fq % 2
                wr = RQ[k]
                wload(Q(k, 0, 8192).rearrange("p (c f) -> p c f", f=512),
                      dr["l0_wq"][:, fq * 512:(fq + 1) * 512]
                      .rearrange("(c p) f -> p c f", p=128), wr)
                for fc in range(4):
                    h0 = fq * 8 + fc * 2
                    j = h0 // 2
                    for tg in range(4):
                        bank = RPA[tg % 2]
                        bap = PAb(tg % 2)
                        for kc in range(16):
                            b.op("tensor", lambda e, fc=fc, tg=tg, kc=kc, k=k, bap=bap: e.matmul(
                                bap,
                                Q(k, kc * 512 + fc * 128, kc * 512 + (fc + 1) * 128),
                                A[:, kc * 2048 + tg * 512:kc * 2048 + (tg + 1) * 512],
                                start=(kc == 0), stop=(kc == 15)),
                                reads=[RA[kc], wr], writes=[bank])
                        b.op("scalar", lambda e, bap=bap: e.activation(
                            D2[:, 1024:1536], bap, AF.Relu),
                            reads=[bank], writes=[RD2])
                        for half in range(2):
                            po = 64 * half
                            for ts in range(4):
                                sr, sap = oslot(ts, half)
                                b.op("tensor", lambda e, ts=ts, j=j, po=po, sap=sap: e.matmul(
                                    sap,
                                    D2[po:po + 64, 1024 + ts * 128:1024 + (ts + 1) * 128],
                                    scanb[po:po + 64, j * 65:(j + 1) * 65],
                                    start=True, stop=True, skip_group_check=True),
                                    reads=[RD2, Rscb], writes=[sr])
                        for g in range(2):
                            b.op("vector", lambda e, g=g: e.tensor_scalar_max(
                                smallf[:, g * 4:(g + 1) * 4],
                                PBp[:, g * 512 + 64: g * 512 + 64 + 260]
                                .rearrange("p (s c) -> p s c", c=65)[:, :, 0:1],
                                1e-6),
                                reads=[RPB[g]], writes=[Rsm])
                        b.op("vector", lambda e: e.reciprocal(
                            smallf[:, 8:16], smallf[:, 0:8]),
                            reads=[Rsm], writes=[Rsm])
                        for ts in range(4):
                            for half in range(2):
                                sr, sap = oslot(ts, half)
                                b.op("vector", lambda e, ts=ts, half=half, sap=sap: e.tensor_scalar(
                                    D2[:, 2048 + ts * 128 + half * 64:
                                        2048 + ts * 128 + half * 64 + 64],
                                    sap[:, 0:64],
                                    smallf[:, 8 + half * 4 + ts:9 + half * 4 + ts],
                                    None, ALU.mult),
                                    reads=[sr, Rsm], writes=[RD2])
                        for ts in range(4):
                            transpose(D2[:, 2560 + ts * 128:2560 + (ts + 1) * 128],
                                      D2[:, 2048 + ts * 128:2048 + (ts + 1) * 128],
                                      RD2, RD2, use2=(ts % 2 == 1))
                        b.dma("sync", lambda e, h0=h0, tg=tg: e.dma_start(
                            o1_s[h0 * 64:h0 * 64 + 128,
                                 tg * 512:(tg + 1) * 512],
                            D2[:, 2560:3072]),
                            reads=[RD2], writes=[Rscr["o1_s"]])
            if sub < 3:
                return
            for fh in range(2):
                wr = [RQ[0], RQ[1]]
                wload(C1[:, 0:16384].rearrange("p (c f) -> p c f", f=1024),
                      dr["l0_wo"][:, fh * 1024:(fh + 1) * 1024]
                      .rearrange("(c p) f -> p c f", p=128), wr)
                for i in range(NT):
                    ob = RO[i % 2]
                    ocol = 16384 + (i % 2) * 2048
                    b.dma("sync", lambda e, i=i, ocol=ocol: e.dma_start(
                        C1[:, ocol:ocol + 2048].rearrange("p (c f) -> p c f", f=128),
                        o1_s[:, i * 128:(i + 1) * 128]
                        .rearrange("(c p) f -> p c f", p=128)),
                        reads=[Rscr["o1_s"]], writes=[ob])
                    for fb in range(2):
                        bank = RPA[2 * (i % 2) + fb]
                        bap = PAb(2 * (i % 2) + fb)
                        for c in range(16):
                            b.op("tensor", lambda e, c=c, fb=fb, bap=bap, ocol=ocol: e.matmul(
                                bap, C1[:, ocol + c * 128:ocol + (c + 1) * 128],
                                C1[:, c * 1024 + fb * 512:c * 1024 + (fb + 1) * 512],
                                start=(c == 0), stop=(c == 15)),
                                reads=[ob] + wr, writes=[bank])
                        b.op("vector", lambda e, i=i, fh=fh, fb=fb, bap=bap: e.tensor_tensor(
                            Bacc[:, i * 2048 + fh * 1024 + fb * 512:
                                 i * 2048 + fh * 1024 + (fb + 1) * 512],
                            bap, C1[:, 22528:24576][:, fh * 1024 + fb * 512:
                                      fh * 1024 + (fb + 1) * 512],
                            ALU.add),
                            reads=[bank, Rbob], writes=[RB[i]])

        def phase_router():
            for i in range(NT):
                bap = PAb(i % 2)
                bank = RPA[i % 2]
                for kc in range(16):
                    b.op("tensor", lambda e, i=i, kc=kc, bap=bap: e.matmul(
                        bap[:, 0:8],
                        A[:, kc * 2048 + i * 128: kc * 2048 + (i + 1) * 128],
                        wsm[:, kc * 8:(kc + 1) * 8],
                        start=(kc == 0), stop=(kc == 15)),
                        reads=[RA[kc], Rwsm], writes=[bank])
                lg = smallf[:, 16:24]
                b.op("vector", lambda e, bap=bap: e.tensor_copy(lg, bap[:, 0:8]),
                     reads=[bank], writes=[Rsm])
                b.op("vector", lambda e: e.reduce_max(smallf[:, 24:25], lg, AXX),
                     reads=[Rsm], writes=[Rsm])
                b.op("vector", lambda e: e.tensor_scalar(
                    smallf[:, 32:40], lg, smallf[:, 24:25], None, ALU.is_ge),
                    reads=[Rsm], writes=[Rsm])
                b.op("vector", lambda e: e.scalar_tensor_tensor(
                    smallf[:, 40:48], smallf[:, 32:40], -1e30, lg,
                    ALU.mult, ALU.add),
                    reads=[Rsm], writes=[Rsm])
                b.op("vector", lambda e: e.reduce_max(
                    smallf[:, 25:26], smallf[:, 40:48], AXX),
                    reads=[Rsm], writes=[Rsm])
                b.op("vector", lambda e: e.tensor_scalar(
                    smallf[:, 32:40], lg, smallf[:, 25:26], None, ALU.is_ge),
                    reads=[Rsm], writes=[Rsm])
                b.op("scalar", lambda e: e.activation(
                    smallf[:, 40:48], lg, AF.Sigmoid),
                    reads=[Rsm], writes=[Rsm])
                b.op("vector", lambda e, i=i: e.tensor_tensor(
                    rw[:, i * 8:(i + 1) * 8], smallf[:, 40:48], smallf[:, 32:40],
                    ALU.mult),
                    reads=[Rsm], writes=[Rrw])

        def phase_l0_moe(nexp=9):
            RQD = [Res(f"qd{i}") for i in range(8)]
            RHW = [Res(f"hws{i}") for i in range(4)]
            for e in list(range(8))[:max(0, nexp - 1)] + [8]:
                shared = (e == 8)
                for q in range(4):
                    dsrc = (dr["l0_sh_down"] if shared else dr["l0_down"][e])
                    for fh in range(2):
                        for pc in range(8):
                            b.dma("gpsimd", lambda e_, fh=fh, pc=pc, dsrc=dsrc: e_.dma_start(
                                Q(2, pc * 1024, (pc + 1) * 1024),
                                dsrc[pc * 128:(pc + 1) * 128,
                                     fh * 1024:(fh + 1) * 1024]),
                                writes=[RQD[pc]])
                        if fh == 0:
                            pend_gu = []
                            for fb in range(4):
                                k = fb % 2
                                gud = Q(k, 0, 8192).rearrange(
                                    "p (c f) -> p c f", f=512)
                                if shared:
                                    wload(gud[:, :, 0:256],
                                          dr["l0_sh_gate"][:, fb * 256:(fb + 1) * 256]
                                          .rearrange("(c p) f -> p c f", p=128), RQ[k])
                                    wload(gud[:, :, 256:512],
                                          dr["l0_sh_up"][:, fb * 256:(fb + 1) * 256]
                                          .rearrange("(c p) f -> p c f", p=128), RQ[k])
                                else:
                                    wload(gud[:, :, 0:256],
                                          dr["l0_gate_up"][e][:, fb * 256:(fb + 1) * 256]
                                          .rearrange("(c p) f -> p c f", p=128), RQ[k])
                                    wload(gud[:, :, 256:512],
                                          dr["l0_gate_up"][e][:, 1024 + fb * 256:1024 + (fb + 1) * 256]
                                          .rearrange("(c p) f -> p c f", p=128), RQ[k])
                                for t4 in range(4):
                                    i = q * 4 + t4
                                    bap = PBp[:, (t4 % 2) * 512:(t4 % 2) * 512 + 512]
                                    bank = RPB[t4 % 2]
                                    for kc in range(16):
                                        b.op("tensor", lambda e_, i=i, kc=kc, k=k, bap=bap: e_.matmul(
                                            bap,
                                            A[:, kc * 2048 + i * 128: kc * 2048 + (i + 1) * 128],
                                            Q(k, kc * 512, (kc + 1) * 512),
                                            start=(kc == 0), stop=(kc == 15)),
                                            reads=[RA[kc], RQ[k]], writes=[bank])
                                    if t4 < 3:
                                        hwg = scanb[:, t4 * 512:t4 * 512 + 256]
                                        hwh = scanb[:, t4 * 512 + 256:t4 * 512 + 512]
                                    else:
                                        hwg = hwrk[:, 0:256]
                                        hwh = scanb[:, 1536:1792]
                                    rhw = RHW[t4]
                                    if shared:
                                        b.op("scalar", lambda e_, bap=bap, hwg=hwg: e_.activation(
                                            hwg, bap[:, 0:256], AF.Silu),
                                            reads=[bank], writes=[rhw])
                                        b.op("vector", lambda e_, bap=bap, hwg=hwg, hwh=hwh: e_.tensor_tensor(
                                            hwh, bap[:, 256:512],
                                            hwg, ALU.mult),
                                            reads=[bank, rhw], writes=[rhw])
                                    else:
                                        b.op("scalar", lambda e_, i=i, e=e, bap=bap, hwg=hwg: e_.activation(
                                            hwg, bap[:, 0:256], AF.Silu,
                                            scale=rw[:, i * 8 + e:i * 8 + e + 1]),
                                            reads=[bank, Rrw], writes=[rhw])
                                        b.op("vector", lambda e_, i=i, e=e, bap=bap, hwg=hwg, hwh=hwh: e_.scalar_tensor_tensor(
                                            hwh, bap[:, 256:512],
                                            rw[:, i * 8 + e:i * 8 + e + 1],
                                            hwg, ALU.mult, ALU.mult),
                                            reads=[bank, Rrw, rhw], writes=[rhw])
                                    def mk_tr(fb=fb, t4=t4, hwh=hwh, rhw=rhw):
                                        def tr():
                                            for half in range(2):
                                                ptx = PT if half == 0 else PT2
                                                rptx = RPT if half == 0 else RPT2
                                                b.op("tensor", lambda e_, half=half, hwh=hwh, ptx=ptx: e_.transpose(
                                                    ptx[:, 0:128],
                                                    hwh[:, half * 128:(half + 1) * 128],
                                                    idb[:]),
                                                    reads=[rhw, Rconst], writes=[rptx])
                                                b.op("vector", lambda e_, fb=fb, t4=t4, half=half, ptx=ptx: e_.tensor_copy(
                                                    D2[:, (2 * fb + half) * 512 + t4 * 128:
                                                          (2 * fb + half) * 512 + (t4 + 1) * 128],
                                                    ptx[:, 0:128]),
                                                    reads=[rptx], writes=[RHF[2 * fb + half]])
                                        return tr
                                    pend_gu.append(mk_tr())
                                    if len(pend_gu) > 1:
                                        pend_gu.pop(0)()
                            while pend_gu:
                                pend_gu.pop(0)()
                        for t4 in range(4):
                            i = q * 4 + t4
                            bap = PA[:, (t4 % 2) * 1024:(t4 % 2) * 1024 + 1024]
                            banks = [RPA[(t4 % 2) * 2], RPA[(t4 % 2) * 2 + 1]]
                            for ic in range(8):
                                for fb2 in range(2):
                                    b.op("tensor", lambda e_, ic=ic, fb2=fb2, t4=t4, bap=bap: e_.matmul(
                                        bap[:, fb2 * 512:(fb2 + 1) * 512],
                                        D2[:, ic * 512 + t4 * 128: ic * 512 + (t4 + 1) * 128],
                                        Q(2, ic * 1024 + fb2 * 512, ic * 1024 + (fb2 + 1) * 512),
                                        start=(ic == 0), stop=(ic == 7)),
                                        reads=[RHF[ic], RQD[ic]], writes=[banks[fb2]])
                            b.op("vector", lambda e_, i=i, fh=fh, bap=bap: e_.tensor_tensor(
                                Bacc[:, i * 2048 + fh * 1024: i * 2048 + (fh + 1) * 1024],
                                bap,
                                Bacc[:, i * 2048 + fh * 1024: i * 2048 + (fh + 1) * 1024],
                                ALU.add),
                                reads=banks + [RB[i]], writes=[RB[i]])

        def phase_x1_ln():
            for i in range(NT):
                xcol = (i % 2) * 2048
                xres = RQ[i % 2]
                b.dma("gpsimd", lambda e, i=i, xcol=xcol: e.dma_start(
                    Q(0, xcol, xcol + 2048), dr["x0"][i * 128:(i + 1) * 128, :]),
                    writes=[xres])
                b.op("vector", lambda e, i=i, xcol=xcol: e.tensor_tensor(
                    D1[:, :], Q(0, xcol, xcol + 2048),
                    Bacc[:, i * 2048:(i + 1) * 2048], ALU.add),
                    reads=[xres, RB[i]], writes=[RD1])
                b.dma("sync", lambda e, i=i: e.dma_start(
                    x1_s[i * 128:(i + 1) * 128, :], D1[:, :]),
                    reads=[RD1], writes=[Rscr["x1_s"]])
                b.op("vector", lambda e: e.reduce_sum(stat[:, 0:1], D1[:, :], AXX),
                     reads=[RD1], writes=[Rstat])
                b.op("scalar", lambda e: e.activation(
                    D2[:, 0:2048], D1[:, :], AF.Square, accum_out=stat[:, 1:2]),
                    reads=[RD1], writes=[RD2, Rstat])
                b.op("vector", lambda e: e.tensor_scalar(
                    stat[:, 2:3], stat[:, 0:1], 1.0 / DM, None, ALU.mult),
                    reads=[Rstat], writes=[Rstat])
                b.op("vector", lambda e: e.tensor_scalar(
                    stat[:, 3:4], stat[:, 1:2], 1.0 / DM, None, ALU.mult),
                    reads=[Rstat], writes=[Rstat])
                b.op("vector", lambda e: e.scalar_tensor_tensor(
                    stat[:, 4:5], stat[:, 2:3], stat[:, 2:3], stat[:, 3:4],
                    ALU.mult, ALU.subtract),
                    reads=[Rstat], writes=[Rstat])
                b.op("scalar", lambda e: e.activation(
                    stat[:, 5:6], stat[:, 4:5], AF.Sqrt, bias=cstc(0), scale=cstc(2)),
                    reads=[Rstat, Rcst], writes=[Rstat])
                b.op("vector", lambda e: e.reciprocal(stat[:, 6:7], stat[:, 5:6]),
                     reads=[Rstat], writes=[Rstat])
                b.op("vector", lambda e: e.tensor_scalar(
                    D2[:, 0:2048], D1[:, :], stat[:, 2:3], stat[:, 6:7],
                    ALU.subtract, ALU.mult),
                    reads=[RD1, Rstat], writes=[RD2])
                b.op("vector", lambda e, i=i: e.reduce_max(
                    stat[:, 32 + i:33 + i], D2[:, 0:2048], AXX,
                    apply_absolute_value=True),
                    reads=[RD2], writes=[Rstat])
                for c in range(16):
                    transpose(A[:, c * 2048 + i * 128: c * 2048 + (i + 1) * 128],
                              D2[:, c * 128:(c + 1) * 128],
                              RA[c], RD2, use2=(c % 2 == 1))

        def phase_l1_proj():
            b.dma("sync", lambda e: e.dma_start(
                statx[:, 0:64].rearrange("p (c j) -> p c j", j=4),
                dr["l1_qcw"].rearrange("(c p) j -> p c j", p=128)), writes=[Rstx])
            b.dma("sync", lambda e: e.dma_start(
                statx[:, 64:128].rearrange("p (c j) -> p c j", j=4),
                dr["l1_kcw"].rearrange("(c p) j -> p c j", p=128)), writes=[Rstx])
            b.dma("sync", lambda e: e.dma_start(
                statx[:, 128:256].rearrange("p (c j) -> p c j", j=4),
                dr["l1_vcw"].rearrange("(c p) j -> p c j", p=128)), writes=[Rstx])
            b.dma("sync", lambda e: e.dma_start(
                statx[:, 256:272].rearrange("p (c o) -> p c o", o=1),
                dr["l1_qcb"].rearrange("o (c p) -> p c o", p=128),
                allow_slow_non_contiguous=True), writes=[Rstx])
            b.dma("sync", lambda e: e.dma_start(
                statx[:, 272:288].rearrange("p (c o) -> p c o", o=1),
                dr["l1_kcb"].rearrange("o (c p) -> p c o", p=128),
                allow_slow_non_contiguous=True), writes=[Rstx])
            b.dma("sync", lambda e: e.dma_start(
                statx[:, 288:320].rearrange("p (c o) -> p c o", o=1),
                dr["l1_vcb"].rearrange("o (c p) -> p c o", p=128),
                allow_slow_non_contiguous=True), writes=[Rstx])

            plans = [
                (dr["l1_wq"], q_s, 4, True, 0, 256),
                (dr["l1_wk"], k_s, 4, True, 64, 272),
                (dr["l1_wv"], v_s, 8, False, 128, 288),
            ]
            ZC, T1 = 0, 516
            for wd, scr, nq, l2, cwo, cbo in plans:
                for fq in range(nq):
                    k_ = fq % 2
                    wload(Q(k_, 0, 8192).rearrange("p (c f) -> p c f", f=512),
                          wd[:, fq * 512:(fq + 1) * 512]
                          .rearrange("(c p) f -> p c f", p=128), RQ[k_])
                    for fc in range(4):
                        ch = fq * 4 + fc
                        b.op("vector", lambda e: e.memset(
                            scanb[:, ZC:ZC + 3], 0.0), writes=[Rscb])
                        for tg in range(4):
                            bank = RPA[tg % 2]
                            bap = PAb(tg % 2)
                            for kc in range(16):
                                b.op("tensor", lambda e, fc=fc, tg=tg, kc=kc, k_=k_, bap=bap: e.matmul(
                                    bap,
                                    Q(k_, kc * 512 + fc * 128, kc * 512 + (fc + 1) * 128),
                                    A[:, kc * 2048 + tg * 512:kc * 2048 + (tg + 1) * 512],
                                    start=(kc == 0), stop=(kc == 15)),
                                    reads=[RA[kc], RQ[k_]], writes=[bank])
                            b.op("vector", lambda e, bap=bap: e.tensor_copy(
                                scanb[:, ZC + 3:ZC + 515], bap),
                                reads=[bank], writes=[Rscb])
                            b.op("vector", lambda e, ch=ch, cwo=cwo: e.tensor_scalar(
                                scanb[:, T1:T1 + 512], scanb[:, ZC:ZC + 512],
                                statx[:, cwo + ch * 4:cwo + ch * 4 + 1], None, ALU.mult),
                                reads=[Rscb, Rstx], writes=[Rscb])
                            for j in (1, 2, 3):
                                b.op("vector", lambda e, j=j, ch=ch, cwo=cwo: e.scalar_tensor_tensor(
                                    scanb[:, T1:T1 + 512], scanb[:, ZC + j:ZC + j + 512],
                                    statx[:, cwo + ch * 4 + j:cwo + ch * 4 + j + 1],
                                    scanb[:, T1:T1 + 512], ALU.mult, ALU.add),
                                    reads=[Rscb, Rstx], writes=[Rscb])
                            b.op("vector", lambda e: e.tensor_copy(
                                scanb[:, ZC:ZC + 3], scanb[:, ZC + 512:ZC + 515]),
                                reads=[Rscb], writes=[Rscb])
                            b.op("scalar", lambda e, tg=tg, ch=ch, cbo=cbo: e.activation(
                                D2[:, tg * 512:(tg + 1) * 512], scanb[:, T1:T1 + 512],
                                AF.Silu, bias=statx[:, cbo + ch:cbo + ch + 1]),
                                reads=[Rscb, Rstx], writes=[RD2])
                            if l2:
                                b.op("vector", lambda e, tg=tg: e.tensor_tensor(
                                    scanb[:, T1:T1 + 512],
                                    D2[:, tg * 512:(tg + 1) * 512],
                                    D2[:, tg * 512:(tg + 1) * 512], ALU.mult),
                                    reads=[RD2], writes=[Rscb])
                                b.op("tensor", lambda e: e.matmul(
                                    PBp[0:1, 0:512], onescb[:], scanb[:, T1:T1 + 512],
                                    start=True, stop=True),
                                    reads=[Rscb, Rconst], writes=[RPB[0]])
                                b.op("scalar", lambda e: e.activation(
                                    D1[0:1, 0:512], PBp[0:1, 0:512], AF.Sqrt),
                                    reads=[RPB[0]], writes=[RD1])
                                b.op("vector", lambda e: e.tensor_scalar_max(
                                    D1[0:1, 0:512], D1[0:1, 0:512], 1e-12),
                                    reads=[RD1], writes=[RD1])
                                b.op("vector", lambda e: e.reciprocal(
                                    D1[0:1, 512:1024], D1[0:1, 0:512]),
                                    reads=[RD1], writes=[RD1])
                                b.op("tensor", lambda e: e.matmul(
                                    PBp[:, 512:1024], onesr[:], D1[0:1, 512:1024],
                                    start=True, stop=True),
                                    reads=[RD1, Rconst], writes=[RPB[1]])
                                b.op("vector", lambda e, tg=tg: e.tensor_tensor(
                                    D2[:, tg * 512:(tg + 1) * 512],
                                    D2[:, tg * 512:(tg + 1) * 512],
                                    PBp[:, 512:1024], ALU.mult),
                                    reads=[RD2, RPB[1]], writes=[RD2])
                        b.dma("sync", lambda e, scr=scr, ch=ch: e.dma_start(
                            scr[ch * 128:(ch + 1) * 128, :], D2[:, 0:2048]),
                            reads=[RD2], writes=[Rscr[{id(q_s): "q_s", id(k_s): "k_s", id(v_s): "v_s"}[id(scr)]]])
            b.dma("gpsimd", lambda e: e.dma_start(
                C1[:, 22528:24576][:, 0:256].rearrange("p (c f) -> p c f", f=16),
                dr["l1_wa"].rearrange("(c p) f -> p c f", p=128)), writes=[Rbob])
            b.dma("gpsimd", lambda e: e.dma_start(
                C1[:, 22528:24576][:, 256:512].rearrange("p (c f) -> p c f", f=16),
                dr["l1_wb"].rearrange("(c p) f -> p c f", p=128)), writes=[Rbob])
            for tg in range(4):
                for which in range(2):
                    bap = PBp[0:16, which * 512:which * 512 + 512]
                    bank = RPB[which]
                    for kc in range(16):
                        b.op("tensor", lambda e, tg=tg, kc=kc, which=which, bap=bap: e.matmul(
                            bap,
                            C1[:, 22528:24576][:, which * 256 + kc * 16:which * 256 + (kc + 1) * 16],
                            A[:, kc * 2048 + tg * 512:kc * 2048 + (tg + 1) * 512],
                            start=(kc == 0), stop=(kc == 15)),
                            reads=[RA[kc], Rbob], writes=[bank])
                b.op("scalar", lambda e: e.activation(
                    statx[0:16, 704:1216], PBp[0:16, 0:512], AF.Sigmoid),
                    reads=[RPB[0]], writes=[Rstx])
                b.op("scalar", lambda e: e.activation(
                    D2[0:16, 0:512], PBp[0:16, 512:1024], AF.Sigmoid),
                    reads=[RPB[1]], writes=[RD2])
                b.op("scalar", lambda e: e.activation(
                    statx[0:16, 704:1216], statx[0:16, 704:1216], AF.Ln),
                    reads=[Rstx], writes=[Rstx])
                for blk in range(4):
                    cidx = tg * 4 + blk
                    b.op("tensor", lambda e, blk=blk: e.transpose(
                        PA[:, 0:16], statx[0:16, 704 + blk * 128:704 + (blk + 1) * 128],
                        idf[0:16, 0:16]),
                        reads=[Rstx, Rconst], writes=[RPA[0]])
                    b.op("vector", lambda e, cidx=cidx: e.tensor_copy(
                        statx[:, 320 + cidx * 16:320 + (cidx + 1) * 16], PA[:, 0:16]),
                        reads=[RPA[0]], writes=[Rstx])
                    b.op("tensor", lambda e, blk=blk: e.transpose(
                        PT[:, 0:16], D2[0:16, blk * 128:(blk + 1) * 128], idb[0:16, 0:16]),
                        reads=[RD2, Rconst], writes=[RPT])
                    b.op("vector", lambda e, cidx=cidx: e.tensor_copy(
                        bT[:, cidx * 16:(cidx + 1) * 16], PT[:, 0:16]),
                        reads=[RPT], writes=[RbT])

        def phase_l1_scan():
            RD2o = Res("d2odd")
            RD1h = [Res(f"st{h}") for h in range(8)]
            Rstx0, Rstx1 = Res("sx0"), Res("sx1")

            def SL(par, off, w):
                if par == 0:
                    return scanb[:, off:off + w], Rscb
                return D2[:, 2048 + off:2048 + off + w], RD2o
            FS = {0: dict(eL=592, eLn=593, negL=594, eLCb=595,
                          expD=736, w1f=864),
                  1: dict(eL=216, eLn=217, negL=218, eLCb=219,
                          expD=88, w1f=992)}
            PBK = {0: dict(b1=(PA, 0, RPA[0]), b2=(PA, 512, RPA[1]),
                           b3=(PA, 1024, RPA[2]), pt=(PT, RPT)),
                   1: dict(b1=(PA, 1536, RPA[3]), b2=(PBp, 0, RPB[0]),
                           b3=(PBp, 512, RPB[1]), pt=(PT2, RPT2))}

            for hf in range(2):
                b.op("vector", lambda e: e.memset(D1[:, :], 0.0), writes=RD1h)
                for c in range(16):
                    d = c % 2
                    base = d * 4096
                    qr = RQ[d]
                    for nm, lo, hi, rows in (("q", 0, 1024, 1024),
                                             ("k", 1024, 2048, 1024),
                                             ("v", 2048, 4096, 2048)):
                        scr = {"q": q_s, "k": k_s, "v": v_s}[nm]
                        b.dma("sync", lambda e, hf=hf, c=c, base=base, lo=lo, scr=scr, rows=rows: e.dma_start(
                            Q(0, base + lo, base + lo + rows)
                            .rearrange("p (h f) -> p h f", f=128),
                            scr[hf * rows:(hf + 1) * rows, c * 128:(c + 1) * 128]
                            .rearrange("(h p) f -> p h f", p=128)),
                            writes=[qr])

                    def qf(h, base=base):
                        return Q(0, base + h * 128, base + (h + 1) * 128)

                    def kf(h, base=base):
                        return Q(0, base + 1024 + h * 128, base + 1024 + (h + 1) * 128)

                    def vf(h, half, base=base):
                        return Q(0, base + 2048 + (2 * h + half) * 128,
                                 base + 2048 + (2 * h + half + 1) * 128)

                    b.op("tensor", lambda e, c=c: e.matmul(
                        PA[:, 1024:1040], muift[:],
                        statx[:, 320 + c * 16:320 + (c + 1) * 16],
                        start=True, stop=True),
                        reads=[Rstx, Rconst], writes=[RPA[2]])
                    b.op("vector", lambda e: e.tensor_copy(
                        statx[:, 576:592], PA[:, 1024:1040]),
                        reads=[RPA[2]], writes=[Rstx])

                    def head_steps(h, kf=kf, qf=qf, vf=vf, qr=qr):
                        par = h % 2
                        hg = hf * 8 + h
                        F = FS[par]
                        BK = PBK[par]
                        pb1, o1_, rb1 = BK["b1"]
                        pb2, o2_, rb2 = BK["b2"]
                        pb3, o3_, rb3 = BK["b3"]
                        ptp, rpt = BK["pt"]
                        Lcol = statx[:, 576 + hg:577 + hg]
                        bcol = bT[:, c * 16 + hg:c * 16 + hg + 1]
                        sst = D1[:, h * 256:(h + 1) * 256]
                        rs = RD1h[h]
                        rsx = Rstx0 if par == 0 else Rstx1
                        lrow = statx[0:1, 608:736] if par == 0 else rw[0:1, 0:128]
                        lrr = rsx if par == 0 else Rrw
                        s0b, s0r = SL(par, 1152, 256)
                        vtm, vtr = SL(par, 1408, 256)
                        ntc, ntr = SL(par, 0, 128)
                        y0, _ = SL(par, 640, 256)
                        y1s, _ = SL(par, 896, 256)
                        ktm, _ = SL(par, 1664, 128)
                        kp, _ = SL(par, 1792, 128)
                        pta, _ = SL(par, 128, 128)
                        ptb, _ = SL(par, 256, 128)
                        pna, _ = SL(par, 384, 128)
                        pnb, _ = SL(par, 512, 128)
                        sres = ntr
                        S = []
                        A_ = S.append
                        A_(lambda: b.op("vector", lambda e: e.tensor_copy(s0b, sst),
                                        reads=[rs], writes=[sres]))
                        for half in range(2):
                            A_(lambda half=half: b.op("tensor", lambda e: e.transpose(
                                ptp[:, 0:128], vf(h, half), idb[:]),
                                reads=[qr, Rconst], writes=[rpt]))
                            A_(lambda half=half: b.op("scalar", lambda e: e.copy(
                                vtm[:, half * 128:(half + 1) * 128], ptp[:, 0:128]),
                                reads=[rpt], writes=[sres]))
                        A_(lambda: b.op("scalar", lambda e: e.activation(
                            statx[:, F["eL"]:F["eL"] + 1], Lcol, AF.Exp),
                            reads=[Rstx], writes=[rsx]))
                        A_(lambda: b.op("vector", lambda e: e.tensor_scalar(
                            statx[:, F["eLn"]:F["eLn"] + 1],
                            statx[:, F["eL"]:F["eL"] + 1], -1.0, None, ALU.mult),
                            reads=[rsx], writes=[rsx]))
                        A_(lambda: b.op("vector", lambda e: e.tensor_scalar(
                            statx[:, F["negL"]:F["negL"] + 1], Lcol, -1.0,
                            None, ALU.mult),
                            reads=[Rstx], writes=[rsx]))
                        A_(lambda: b.op("tensor", lambda e: e.transpose(
                            pb3[0:1, o3_:o3_ + 128], Lcol, idf[:]),
                            reads=[Rstx, Rconst], writes=[rb3]))
                        A_(lambda: b.op("vector", lambda e: e.tensor_copy(
                            lrow, pb3[0:1, o3_:o3_ + 128]),
                            reads=[rb3], writes=[lrr]))
                        A_(lambda: b.op("tensor", lambda e: e.matmul(
                            pb3[:, o3_:o3_ + 128], onesr[:], lrow,
                            start=True, stop=True),
                            reads=[lrr, Rconst], writes=[rb3]))
                        A_(lambda: b.op("vector", lambda e: e.tensor_scalar(
                            pb3[:, o3_ + 128:o3_ + 256],
                            pb3[:, o3_:o3_ + 128],
                            statx[:, F["negL"]:F["negL"] + 1], 0.0,
                            ALU.add, ALU.min),
                            reads=[rb3, rsx], writes=[rb3]))
                        A_(lambda: b.op("scalar", lambda e: e.activation(
                            statx[:, F["expD"]:F["expD"] + 128],
                            pb3[:, o3_ + 128:o3_ + 256], AF.Exp),
                            reads=[rb3], writes=[rsx]))
                        A_(lambda: b.op("tensor", lambda e: e.matmul(
                            pb1[:, o1_:o1_ + 256], kf(h), s0b,
                            start=True, stop=True),
                            reads=[qr, sres], writes=[rb1]))
                        A_(lambda: b.op("vector", lambda e: e.scalar_tensor_tensor(
                            y0, pb1[:, o1_:o1_ + 256],
                            statx[:, F["eLn"]:F["eLn"] + 1], vtm,
                            ALU.mult, ALU.add),
                            reads=[rb1, rsx, sres], writes=[sres]))
                        A_(lambda: b.op("tensor", lambda e: e.matmul(
                            pb2[:, o2_:o2_ + 128], kf(h), kf(h),
                            start=True, stop=True),
                            reads=[qr], writes=[rb2]))
                        A_(lambda: b.op("vector", lambda e: e.tensor_tensor(
                            statx[:, F["w1f"]:F["w1f"] + 128],
                            statx[:, F["expD"]:F["expD"] + 128],
                            pb2[:, o2_:o2_ + 128], ALU.mult),
                            reads=[rsx, rb2], writes=[rsx]))
                        A_(lambda: b.op("vector", lambda e: e.scalar_tensor_tensor(
                            ntc, statx[:, F["w1f"]:F["w1f"] + 128], bcol,
                            msut[:], ALU.mult, ALU.mult),
                            reads=[rsx, RbT, Rconst], writes=[sres]))
                        A_(lambda: b.op("tensor", lambda e: e.transpose(
                            ptp[:, 0:128], ntc, idb[:]),
                            reads=[sres, Rconst], writes=[rpt]))
                        A_(lambda: b.op("scalar", lambda e: e.copy(
                            pna, ptp[:, 0:128]),
                            reads=[rpt], writes=[sres]))
                        A_(lambda: b.op("vector", lambda e: e.tensor_copy(
                            pta, ntc),
                            reads=[sres], writes=[sres]))
                        ycur, ynext = y0, y1s
                        pt_cur, pt_nxt = pta, ptb
                        pn_cur, pn_nxt = pna, pnb
                        for j in range(7):
                            A_(lambda pt_cur=pt_cur, ycur=ycur: b.op("tensor", lambda e: e.matmul(
                                pb1[:, o1_:o1_ + 256], pt_cur, ycur,
                                start=True, stop=True),
                                reads=[sres], writes=[rb1]))
                            A_(lambda ycur=ycur, ynext=ynext: b.op("vector", lambda e: e.tensor_tensor(
                                ynext, pb1[:, o1_:o1_ + 256], ycur, ALU.add),
                                reads=[rb1, sres], writes=[sres]))
                            ycur, ynext = ynext, ycur
                            if j < 6:
                                A_(lambda pn_cur=pn_cur, pt_cur=pt_cur: b.op("tensor", lambda e: e.matmul(
                                    pb2[:, o2_:o2_ + 128], pn_cur, pt_cur,
                                    start=True, stop=True),
                                    reads=[sres], writes=[rb2]))
                                A_(lambda pt_nxt=pt_nxt: b.op("scalar", lambda e: e.copy(
                                    pt_nxt, pb2[:, o2_:o2_ + 128]),
                                    reads=[rb2], writes=[sres]))
                                if j < 5:
                                    A_(lambda pt_cur=pt_cur, pn_cur=pn_cur: b.op("tensor", lambda e: e.matmul(
                                        pb2[:, o2_:o2_ + 128], pt_cur, pn_cur,
                                        start=True, stop=True),
                                        reads=[sres], writes=[rb2]))
                                    A_(lambda pn_nxt=pn_nxt: b.op("scalar", lambda e: e.copy(
                                        pn_nxt, pb2[:, o2_:o2_ + 128]),
                                        reads=[rb2], writes=[sres]))
                                pt_cur, pt_nxt = pt_nxt, pt_cur
                                pn_cur, pn_nxt = pn_nxt, pn_cur
                        R_ = ycur
                        Rtmp = y1s if R_ is y0 else y0
                        A_(lambda: b.op("tensor", lambda e: e.matmul(
                            pb2[:, o2_:o2_ + 128], kf(h), qf(h),
                            start=True, stop=True),
                            reads=[qr], writes=[rb2]))
                        A_(lambda: b.op("vector", lambda e: e.tensor_tensor(
                            statx[:, F["w1f"]:F["w1f"] + 128],
                            statx[:, F["expD"]:F["expD"] + 128],
                            pb2[:, o2_:o2_ + 128], ALU.mult),
                            reads=[rsx, rb2], writes=[rsx]))
                        A_(lambda: b.op("vector", lambda e: e.scalar_tensor_tensor(
                            ntc, statx[:, F["w1f"]:F["w1f"] + 128], bcol,
                            muibt[:], ALU.mult, ALU.mult),
                            reads=[rsx, RbT, Rconst], writes=[sres]))
                        A_(lambda R_=R_: b.op("tensor", lambda e: e.matmul(
                            pb1[:, o1_ + 256:o1_ + 512], ntc, R_,
                            start=True, stop=True, skip_group_check=True),
                            reads=[sres], writes=[rb1]))
                        A_(lambda: b.op("tensor", lambda e: e.matmul(
                            pb1[:, o1_:o1_ + 256], qf(h), s0b,
                            start=True, stop=True, skip_group_check=True),
                            reads=[qr, sres], writes=[rb1]))
                        A_(lambda Rtmp=Rtmp: b.op("vector", lambda e: e.tensor_scalar(
                            Rtmp, pb1[:, o1_:o1_ + 256],
                            statx[:, F["eL"]:F["eL"] + 1], None, ALU.mult),
                            reads=[rb1, rsx], writes=[sres]))
                        A_(lambda Rtmp=Rtmp: b.op("vector", lambda e: e.tensor_tensor(
                            Rtmp, Rtmp, pb1[:, o1_ + 256:o1_ + 512], ALU.add),
                            reads=[sres, rb1], writes=[sres]))
                        for half in range(2):
                            A_(lambda half=half, Rtmp=Rtmp: b.op("tensor", lambda e: e.transpose(
                                ptp[:, 0:128], Rtmp[:, half * 128:(half + 1) * 128],
                                idb[:]),
                                reads=[sres, Rconst], writes=[rpt]))
                            A_(lambda half=half: b.op("scalar", lambda e: e.copy(
                                D2[:, h * 256 + half * 128:h * 256 + (half + 1) * 128],
                                ptp[:, 0:128]),
                                reads=[rpt], writes=[RD2]))
                        A_(lambda: b.op("tensor", lambda e: e.transpose(
                            ptp[:, 0:128], kf(h), idb[:]),
                            reads=[qr, Rconst], writes=[rpt]))
                        A_(lambda: b.op("scalar", lambda e: e.copy(
                            ktm, ptp[:, 0:128]),
                            reads=[rpt], writes=[sres]))
                        A_(lambda: b.op("vector", lambda e: e.tensor_tensor(
                            statx[:, F["negL"]:F["negL"] + 1],
                            statx[:, F["expD"] + 127:F["expD"] + 128], bcol,
                            ALU.mult),
                            reads=[rsx, RbT], writes=[rsx]))
                        A_(lambda: b.op("vector", lambda e: e.tensor_scalar(
                            kp, ktm, statx[:, F["negL"]:F["negL"] + 1],
                            None, ALU.mult),
                            reads=[sres, rsx], writes=[sres]))
                        A_(lambda R_=R_: b.op("tensor", lambda e: e.matmul(
                            pb3[:, o3_ + 256:o3_ + 512], kp, R_,
                            start=True, stop=True, skip_group_check=True),
                            reads=[sres], writes=[rb3]))
                        A_(lambda: b.op("vector", lambda e: e.tensor_tensor(
                            statx[:, F["eLCb"]:F["eLCb"] + 1],
                            statx[:, F["expD"] + 127:F["expD"] + 128],
                            statx[:, F["eL"]:F["eL"] + 1], ALU.mult),
                            reads=[rsx], writes=[rsx]))
                        A_(lambda: b.op("vector", lambda e: e.scalar_tensor_tensor(
                            sst, sst, statx[:, F["eLCb"]:F["eLCb"] + 1],
                            pb3[:, o3_ + 256:o3_ + 512], ALU.mult, ALU.add),
                            reads=[rs, rsx, rb3], writes=[rs]))
                        return S

                    for hp in range(4):
                        s_even = head_steps(2 * hp)
                        s_odd = head_steps(2 * hp + 1)
                        n = max(len(s_even), len(s_odd))
                        for idx in range(n):
                            if idx < len(s_even):
                                s_even[idx]()
                            if idx < len(s_odd):
                                s_odd[idx]()
                    b.dma("sync", lambda e, hf=hf, c=c: e.dma_start(
                        outs_s[hf * 2048:(hf + 1) * 2048, c * 128:(c + 1) * 128]
                        .rearrange("(h p) f -> p h f", p=128),
                        D2[:, 0:2048].rearrange("p (h f) -> p h f", f=128)),
                        reads=[RD2], writes=[Rscr["outs_s"]])

        def phase_l1_wo():
            for kq in range(4):
                wr = [RQ[0], RQ[1]]
                wload(C1[:, 0:16384].rearrange("p (c f) -> p c f", f=2048),
                      dr["l1_wo"][kq * 1024:(kq + 1) * 1024, :]
                      .rearrange("(c p) f -> p c f", p=128), wr)
                for i in range(NT):
                    ob = RO[i % 2]
                    ocol = 16384 + (i % 2) * 1024
                    b.dma("sync", lambda e, i=i, kq=kq, ocol=ocol: e.dma_start(
                        C1[:, ocol:ocol + 1024].rearrange("p (c f) -> p c f", f=128),
                        outs_s[kq * 1024:(kq + 1) * 1024, i * 128:(i + 1) * 128]
                        .rearrange("(c p) f -> p c f", p=128)),
                        reads=[Rscr["outs_s"]], writes=[ob])
                    if kq == 3:
                        b.dma("gpsimd", lambda e, i=i: e.dma_start(
                            D1[:, :], x1_s[i * 128:(i + 1) * 128, :]),
                            reads=[Rscr["x1_s"], RD1], writes=[RD1])
                    for fb in range(4):
                        bank = RPA[fb]
                        bap = PAb(fb)
                        for cc in range(8):
                            b.op("tensor", lambda e, cc=cc, fb=fb, bap=bap, ocol=ocol: e.matmul(
                                bap, C1[:, ocol + cc * 128:ocol + (cc + 1) * 128],
                                C1[:, cc * 2048 + fb * 512:cc * 2048 + (fb + 1) * 512],
                                start=(cc == 0), stop=(cc == 7)),
                                reads=[ob] + wr, writes=[bank])
                        if kq == 0:
                            b.op("vector", lambda e, i=i, fb=fb, bap=bap: e.tensor_copy(
                                Bacc[:, i * 2048 + fb * 512:i * 2048 + (fb + 1) * 512],
                                bap),
                                reads=[bank, RB[i]], writes=[RB[i]])
                        elif kq < 3:
                            b.op("vector", lambda e, i=i, fb=fb, bap=bap: e.tensor_tensor(
                                Bacc[:, i * 2048 + fb * 512:i * 2048 + (fb + 1) * 512],
                                bap,
                                Bacc[:, i * 2048 + fb * 512:i * 2048 + (fb + 1) * 512],
                                ALU.add),
                                reads=[bank, RB[i]], writes=[RB[i]])
                        else:
                            b.op("vector", lambda e, i=i, fb=fb, bap=bap: e.tensor_tensor(
                                scanb[:, 0:512], bap,
                                Bacc[:, i * 2048 + fb * 512:i * 2048 + (fb + 1) * 512],
                                ALU.add),
                                reads=[bank, RB[i]], writes=[Rscb])
                            st_ = 704 if fb % 2 == 0 else 192
                            b.op("vector", lambda e, fb=fb, st_=st_: e.tensor_tensor(
                                statx[:, st_:st_ + 512], scanb[:, 0:512],
                                D1[:, fb * 512:(fb + 1) * 512], ALU.add),
                                reads=[Rscb, RD1], writes=[Rstx])
                            b.dma("sync", lambda e, i=i, fb=fb, st_=st_: e.dma_start(
                                xa_s[i * 128:(i + 1) * 128, fb * 512:(fb + 1) * 512],
                                statx[:, st_:st_ + 512]),
                                reads=[Rstx], writes=[Rscr["xa_s"]])

        def round_min_sign(dst_bf_ap, dst_res, clipv, sg_ap, sg_res, n, par=0):
            ao = 704 if par == 0 else 88
            io = par * 512
            av = statx[:, ao:ao + n]
            it = ITL[:, io:io + n]
            rit = RIT if par == 0 else RIT1
            b.op("scalar", lambda e: e.activation(it, av, AF.Identity),
                 reads=[Rstx], writes=[rit])
            b.op("vector", lambda e: e.scalar_tensor_tensor(
                av, it, -1.0, av, ALU.mult, ALU.add),
                reads=[rit, Rstx], writes=[Rstx])
            b.op("vector", lambda e: e.scalar_tensor_tensor(
                av, av, 0.5, it, ALU.is_ge, ALU.add),
                reads=[Rstx, rit], writes=[Rstx])
            b.op("vector", lambda e: e.scalar_tensor_tensor(
                dst_bf_ap, av, float(clipv), sg_ap,
                ALU.min, ALU.mult),
                reads=[Rstx, sg_res], writes=[dst_res])

        def quant_chain(dst_bf_ap, src_f32_ap, inv_scale_ap, clipv, sg_ap,
                        n, par=0, rsrc_extra=()):
            ao = 704 if par == 0 else 88
            b.op("scalar", lambda e: e.activation(
                statx[:, ao:ao + n], src_f32_ap, AF.Abs, scale=inv_scale_ap),
                reads=[RD1, Rstx] + list(rsrc_extra), writes=[Rstx])
            b.op("scalar", lambda e: e.activation(
                sg_ap, src_f32_ap, AF.Sign),
                reads=[RD1] + list(rsrc_extra), writes=[RD2])
            round_min_sign(dst_bf_ap, RD2, clipv, sg_ap, RD2, n, par=par)

        def phase_prep_wq():
            RD1h2 = [Res("pd0"), Res("pd1")]
            for hc in range(64):
                for hf2 in range(2):
                    dst = D1[:, hf2 * 1024:(hf2 + 1) * 1024]
                    b.dma("sync", lambda e, hc=hc, hf2=hf2, dst=dst: e.dma_start(
                        dst, dr["l1_w1"][hc * 128:(hc + 1) * 128,
                                         hf2 * 1024:(hf2 + 1) * 1024]),
                        writes=[RD1h2[hf2]])
                    b.op("vector", lambda e, hc=hc, hf2=hf2, dst=dst: e.reduce_sum(
                        statx[:, 680 + hf2:681 + hf2], dst, AXX,
                        apply_absolute_value=True),
                        reads=[RD1h2[hf2]], writes=[Rstx])
                b.op("vector", lambda e, hc=hc: e.tensor_tensor(
                    statx[:, 682:683], statx[:, 680:681], statx[:, 681:682],
                    ALU.add),
                    reads=[Rstx], writes=[Rstx])
                b.op("vector", lambda e, hc=hc: e.tensor_scalar(
                    statx[:, hc:hc + 1], statx[:, 682:683], 1.0 / DM, None,
                    ALU.mult),
                    reads=[Rstx], writes=[Rstx])
                b.op("vector", lambda e, hc=hc: e.reciprocal(
                    statx[:, 600 + hc:601 + hc], statx[:, 682:683]),
                    reads=[Rstx], writes=[Rstx])
                b.op("vector", lambda e, hc=hc: e.tensor_scalar(
                    statx[:, 600 + hc:601 + hc], statx[:, 600 + hc:601 + hc],
                    float(DM), None, ALU.mult),
                    reads=[Rstx], writes=[Rstx])
            for dc in range(16):
                for pc in range(8):
                    dst = D1[:, (pc % 2) * 1024:(pc % 2) * 1024 + 1024]
                    b.dma("sync", lambda e, dc=dc, pc=pc, dst=dst: e.dma_start(
                        dst, dr["l1_w2"][dc * 128:(dc + 1) * 128,
                                         pc * 1024:(pc + 1) * 1024]),
                        writes=[RD1h2[pc % 2]])
                    b.op("vector", lambda e, pc=pc, dst=dst: e.reduce_sum(
                        statx[:, 684 + pc:685 + pc], dst, AXX,
                        apply_absolute_value=True),
                        reads=[RD1h2[pc % 2]], writes=[Rstx])
                for st in (1, 2, 4):
                    for j in range(0, 8, 2 * st):
                        b.op("vector", lambda e, j=j, st=st: e.tensor_tensor(
                            statx[:, 684 + j:685 + j], statx[:, 684 + j:685 + j],
                            statx[:, 684 + j + st:685 + j + st], ALU.add),
                            reads=[Rstx], writes=[Rstx])
                b.op("vector", lambda e, dc=dc: e.tensor_scalar(
                    statx[:, 64 + dc:65 + dc], statx[:, 684:685], 1.0 / 8192,
                    None, ALU.mult),
                    reads=[Rstx], writes=[Rstx])
                b.op("vector", lambda e, dc=dc: e.reciprocal(
                    statx[:, 664 + dc:665 + dc], statx[:, 684:685]),
                    reads=[Rstx], writes=[Rstx])
                b.op("vector", lambda e, dc=dc: e.tensor_scalar(
                    statx[:, 664 + dc:665 + dc], statx[:, 664 + dc:665 + dc],
                    8192.0, None, ALU.mult),
                    reads=[Rstx], writes=[Rstx])
            for hc in range(64):
                for hf2 in range(2):
                    dst = D1[:, hf2 * 1024:(hf2 + 1) * 1024]
                    b.dma("sync", lambda e, hc=hc, hf2=hf2, dst=dst: e.dma_start(
                        dst, dr["l1_w1"][hc * 128:(hc + 1) * 128,
                                         hf2 * 1024:(hf2 + 1) * 1024]),
                        writes=[RD1h2[hf2]])
                    for sp in range(2):
                        pc = hf2 * 2 + sp
                        quant_chain(D2[:, pc * 512:(pc + 1) * 512],
                                    D1[:, pc * 512:(pc + 1) * 512],
                                    statx[:, 600 + hc:601 + hc], 1.0,
                                    D2[:, 2048 + (pc % 2) * 512:
                                          2048 + (pc % 2) * 512 + 512], 512,
                                    par=pc % 2,
                                    rsrc_extra=[RD1h2[hf2]])
                for t in range(16):
                    transpose(D2[:, 2048 + t * 128:2048 + (t + 1) * 128],
                              D2[:, t * 128:(t + 1) * 128],
                              RD2, RD2, use2=(t % 2 == 1))
                b.dma("sync", lambda e, hc=hc: e.dma_start(
                    wq1t_s[:, hc * 128:(hc + 1) * 128]
                    .rearrange("(c p) f -> p c f", p=128),
                    D2[:, 2048:4096].rearrange("p (c f) -> p c f", f=128)),
                    reads=[RD2], writes=[Rscr["wq1t_s"]])
            for dc in range(16):
                for pc4 in range(4):
                    for hf2 in range(2):
                        pp = pc4 * 2 + hf2
                        dst = D1[:, hf2 * 1024:(hf2 + 1) * 1024]
                        b.dma("sync", lambda e, dc=dc, pp=pp, dst=dst: e.dma_start(
                            dst, dr["l1_w2"][dc * 128:(dc + 1) * 128,
                                             pp * 1024:(pp + 1) * 1024]),
                            writes=[RD1h2[hf2]])
                        for sp in range(2):
                            sx = hf2 * 2 + sp
                            quant_chain(D2[:, sx * 512:(sx + 1) * 512],
                                        D1[:, sx * 512:(sx + 1) * 512],
                                        statx[:, 664 + dc:665 + dc], 1.0,
                                        D2[:, 2048 + (sx % 2) * 512:
                                              2048 + (sx % 2) * 512 + 512], 512,
                                        par=sx % 2,
                                        rsrc_extra=[RD1h2[hf2]])
                    for t in range(16):
                        transpose(D2[:, 2048 + t * 128:2048 + (t + 1) * 128],
                                  D2[:, t * 128:(t + 1) * 128],
                                  RD2, RD2, use2=(t % 2 == 1))
                    b.dma("sync", lambda e, dc=dc, pc4=pc4: e.dma_start(
                        wq2t_s[pc4 * 2048:(pc4 + 1) * 2048,
                               dc * 128:(dc + 1) * 128]
                        .rearrange("(c p) f -> p c f", p=128),
                        D2[:, 2048:4096].rearrange("p (c f) -> p c f", f=128)),
                        reads=[RD2], writes=[Rscr["wq2t_s"]])

        def phase_l1_ffn():
            for i in range(NT):
                b.op("vector", lambda e, i=i: e.tensor_scalar_max(
                    stat[:, 48 + i:49 + i], stat[:, 32 + i:33 + i], 1e-5),
                    reads=[Rstat], writes=[Rstat])
                b.op("vector", lambda e, i=i: e.tensor_scalar(
                    stat[:, 64 + i:65 + i], stat[:, 48 + i:49 + i], 1.0 / 127.0,
                    None, ALU.mult),
                    reads=[Rstat], writes=[Rstat])
                b.op("vector", lambda e, i=i: e.reciprocal(
                    stat[:, 48 + i:49 + i], stat[:, 48 + i:49 + i]),
                    reads=[Rstat], writes=[Rstat])
                b.op("vector", lambda e, i=i: e.tensor_scalar(
                    stat[:, 48 + i:49 + i], stat[:, 48 + i:49 + i], 127.0,
                    None, ALU.mult),
                    reads=[Rstat], writes=[Rstat])

            def build_bcast(colbase, dst_ap, dst_res, nda=True):
                for i in range(NT):
                    b.op("tensor", lambda e, i=i, colbase=colbase: e.transpose(
                        PA[0:1, 1024:1152], stat[:, colbase + i:colbase + i + 1],
                        idf[:]),
                        reads=[Rstat, Rconst], writes=[RPA[2]])
                    b.op("vector", lambda e, i=i: e.tensor_copy(
                        D1[0:1, i * 128:(i + 1) * 128], PA[0:1, 1024:1152]),
                        reads=[RPA[2]], writes=[RD1])
                for fb in range(4):
                    b.op("tensor", lambda e, fb=fb: e.matmul(
                        PA[:, 1024:1536], onesr[:], D1[0:1, fb * 512:(fb + 1) * 512],
                        start=True, stop=True),
                        reads=[RD1, Rconst], writes=[RPA[2]])
                    b.op("vector", lambda e, fb=fb, dst_ap=dst_ap: e.tensor_copy(
                        dst_ap[:, fb * 512:(fb + 1) * 512], PA[:, 1024:1536]),
                        reads=[RPA[2]], writes=[dst_res])

            build_bcast(48, D2[:, 0:2048], RD2)
            build_bcast(64, D2[:, 2048:4096], RD2)
            for i in range(NT):
                b.op("tensor", lambda e, i=i: e.transpose(
                    PA[0:1, 1024:1152], statx[:, 64 + i:65 + i], idf[:]),
                    reads=[Rstx, Rconst], writes=[RPA[2]])
                b.op("vector", lambda e, i=i: e.tensor_copy(
                    D1[0:1, i * 128:(i + 1) * 128], PA[0:1, 1024:1152]),
                    reads=[RPA[2]], writes=[RD1])
            for fb in range(4):
                b.op("tensor", lambda e, fb=fb: e.matmul(
                    PA[:, 1024:1536], onesr[:], D1[0:1, fb * 512:(fb + 1) * 512],
                    start=True, stop=True),
                    reads=[RD1, Rconst], writes=[RPA[2]])
                b.op("vector", lambda e, fb=fb: e.tensor_copy(
                    C1[:, 8192 + fb * 512:8192 + (fb + 1) * 512], PA[:, 1024:1536]),
                    reads=[RPA[2]], writes=[RQ[1]])

            for cc in range(16):
                for pc in range(4):
                    src = A[:, cc * 2048 + pc * 512: cc * 2048 + (pc + 1) * 512]
                    par = pc % 2
                    ao = 704 if par == 0 else 88
                    sgs = scanb[:, par * 512:par * 512 + 512]
                    b.op("vector", lambda e, src=src, pc=pc, ao=ao: e.tensor_tensor(
                        statx[:, ao:ao + 512], src, D2[:, pc * 512:(pc + 1) * 512],
                        ALU.mult),
                        reads=[RA[cc], RD2], writes=[Rstx])
                    b.op("scalar", lambda e, src=src, sgs=sgs: e.activation(
                        sgs, src, AF.Sign),
                        reads=[RA[cc]], writes=[Rscb])
                    b.op("scalar", lambda e, ao=ao: e.activation(
                        statx[:, ao:ao + 512], statx[:, ao:ao + 512], AF.Abs),
                        reads=[Rstx], writes=[Rstx])
                    round_min_sign(src, RA[cc], 127.0, sgs, Rscb, 512, par=par)

            RMX, RMN, S2B, SG2 = 512, 1024, 0, 512
            RBH = [Res(f"h1c{i}") for i in range(64)]
            for g in range(4):
                gcol = g * 512
                for hc in range(64):
                    wk = hc % 2
                    wcol = 16384 + wk * 2048
                    wload(C1[:, wcol:wcol + 2048].rearrange("p (c f) -> p c f", f=128),
                          wq1t_s[:, hc * 128:(hc + 1) * 128]
                          .rearrange("(c p) f -> p c f", p=128), RO[wk])
                    bank = RPA[hc % 2]
                    bap = PAb(hc % 2)
                    for kc in range(16):
                        b.op("tensor", lambda e, hc=hc, kc=kc, gcol=gcol, wcol=wcol, bap=bap: e.matmul(
                            bap,
                            C1[:, wcol + kc * 128:wcol + (kc + 1) * 128],
                            A[:, kc * 2048 + gcol: kc * 2048 + gcol + 512],
                            start=(kc == 0), stop=(kc == 15)),
                            reads=[RA[kc], RO[hc % 2]], writes=[bank])
                    b.op("vector", lambda e, gcol=gcol, bap=bap: e.tensor_tensor(
                        statx[:, 704:1216], bap, D2[:, 2048 + gcol:2048 + gcol + 512],
                        ALU.mult),
                        reads=[bank, RD2], writes=[Rstx])
                    b.op("scalar", lambda e, hc=hc: e.activation(
                        Bacc[:, hc * 512:(hc + 1) * 512], statx[:, 704:1216],
                        AF.Gelu_apprx_tanh, scale=statx[:, hc:hc + 1]),
                        reads=[Rstx], writes=[RBH[hc]])
                    if hc == 0:
                        b.op("vector", lambda e: e.tensor_copy(
                            scanb[:, RMX:RMX + 512], Bacc[:, 0:512]),
                            reads=[RBH[0]], writes=[Rscb])
                        b.op("vector", lambda e: e.tensor_copy(
                            scanb[:, RMN:RMN + 512], Bacc[:, 0:512]),
                            reads=[RBH[0]], writes=[Rscb])
                    else:
                        b.op("vector", lambda e, hc=hc: e.tensor_tensor(
                            scanb[:, RMX:RMX + 512], scanb[:, RMX:RMX + 512],
                            Bacc[:, hc * 512:(hc + 1) * 512], ALU.max),
                            reads=[RBH[hc], Rscb], writes=[Rscb])
                        b.op("vector", lambda e, hc=hc: e.tensor_tensor(
                            scanb[:, RMN:RMN + 512], scanb[:, RMN:RMN + 512],
                            Bacc[:, hc * 512:(hc + 1) * 512], ALU.min),
                            reads=[RBH[hc], Rscb], writes=[Rscb])
                b.op("vector", lambda e: e.tensor_scalar(
                    scanb[:, RMN:RMN + 512], scanb[:, RMN:RMN + 512], -1.0,
                    None, ALU.mult),
                    reads=[Rscb], writes=[Rscb])
                b.op("vector", lambda e: e.tensor_tensor(
                    scanb[:, RMX:RMX + 512], scanb[:, RMX:RMX + 512],
                    scanb[:, RMN:RMN + 512], ALU.max),
                    reads=[Rscb], writes=[Rscb])
                for ts in range(4):
                    b.op("tensor", lambda e, ts=ts: e.transpose(
                        PT[:, 0:128], scanb[:, RMX + ts * 128:RMX + (ts + 1) * 128],
                        idb[:]),
                        reads=[Rscb, Rconst], writes=[RPT])
                    b.op("vector", lambda e, ts=ts: e.reduce_max(
                        stat[:, 80 + ts:81 + ts], PT[:, 0:128], AXX),
                        reads=[RPT], writes=[Rstat])
                    b.op("vector", lambda e, ts=ts: e.tensor_scalar_max(
                        stat[:, 80 + ts:81 + ts], stat[:, 80 + ts:81 + ts], 1e-5),
                        reads=[Rstat], writes=[Rstat])
                    b.op("vector", lambda e, ts=ts: e.tensor_scalar(
                        stat[:, 84 + ts:85 + ts], stat[:, 80 + ts:81 + ts],
                        1.0 / 127.0, None, ALU.mult),
                        reads=[Rstat], writes=[Rstat])
                    b.op("vector", lambda e, ts=ts: e.reciprocal(
                        stat[:, 80 + ts:81 + ts], stat[:, 80 + ts:81 + ts]),
                        reads=[Rstat], writes=[Rstat])
                    b.op("vector", lambda e, ts=ts: e.tensor_scalar(
                        stat[:, 80 + ts:81 + ts], stat[:, 80 + ts:81 + ts], 127.0,
                        None, ALU.mult),
                        reads=[Rstat], writes=[Rstat])
                    b.op("tensor", lambda e, ts=ts: e.transpose(
                        PA[0:1, 1024:1152], stat[:, 80 + ts:81 + ts], idf[:]),
                        reads=[Rstat, Rconst], writes=[RPA[2]])
                    b.op("vector", lambda e, ts=ts: e.tensor_copy(
                        D1[0:1, ts * 128:(ts + 1) * 128], PA[0:1, 1024:1152]),
                        reads=[RPA[2]], writes=[RD1])
                b.op("tensor", lambda e: e.matmul(
                    PA[:, 1024:1536], onesr[:], D1[0:1, 0:512],
                    start=True, stop=True),
                    reads=[RD1, Rconst], writes=[RPA[2]])
                b.op("vector", lambda e: e.tensor_copy(
                    scanb[:, S2B:S2B + 512], PA[:, 1024:1536]),
                    reads=[RPA[2]], writes=[Rscb])
                for hc in range(64):
                    hb = Bacc[:, hc * 512:(hc + 1) * 512]
                    par = hc % 2
                    ao = 704 if par == 0 else 88
                    sgs = scanb[:, SG2 + par * 512:SG2 + par * 512 + 512]
                    b.op("vector", lambda e, hb=hb, ao=ao: e.tensor_tensor(
                        statx[:, ao:ao + 512], hb, scanb[:, S2B:S2B + 512], ALU.mult),
                        reads=[RBH[hc], Rscb], writes=[Rstx])
                    b.op("scalar", lambda e, hb=hb, sgs=sgs: e.activation(
                        sgs, hb, AF.Sign),
                        reads=[RBH[hc]], writes=[Rscb])
                    b.op("scalar", lambda e, ao=ao: e.activation(
                        statx[:, ao:ao + 512], statx[:, ao:ao + 512], AF.Abs),
                        reads=[Rstx], writes=[Rstx])
                    round_min_sign(hb, RBH[hc], 127.0, sgs, Rscb, 512, par=par)
                for fq in range(4):
                    for kc2 in range(64):
                        wk = kc2 % 2
                        wload(Q(0, wk * 512, wk * 512 + 512)
                              .rearrange("p (o f) -> p o f", f=512),
                              wq2t_s[kc2 * 128:(kc2 + 1) * 128,
                                     fq * 512:(fq + 1) * 512]
                              .rearrange("(o p) f -> p o f", p=128), RQ[wk])
                        for ts in range(4):
                            b.op("tensor", lambda e, kc2=kc2, ts=ts, wk=wk: e.matmul(
                                PAb(ts),
                                Bacc[:, kc2 * 512 + ts * 128: kc2 * 512 + (ts + 1) * 128],
                                Q(0, wk * 512, wk * 512 + 512),
                                start=(kc2 == 0), stop=(kc2 == 63)),
                                reads=[RBH[kc2], RQ[wk]], writes=[RPA[ts]])
                    for ts in range(4):
                        i = g * 4 + ts
                        b.dma("gpsimd", lambda e, i=i, fq=fq, ts=ts: e.dma_start(
                            D1[:, ts * 512:(ts + 1) * 512],
                            xa_s[i * 128:(i + 1) * 128, fq * 512:(fq + 1) * 512]),
                            writes=[RD1])
                        st_ = 704 if ts % 2 == 0 else 192
                        b.op("vector", lambda e, ts=ts, fq=fq, st_=st_: e.tensor_tensor(
                            statx[:, st_:st_ + 512], PAb(ts),
                            C1[:, 8192 + fq * 512:8192 + (fq + 1) * 512], ALU.mult),
                            reads=[RPA[ts], RQ[1]], writes=[Rstx])
                        b.op("vector", lambda e, ts=ts, st_=st_: e.scalar_tensor_tensor(
                            statx[:, st_:st_ + 512], statx[:, st_:st_ + 512],
                            stat[:, 84 + ts:85 + ts], D1[:, ts * 512:(ts + 1) * 512],
                            ALU.mult, ALU.add),
                            reads=[Rstx, Rstat, RD1], writes=[Rstx])
                        b.dma("sync", lambda e, i=i, fq=fq, st_=st_: e.dma_start(
                            out[i * 128:(i + 1) * 128, fq * 512:(fq + 1) * 512],
                            statx[:, st_:st_ + 512]),
                            reads=[Rstx], writes=[Rout])

        PH = {"ln": 1, "kv": 2, "qo1": 3, "qo2": 4, "qo": 5, "rt": 6,
              "moe1": 7, "moe": 8, "x1": 9, "l1p": 10, "l1s": 11, "l1a": 12,
              "full": 99}
        lim = PH.get(upto, 99)
        load_consts()
        phase_ln(dr["x0"], s1_track=False)
        if lim >= 2:
            b.phase_barrier()
            phase_l0_kv()
        if lim >= 3:
            b.phase_barrier()
            phase_l0_qo(sub=min(lim - 2, 3))
        if lim >= 6:
            b.phase_barrier()
            phase_router()
        if lim == 7:
            b.phase_barrier()
            phase_l0_moe(nexp=1)
        elif lim >= 8:
            b.phase_barrier()
            phase_l0_moe()
        if lim >= 9:
            b.phase_barrier()
            phase_x1_ln()
        if lim >= 10:
            b.phase_barrier()
            phase_l1_proj()
        if lim >= 11:
            b.phase_barrier()
            phase_l1_scan()
        if lim >= 12:
            b.phase_barrier()
            phase_l1_wo()
        if lim >= 13:
            b.phase_barrier()
            phase_prep_wq()
            b.phase_barrier()
            phase_l1_ffn()

        if dump == "wq1":
            for i in range(NT):
                b.dma("gpsimd", lambda e, i=i: e.dma_start(
                    out[i * 128:(i + 1) * 128, :],
                    wq1t_s[i * 128:(i + 1) * 128, 0:2048]),
                    reads=[Rscr["wq1t_s"]], writes=[Rout])
        elif dump == "wq2":
            for i in range(NT):
                b.dma("gpsimd", lambda e, i=i: e.dma_start(
                    out[i * 128:(i + 1) * 128, :],
                    wq2t_s[i * 128:(i + 1) * 128, 0:2048]),
                    reads=[Rscr["wq2t_s"]], writes=[Rout])
        elif dump == "y1d":
            for c in range(16):
                b.op("vector", lambda e, c=c: e.tensor_copy(
                    D1[:, :], A[:, c * 2048:(c + 1) * 2048]),
                    reads=[RA[c]], writes=[RD1])
                b.dma("gpsimd", lambda e, c=c: e.dma_start(
                    out[c * 128:(c + 1) * 128, :], D1[:, :]),
                    reads=[RD1], writes=[Rout])
        elif dump == "x1":
            for i in range(NT):
                b.dma("gpsimd", lambda e, i=i: e.dma_start(
                    out[i * 128:(i + 1) * 128, :], x1_s[i * 128:(i + 1) * 128, :]),
                    reads=[Rscr["x1_s"]], writes=[Rout])
        elif dump == "l1a":
            for i in range(NT):
                b.dma("gpsimd", lambda e, i=i: e.dma_start(
                    out[i * 128:(i + 1) * 128, :], xa_s[i * 128:(i + 1) * 128, :]),
                    reads=[Rscr["xa_s"]], writes=[Rout])
        elif dump in ("acc", "moe"):
            for i in range(NT):
                b.op("vector", lambda e, i=i: e.tensor_copy(
                    D1[:, :], Bacc[:, i * 2048:(i + 1) * 2048]),
                    reads=[RB[i]], writes=[RD1])
                b.dma("gpsimd", lambda e, i=i: e.dma_start(
                    out[i * 128:(i + 1) * 128, :], D1[:, :]),
                    reads=[RD1], writes=[Rout])
        elif dump == "xn":
            for c in range(16):
                b.op("vector", lambda e, c=c: e.tensor_copy(
                    D1[:, :], A[:, c * 2048:(c + 1) * 2048]),
                    reads=[RA[c]], writes=[RD1])
                b.dma("gpsimd", lambda e, c=c: e.dma_start(
                    out[c * 128:(c + 1) * 128, :], D1[:, :]),
                    reads=[RD1], writes=[Rout])

        tok = b.barrier_token()
        b.emit("sync", lambda e: e.nop(), extra_waits=tok)
        b.run(sems)
    return nc



def make_in_maps(inputs):
    x = np.ascontiguousarray(inputs["L0_attn_hidden_states"])
    maps = []
    common = {
        "l0_wq": np.ascontiguousarray(inputs["L0_attn_Wq"]),
        "l0_wk": np.ascontiguousarray(inputs["L0_attn_Wk"]),
        "l0_wv": np.ascontiguousarray(inputs["L0_attn_Wv"]),
        "l0_wo": np.ascontiguousarray(inputs["L0_attn_Wo"]),
        "l0_bo": np.ascontiguousarray(inputs["L0_attn_bo"]).reshape(1, DM),
        "l0_router": np.ascontiguousarray(inputs["L0_ffn_router_w"]),
        "l0_gate_up": np.ascontiguousarray(inputs["L0_ffn_gate_up_w"]),
        "l0_down": np.ascontiguousarray(inputs["L0_ffn_down_w"]),
        "l0_sh_gate": np.ascontiguousarray(inputs["L0_ffn_shared_gate_w"]),
        "l0_sh_up": np.ascontiguousarray(inputs["L0_ffn_shared_up_w"]),
        "l0_sh_down": np.ascontiguousarray(inputs["L0_ffn_shared_down_w"]),
        "l1_wq": np.ascontiguousarray(inputs["L1_attn_Wq"]),
        "l1_wk": np.ascontiguousarray(inputs["L1_attn_Wk"]),
        "l1_wv": np.ascontiguousarray(inputs["L1_attn_Wv"]),
        "l1_wo": np.ascontiguousarray(inputs["L1_attn_Wo"]),
        "l1_wa": np.ascontiguousarray(inputs["L1_attn_Wa"]),
        "l1_wb": np.ascontiguousarray(inputs["L1_attn_Wb"]),
        "l1_qcw": np.ascontiguousarray(inputs["L1_attn_qcw"]),
        "l1_qcb": np.ascontiguousarray(inputs["L1_attn_qcb"]).reshape(1, DM),
        "l1_kcw": np.ascontiguousarray(inputs["L1_attn_kcw"]),
        "l1_kcb": np.ascontiguousarray(inputs["L1_attn_kcb"]).reshape(1, DM),
        "l1_vcw": np.ascontiguousarray(inputs["L1_attn_vcw"]),
        "l1_vcb": np.ascontiguousarray(inputs["L1_attn_vcb"]).reshape(1, 4096),
        "l1_w1": np.ascontiguousarray(inputs["L1_ffn_w1"]),
        "l1_w2": np.ascontiguousarray(inputs["L1_ffn_w2"]),
    }
    for i in range(8):
        m = dict(common)
        m["x0"] = np.ascontiguousarray(x[i])
        maps.append(m)
    return maps


def kernel(**inputs):
    nc = build_nc()
    maps = make_in_maps(inputs)
    res = run_bass_kernel_spmd(nc, maps, core_ids=list(range(8)))
    return np.stack([np.asarray(res.results[i]["out"], dtype=np.float32)
                     for i in range(8)], axis=0)


# version 06
# speedup vs baseline: 1.2317x; 1.0293x over previous
import contextlib
import numpy as np
import ml_dtypes
import concourse.bass as bass
import concourse.mybir as mybir
from concourse.bass_utils import run_bass_kernel_spmd

F32 = mybir.dt.float32
BF16 = mybir.dt.bfloat16
I32 = mybir.dt.int32
I16 = mybir.dt.int16
AF = mybir.ActivationFunctionType
ALU = mybir.AluOpType
AXX = mybir.AxisListType.X

T_SEQ = 2048
DM = 2048
NT = T_SEQ // 128

ENGINES = ("tensor", "vector", "scalar", "sync", "gpsimd")


class Res:
    __slots__ = ("name", "last_w", "readers")

    def __init__(self, name):
        self.name = name
        self.last_w = None
        self.readers = []


class B:

    def __init__(self, nc):
        self.nc = nc
        self.q = {e: [] for e in ENGINES}
        self.cnt = {}
        self.watermark = {e: {} for e in ENGINES}
        self.dma_idx = {"sync": 0, "gpsimd": 0, "scalar": 0}
        self.pending = {e: None for e in ENGINES}

    def sem(self, name):
        self.cnt[name] = 0

    def _waits(self, eng, toks):
        out = {}
        for t in toks:
            if t is None:
                continue
            s, v = t
            if v <= self.watermark[eng].get(s, 0):
                continue
            out[s] = max(out.get(s, 0), v)
        for s, v in out.items():
            self.watermark[eng][s] = v
        return list(out.items())

    def phase_barrier(self):
        toks = self.barrier_token()
        for e in ENGINES:
            self.pending[e] = toks

    def emit(self, eng, fn, reads=(), writes=(), sem=None, inc=0, extra_waits=()):
        toks = list(extra_waits)
        if self.pending[eng]:
            toks.extend(self.pending[eng])
            self.pending[eng] = None
        for r in reads:
            toks.append(r.last_w)
        for r in writes:
            toks.append(r.last_w)
            toks.extend(r.readers)
        if eng == "tensor":
            toks = [t for t in toks if t is None or t[0] != "t_tensor"]
        waits = self._waits(eng, toks)
        tok = None
        if inc:
            self.cnt[sem] += inc
            tok = (sem, self.cnt[sem])
        self.q[eng].append((fn, waits, sem if inc else None, inc))
        for r in reads:
            if tok is not None:
                r.readers.append(tok)
        for r in writes:
            r.last_w = tok
            r.readers = []
        return tok

    def op(self, eng, fn, reads=(), writes=(), extra_waits=()):
        return self.emit(eng, fn, reads, writes, sem="t_" + eng, inc=1,
                         extra_waits=extra_waits)

    NDMA = 8

    def dma(self, queue, fn, reads=(), writes=(), extra_waits=()):
        idx = self.dma_idx[queue]
        self.dma_idx[queue] = (idx + 1) % self.NDMA
        sem = f"d_{queue}{idx}"
        prev = (sem, self.cnt[sem]) if self.cnt[sem] > 0 else None
        ew = list(extra_waits) + ([prev] if prev else [])
        return self.emit(queue, fn, reads, writes, sem=sem, inc=16,
                         extra_waits=ew)

    def barrier_token(self):
        return [(s, v) for s, v in self.cnt.items() if v > 0]

    def run(self, sems):
        nc = self.nc
        with nc.Block() as blk:
            for eng in ENGINES:
                stream = self.q[eng]
                if not stream:
                    continue

                def make(eng, stream):
                    def body(e):
                        for fn, waits, sname, inc in stream:
                            for ws, wv in waits:
                                e.wait_ge(sems[ws], wv)
                            inst = fn(e)
                            if sname is not None:
                                inst.then_inc(sems[sname], inc)
                    return body

                getattr(blk, eng)(make(eng, stream))


SEM_NAMES = (["t_tensor", "t_vector", "t_scalar", "t_sync", "t_gpsimd"]
             + [f"d_sync{i}" for i in range(B.NDMA)]
             + [f"d_gpsimd{i}" for i in range(B.NDMA)]
             + [f"d_scalar{i}" for i in range(B.NDMA)])


def build_nc(upto="full", dump=None):
    nc = bass.Bass("TRN2", target_bir_lowering=False)

    dr = {}

    def din(name, shape):
        dr[name] = nc.dram_tensor(name, list(shape), F32, kind="ExternalInput").ap()

    din("x0", (T_SEQ, DM))
    din("l0_wq", (DM, DM)); din("l0_wk", (DM, DM)); din("l0_wv", (DM, DM))
    din("l0_wo", (DM, DM)); din("l0_bo", (1, DM))
    din("l0_router", (DM, 8))
    din("l0_gate_up", (8, DM, DM)); din("l0_down", (8, 1024, DM))
    din("l0_sh_gate", (DM, 1024)); din("l0_sh_up", (DM, 1024))
    din("l0_sh_down", (1024, DM))
    din("l1_wq", (DM, DM)); din("l1_wk", (DM, DM)); din("l1_wv", (DM, 4096))
    din("l1_wo", (4096, DM)); din("l1_wa", (DM, 16)); din("l1_wb", (DM, 16))
    din("l1_qcw", (DM, 4)); din("l1_qcb", (1, DM))
    din("l1_kcw", (DM, 4)); din("l1_kcb", (1, DM))
    din("l1_vcw", (4096, 4)); din("l1_vcb", (1, 4096))
    din("l1_w1", (8192, DM)); din("l1_w2", (DM, 8192))

    out = nc.dram_tensor("out", [T_SEQ, DM], F32, kind="ExternalOutput").ap()

    dr["ident"] = nc.inline_tensor(np.eye(128, dtype=ml_dtypes.bfloat16), "identb").ap()
    dr["identf"] = nc.inline_tensor(np.eye(128, dtype=np.float32), "identf").ap()
    msu = -np.triu(np.ones((128, 128), dtype=np.float32), 1)
    mui = np.triu(np.ones((128, 128), dtype=np.float32))
    dr["mask_su"] = nc.inline_tensor(msu.astype(ml_dtypes.bfloat16), "mask_su").ap()
    dr["mask_uif"] = nc.inline_tensor(mui.astype(np.float32), "mask_uif").ap()
    dr["mask_uib"] = nc.inline_tensor(mui.astype(ml_dtypes.bfloat16), "mask_uib").ap()
    dr["ones_row"] = nc.inline_tensor(np.ones((1, 128), dtype=np.float32), "ones_row").ap()
    dr["ones_rowb"] = nc.inline_tensor(np.ones((1, 128), dtype=ml_dtypes.bfloat16), "ones_rowb").ap()
    dr["ones_colb"] = nc.inline_tensor(np.ones((128, 1), dtype=ml_dtypes.bfloat16), "ones_colb").ap()

    x1_s = nc.dram_tensor("x1_s", [T_SEQ, DM], F32, kind="Internal").ap()
    q_s = nc.dram_tensor("q_s", [DM, T_SEQ], BF16, kind="Internal").ap()
    k_s = nc.dram_tensor("k_s", [DM, T_SEQ], BF16, kind="Internal").ap()
    v_s = nc.dram_tensor("v_s", [4096, T_SEQ], BF16, kind="Internal").ap()
    outs_s = nc.dram_tensor("outs_s", [4096, T_SEQ], BF16, kind="Internal").ap()
    o1_s = nc.dram_tensor("o1_s", [DM, T_SEQ], BF16, kind="Internal").ap()
    wq1t_s = nc.dram_tensor("wq1t_s", [DM, 8192], BF16, kind="Internal").ap()
    wq2t_s = nc.dram_tensor("wq2t_s", [8192, DM], BF16, kind="Internal").ap()
    xa_s = nc.dram_tensor("xa_s", [T_SEQ, DM], F32, kind="Internal").ap()

    es = contextlib.ExitStack()
    with es:
        A = es.enter_context(nc.sbuf_tensor("A", [128, NT * DM], BF16))
        Bacc = es.enter_context(nc.sbuf_tensor("Bacc", [128, NT * DM], BF16))
        C1 = es.enter_context(nc.sbuf_tensor("C1", [128, 24576], BF16))
        D1 = es.enter_context(nc.sbuf_tensor("D1", [128, 2048], F32))
        D2 = es.enter_context(nc.sbuf_tensor("D2", [128, 4096], BF16))
        ITL = es.enter_context(nc.sbuf_tensor("ITL", [128, 1024], I16))
        idb = es.enter_context(nc.sbuf_tensor("idb", [128, 128], BF16))
        idf = es.enter_context(nc.sbuf_tensor("idf", [128, 128], F32))
        msut = es.enter_context(nc.sbuf_tensor("msut", [128, 128], BF16))
        muift = es.enter_context(nc.sbuf_tensor("muift", [128, 128], F32))
        onesr = es.enter_context(nc.sbuf_tensor("onesr", [1, 128], F32))
        onesrb = es.enter_context(nc.sbuf_tensor("onesrb", [1, 128], BF16))
        stat = es.enter_context(nc.sbuf_tensor("stat", [128, 96], F32))
        wsm = es.enter_context(nc.sbuf_tensor("wsm", [128, 128], BF16))
        rw = es.enter_context(nc.sbuf_tensor("rw", [128, 128], F32))
        smallf = es.enter_context(nc.sbuf_tensor("smallf", [128, 64], F32))
        cst = es.enter_context(nc.sbuf_tensor("cst", [128, 8], F32))
        hwrk = es.enter_context(nc.sbuf_tensor("hwrk", [128, 256], BF16))
        muibt = es.enter_context(nc.sbuf_tensor("muibt", [128, 128], BF16))
        onescb = es.enter_context(nc.sbuf_tensor("onescb", [128, 1], BF16))
        statx = es.enter_context(nc.sbuf_tensor("statx", [128, 1216], F32))
        scanb = es.enter_context(nc.sbuf_tensor("scanb", [128, 1920], BF16))
        bT = es.enter_context(nc.sbuf_tensor("bT", [128, 256], BF16))
        PA = es.enter_context(nc.psum_tensor("PA", [128, 2048], F32))
        PBp = es.enter_context(nc.psum_tensor("PBp", [128, 1024], F32))
        PT = es.enter_context(nc.psum_tensor("PT", [128, 1024], BF16))
        PT2 = es.enter_context(nc.psum_tensor("PT2", [128, 1024], BF16))
        sems = {n: es.enter_context(nc.semaphore(n)) for n in SEM_NAMES}

        b = B(nc)
        for n in SEM_NAMES:
            b.sem(n)

        RA = [Res(f"A{i}") for i in range(NT)]
        RB = [Res(f"B{i}") for i in range(NT)]
        RQ = [Res(f"Q{i}") for i in range(3)]
        RO = [Res(f"O{i}") for i in range(2)]
        RD1, RD2, RIT = Res("D1"), Res("D2"), Res("ITL")
        RIT1 = Res("ITL1")
        Rstat, Rrw = Res("stat"), Res("rw")
        Rsm, Rbob, Rwsm = Res("smallf"), Res("bob"), Res("wsm")
        Rhw = Res("hwrk")
        Rhw1 = Res("hwrk1")
        RHF = [Res(f"hf{i}") for i in range(8)]
        Rconst = Res("const")
        Rout = Res("out")
        Rscr = {n: Res(n) for n in
                ("x1_s", "q_s", "k_s", "v_s", "outs_s", "wq1t_s", "wq2t_s",
                 "o1_s", "xa_s")}
        RPA = [Res(f"PA{i}") for i in range(4)]
        RPB = [Res(f"PB{i}") for i in range(2)]
        RPT, RPT2 = Res("PT"), Res("PT2")

        def PAb(i):
            return PA[:, i * 512:(i + 1) * 512]

        def Q(k, lo, hi):
            return C1[:, k * 8192 + lo: k * 8192 + hi]

        Rcst = Res("cst")
        Rstx, Rscb, RbT = Res("statx"), Res("scanb"), Res("bT")
        CEPS, CHALF, CNEG1, CONE = (lambda c: None), None, None, None

        def cstc(i):
            return cst[:, i:i + 1]

        def load_consts():
            b.op("vector", lambda e: e.memset(cst[:, 0:1], 1e-5), writes=[Rcst])
            b.op("vector", lambda e: e.memset(cst[:, 1:2], 0.5), writes=[Rcst])
            b.op("vector", lambda e: e.memset(cst[:, 2:3], -1.0), writes=[Rcst])
            b.op("vector", lambda e: e.memset(cst[:, 3:4], 1.0), writes=[Rcst])
            b.dma("sync", lambda e: e.dma_start(muibt[:], dr["mask_uib"][:, :]), writes=[Rconst])
            b.dma("sync", lambda e: e.dma_start(onescb[:], dr["ones_colb"][:, :]), writes=[Rconst])
            b.dma("sync", lambda e: e.dma_start(idb[:], dr["ident"][:, :]), writes=[Rconst])
            b.dma("sync", lambda e: e.dma_start(idf[:], dr["identf"][:, :]), writes=[Rconst])
            b.dma("sync", lambda e: e.dma_start(msut[:], dr["mask_su"][:, :]), writes=[Rconst])
            b.dma("sync", lambda e: e.dma_start(muift[:], dr["mask_uif"][:, :]), writes=[Rconst])
            b.dma("sync", lambda e: e.dma_start(onesr[:], dr["ones_row"][:, :]), writes=[Rconst])
            b.dma("sync", lambda e: e.dma_start(onesrb[:], dr["ones_rowb"][:, :]), writes=[Rconst])
            b.dma("gpsimd", lambda e: e.dma_start(
                wsm[:, 0:128].rearrange("p (c e) -> p c e", e=8),
                dr["l0_router"].rearrange("(c p) e -> p c e", p=128)),
                writes=[Rwsm])

        def transpose(dst_ap, src_ap, rdst, rsrc, use2=False):
            pr = RPT2 if use2 else RPT
            pap = (PT2 if use2 else PT)[:, 0:128]
            b.op("tensor", lambda e: e.transpose(pap, src_ap, idb[:]),
                 reads=[rsrc, Rconst], writes=[pr])
            b.op("vector", lambda e: e.tensor_copy(dst_ap, pap),
                 reads=[pr], writes=[rdst])

        def wload(dst_ap, src_ap, wres, extra=()):
            if not isinstance(wres, (list, tuple)):
                wres = [wres]
            return b.dma("gpsimd", lambda e: e.dma_start(dst_ap, src_ap),
                         writes=wres, extra_waits=extra)

        def phase_ln(xin_ap, s1_track):
            for i in range(NT):
                b.dma("sync", lambda e, i=i: e.dma_start(
                    D1[:, :], xin_ap[i * 128:(i + 1) * 128, :]), writes=[RD1])
                b.op("vector", lambda e: e.reduce_sum(stat[:, 0:1], D1[:, :], AXX),
                     reads=[RD1], writes=[Rstat])
                b.op("scalar", lambda e: e.activation(
                    D2[:, 0:2048], D1[:, :], AF.Square, accum_out=stat[:, 1:2]),
                    reads=[RD1], writes=[RD2, Rstat])
                b.op("vector", lambda e: e.tensor_scalar(
                    stat[:, 2:3], stat[:, 0:1], 1.0 / DM, None, ALU.mult),
                    reads=[Rstat], writes=[Rstat])
                b.op("vector", lambda e: e.tensor_scalar(
                    stat[:, 3:4], stat[:, 1:2], 1.0 / DM, None, ALU.mult),
                    reads=[Rstat], writes=[Rstat])
                b.op("vector", lambda e: e.scalar_tensor_tensor(
                    stat[:, 4:5], stat[:, 2:3], stat[:, 2:3], stat[:, 3:4],
                    ALU.mult, ALU.subtract),
                    reads=[Rstat], writes=[Rstat])
                b.op("scalar", lambda e: e.activation(
                    stat[:, 5:6], stat[:, 4:5], AF.Sqrt, bias=cstc(0), scale=cstc(2)),
                    reads=[Rstat, Rcst], writes=[Rstat])
                b.op("vector", lambda e: e.reciprocal(stat[:, 6:7], stat[:, 5:6]),
                     reads=[Rstat], writes=[Rstat])
                b.op("vector", lambda e: e.tensor_scalar(
                    D2[:, 0:2048], D1[:, :], stat[:, 2:3], stat[:, 6:7],
                    ALU.subtract, ALU.mult),
                    reads=[RD1, Rstat], writes=[RD2])
                if s1_track:
                    b.op("vector", lambda e, i=i: e.reduce_max(
                        stat[:, 32 + i:33 + i], D2[:, 0:2048], AXX,
                        apply_absolute_value=True),
                        reads=[RD2], writes=[Rstat])
                for c in range(16):
                    transpose(A[:, c * 2048 + i * 128: c * 2048 + (i + 1) * 128],
                              D2[:, c * 128:(c + 1) * 128],
                              RA[c], RD2, use2=(c % 2 == 1))

        def proj_modeT(wdram, fdim, evac):
            nq = fdim // 512
            for fq in range(nq):
                k = fq % 2
                wr = RQ[k]
                wload(Q(k, 0, 8192).rearrange("p (c f) -> p c f", f=512),
                      wdram[:, fq * 512:(fq + 1) * 512]
                      .rearrange("(c p) f -> p c f", p=128), wr)
                for i in range(NT):
                    bank = RPA[i % 2]
                    bap = PAb(i % 2)
                    for kc in range(16):
                        b.op("tensor", lambda e, i=i, kc=kc, k=k, bap=bap: e.matmul(
                            bap,
                            A[:, kc * 2048 + i * 128: kc * 2048 + (i + 1) * 128],
                            Q(k, kc * 512, (kc + 1) * 512),
                            start=(kc == 0), stop=(kc == 15)),
                            reads=[RA[kc], wr], writes=[bank])
                    evac(i, fq, bap, bank)

        def phase_l0_kv():
            def evk(i, fq, bap, bank):
                b.op("scalar", lambda e, i=i, fq=fq, bap=bap: e.activation(
                    Bacc[:, i * 2048 + fq * 512: i * 2048 + (fq + 1) * 512],
                    bap, AF.Relu),
                    reads=[bank], writes=[RB[i]])
            proj_modeT(dr["l0_wk"], DM, evk)

            b.op("vector", lambda e: e.memset(
                D2[:, 0:520].rearrange("p (h c) -> p h c", c=65)[:, :, 64:65], 1.0),
                writes=[RD2])

            kv_first = {}

            def kvslot(j):
                if j < 7:
                    return RPB[0], PBp[:, j * 65:(j + 1) * 65]
                if j < 14:
                    return RPB[1], PBp[:, 512 + (j - 7) * 65: 512 + (j - 6) * 65]
                return RPA[3], PA[:, 1536 + (j - 14) * 65: 1536 + (j - 13) * 65]

            def evv(i, fq, bap, bank):
                b.op("vector", lambda e, bap=bap: e.tensor_copy(
                    D2[:, 0:520].rearrange("p (h c) -> p h c", c=65)[:, :, 0:64],
                    bap.rearrange("p (h c) -> p h c", c=64)),
                    reads=[bank], writes=[RD2])
                for hh in range(4):
                    h0 = 8 * fq + 2 * hh
                    j = h0 // 2
                    kr, kap = kvslot(j)
                    for half in range(2):
                        h = h0 + half
                        po = 64 * half
                        first = (id(kr), po) not in kv_first
                        kv_first[(id(kr), po)] = True
                        b.op("tensor",
                             lambda e, i=i, h=h, hh=hh, half=half, kap=kap, po=po, first=first: e.matmul(
                                 kap[po:po + 64, :],
                                 Bacc[:, i * 2048 + h * 64: i * 2048 + (h + 1) * 64],
                                 D2[:, (2 * hh + half) * 65:(2 * hh + half + 1) * 65],
                                 start=first, stop=(i == NT - 1),
                                 skip_group_check=True,
                                 tile_position=(0, po)),
                             reads=[RB[i], RD2], writes=[kr])
            proj_modeT(dr["l0_wv"], DM, evv)
            for j in range(16):
                kr, kap = kvslot(j)
                b.op("vector", lambda e, kap=kap, j=j: e.tensor_copy(
                    scanb[:, j * 65:(j + 1) * 65], kap),
                    reads=[kr], writes=[Rscb])

        def phase_l0_qo(sub=3):
            b.dma("gpsimd", lambda e: e.dma_start(D2[0:1, 0:2048], dr["l0_bo"][0:1, :]),
                  writes=[RD2])
            for fb in range(4):
                b.op("tensor", lambda e, fb=fb: e.matmul(
                    PAb(0), onesrb[:, :], D2[0:1, fb * 512:(fb + 1) * 512],
                    start=True, stop=True),
                    reads=[RD2, Rconst], writes=[RPA[0]])
                b.op("vector", lambda e, fb=fb: e.tensor_copy(
                    C1[:, 22528:24576][:, fb * 512:(fb + 1) * 512], PAb(0)),
                    reads=[RPA[0]], writes=[Rbob])

            def oslot(ts, half):
                lo = half * 512 + ts * 65
                return RPB[half], PBp[:, lo:lo + 65]

            if sub < 2:
                return
            for fq in range(4):
                k = fq % 2
                wr = RQ[k]
                wload(Q(k, 0, 8192).rearrange("p (c f) -> p c f", f=512),
                      dr["l0_wq"][:, fq * 512:(fq + 1) * 512]
                      .rearrange("(c p) f -> p c f", p=128), wr)
                for fc in range(4):
                    h0 = fq * 8 + fc * 2
                    j = h0 // 2
                    for tg in range(4):
                        bank = RPA[tg % 2]
                        bap = PAb(tg % 2)
                        for kc in range(16):
                            b.op("tensor", lambda e, fc=fc, tg=tg, kc=kc, k=k, bap=bap: e.matmul(
                                bap,
                                Q(k, kc * 512 + fc * 128, kc * 512 + (fc + 1) * 128),
                                A[:, kc * 2048 + tg * 512:kc * 2048 + (tg + 1) * 512],
                                start=(kc == 0), stop=(kc == 15)),
                                reads=[RA[kc], wr], writes=[bank])
                        b.op("scalar", lambda e, bap=bap: e.activation(
                            D2[:, 1024:1536], bap, AF.Relu),
                            reads=[bank], writes=[RD2])
                        for half in range(2):
                            po = 64 * half
                            for ts in range(4):
                                sr, sap = oslot(ts, half)
                                b.op("tensor", lambda e, ts=ts, j=j, po=po, sap=sap: e.matmul(
                                    sap,
                                    D2[po:po + 64, 1024 + ts * 128:1024 + (ts + 1) * 128],
                                    scanb[po:po + 64, j * 65:(j + 1) * 65],
                                    start=True, stop=True, skip_group_check=True),
                                    reads=[RD2, Rscb], writes=[sr])
                        for g in range(2):
                            b.op("vector", lambda e, g=g: e.tensor_scalar_max(
                                smallf[:, g * 4:(g + 1) * 4],
                                PBp[:, g * 512 + 64: g * 512 + 64 + 260]
                                .rearrange("p (s c) -> p s c", c=65)[:, :, 0:1],
                                1e-6),
                                reads=[RPB[g]], writes=[Rsm])
                        b.op("vector", lambda e: e.reciprocal(
                            smallf[:, 8:16], smallf[:, 0:8]),
                            reads=[Rsm], writes=[Rsm])
                        for ts in range(4):
                            for half in range(2):
                                sr, sap = oslot(ts, half)
                                b.op("vector", lambda e, ts=ts, half=half, sap=sap: e.tensor_scalar(
                                    D2[:, 2048 + ts * 128 + half * 64:
                                        2048 + ts * 128 + half * 64 + 64],
                                    sap[:, 0:64],
                                    smallf[:, 8 + half * 4 + ts:9 + half * 4 + ts],
                                    None, ALU.mult),
                                    reads=[sr, Rsm], writes=[RD2])
                        for ts in range(4):
                            transpose(D2[:, 2560 + ts * 128:2560 + (ts + 1) * 128],
                                      D2[:, 2048 + ts * 128:2048 + (ts + 1) * 128],
                                      RD2, RD2, use2=(ts % 2 == 1))
                        b.dma("sync", lambda e, h0=h0, tg=tg: e.dma_start(
                            o1_s[h0 * 64:h0 * 64 + 128,
                                 tg * 512:(tg + 1) * 512],
                            D2[:, 2560:3072]),
                            reads=[RD2], writes=[Rscr["o1_s"]])
            if sub < 3:
                return
            for fh in range(2):
                wr = [RQ[0], RQ[1]]
                wload(C1[:, 0:16384].rearrange("p (c f) -> p c f", f=1024),
                      dr["l0_wo"][:, fh * 1024:(fh + 1) * 1024]
                      .rearrange("(c p) f -> p c f", p=128), wr)
                for i in range(NT):
                    ob = RO[i % 2]
                    ocol = 16384 + (i % 2) * 2048
                    b.dma("sync", lambda e, i=i, ocol=ocol: e.dma_start(
                        C1[:, ocol:ocol + 2048].rearrange("p (c f) -> p c f", f=128),
                        o1_s[:, i * 128:(i + 1) * 128]
                        .rearrange("(c p) f -> p c f", p=128)),
                        reads=[Rscr["o1_s"]], writes=[ob])
                    for fb in range(2):
                        bank = RPA[2 * (i % 2) + fb]
                        bap = PAb(2 * (i % 2) + fb)
                        for c in range(16):
                            b.op("tensor", lambda e, c=c, fb=fb, bap=bap, ocol=ocol: e.matmul(
                                bap, C1[:, ocol + c * 128:ocol + (c + 1) * 128],
                                C1[:, c * 1024 + fb * 512:c * 1024 + (fb + 1) * 512],
                                start=(c == 0), stop=(c == 15)),
                                reads=[ob] + wr, writes=[bank])
                        b.op("vector", lambda e, i=i, fh=fh, fb=fb, bap=bap: e.tensor_tensor(
                            Bacc[:, i * 2048 + fh * 1024 + fb * 512:
                                 i * 2048 + fh * 1024 + (fb + 1) * 512],
                            bap, C1[:, 22528:24576][:, fh * 1024 + fb * 512:
                                      fh * 1024 + (fb + 1) * 512],
                            ALU.add),
                            reads=[bank, Rbob], writes=[RB[i]])

        def phase_router():
            for i in range(NT):
                bap = PAb(i % 2)
                bank = RPA[i % 2]
                for kc in range(16):
                    b.op("tensor", lambda e, i=i, kc=kc, bap=bap: e.matmul(
                        bap[:, 0:8],
                        A[:, kc * 2048 + i * 128: kc * 2048 + (i + 1) * 128],
                        wsm[:, kc * 8:(kc + 1) * 8],
                        start=(kc == 0), stop=(kc == 15)),
                        reads=[RA[kc], Rwsm], writes=[bank])
                lg = smallf[:, 16:24]
                b.op("vector", lambda e, bap=bap: e.tensor_copy(lg, bap[:, 0:8]),
                     reads=[bank], writes=[Rsm])
                b.op("vector", lambda e: e.reduce_max(smallf[:, 24:25], lg, AXX),
                     reads=[Rsm], writes=[Rsm])
                b.op("vector", lambda e: e.tensor_scalar(
                    smallf[:, 32:40], lg, smallf[:, 24:25], None, ALU.is_ge),
                    reads=[Rsm], writes=[Rsm])
                b.op("vector", lambda e: e.scalar_tensor_tensor(
                    smallf[:, 40:48], smallf[:, 32:40], -1e30, lg,
                    ALU.mult, ALU.add),
                    reads=[Rsm], writes=[Rsm])
                b.op("vector", lambda e: e.reduce_max(
                    smallf[:, 25:26], smallf[:, 40:48], AXX),
                    reads=[Rsm], writes=[Rsm])
                b.op("vector", lambda e: e.tensor_scalar(
                    smallf[:, 32:40], lg, smallf[:, 25:26], None, ALU.is_ge),
                    reads=[Rsm], writes=[Rsm])
                b.op("scalar", lambda e: e.activation(
                    smallf[:, 40:48], lg, AF.Sigmoid),
                    reads=[Rsm], writes=[Rsm])
                b.op("vector", lambda e, i=i: e.tensor_tensor(
                    rw[:, i * 8:(i + 1) * 8], smallf[:, 40:48], smallf[:, 32:40],
                    ALU.mult),
                    reads=[Rsm], writes=[Rrw])

        def phase_l0_moe(nexp=9):
            RQD = [Res(f"qd{i}") for i in range(8)]
            RHW = [Res(f"hws{i}") for i in range(4)]
            for e in list(range(8))[:max(0, nexp - 1)] + [8]:
                shared = (e == 8)
                for q in range(4):
                    dsrc = (dr["l0_sh_down"] if shared else dr["l0_down"][e])
                    for fh in range(2):
                        for pc in range(8):
                            b.dma("gpsimd", lambda e_, fh=fh, pc=pc, dsrc=dsrc: e_.dma_start(
                                Q(2, pc * 1024, (pc + 1) * 1024),
                                dsrc[pc * 128:(pc + 1) * 128,
                                     fh * 1024:(fh + 1) * 1024]),
                                writes=[RQD[pc]])
                        if fh == 0:
                            pend_gu = []
                            for fb in range(4):
                                k = fb % 2
                                gud = Q(k, 0, 8192).rearrange(
                                    "p (c f) -> p c f", f=512)
                                if shared:
                                    wload(gud[:, :, 0:256],
                                          dr["l0_sh_gate"][:, fb * 256:(fb + 1) * 256]
                                          .rearrange("(c p) f -> p c f", p=128), RQ[k])
                                    wload(gud[:, :, 256:512],
                                          dr["l0_sh_up"][:, fb * 256:(fb + 1) * 256]
                                          .rearrange("(c p) f -> p c f", p=128), RQ[k])
                                else:
                                    wload(gud[:, :, 0:256],
                                          dr["l0_gate_up"][e][:, fb * 256:(fb + 1) * 256]
                                          .rearrange("(c p) f -> p c f", p=128), RQ[k])
                                    wload(gud[:, :, 256:512],
                                          dr["l0_gate_up"][e][:, 1024 + fb * 256:1024 + (fb + 1) * 256]
                                          .rearrange("(c p) f -> p c f", p=128), RQ[k])
                                for t4 in range(4):
                                    i = q * 4 + t4
                                    bap = PBp[:, (t4 % 2) * 512:(t4 % 2) * 512 + 512]
                                    bank = RPB[t4 % 2]
                                    for kc in range(16):
                                        b.op("tensor", lambda e_, i=i, kc=kc, k=k, bap=bap: e_.matmul(
                                            bap,
                                            A[:, kc * 2048 + i * 128: kc * 2048 + (i + 1) * 128],
                                            Q(k, kc * 512, (kc + 1) * 512),
                                            start=(kc == 0), stop=(kc == 15)),
                                            reads=[RA[kc], RQ[k]], writes=[bank])
                                    if t4 < 3:
                                        hwg = scanb[:, t4 * 512:t4 * 512 + 256]
                                        hwh = scanb[:, t4 * 512 + 256:t4 * 512 + 512]
                                    else:
                                        hwg = hwrk[:, 0:256]
                                        hwh = scanb[:, 1536:1792]
                                    rhw = RHW[t4]
                                    if shared:
                                        b.op("scalar", lambda e_, bap=bap, hwg=hwg: e_.activation(
                                            hwg, bap[:, 0:256], AF.Silu),
                                            reads=[bank], writes=[rhw])
                                        b.op("vector", lambda e_, bap=bap, hwg=hwg, hwh=hwh: e_.tensor_tensor(
                                            hwh, bap[:, 256:512],
                                            hwg, ALU.mult),
                                            reads=[bank, rhw], writes=[rhw])
                                    else:
                                        b.op("scalar", lambda e_, i=i, e=e, bap=bap, hwg=hwg: e_.activation(
                                            hwg, bap[:, 0:256], AF.Silu,
                                            scale=rw[:, i * 8 + e:i * 8 + e + 1]),
                                            reads=[bank, Rrw], writes=[rhw])
                                        b.op("vector", lambda e_, i=i, e=e, bap=bap, hwg=hwg, hwh=hwh: e_.scalar_tensor_tensor(
                                            hwh, bap[:, 256:512],
                                            rw[:, i * 8 + e:i * 8 + e + 1],
                                            hwg, ALU.mult, ALU.mult),
                                            reads=[bank, Rrw, rhw], writes=[rhw])
                                    def mk_tr(fb=fb, t4=t4, hwh=hwh, rhw=rhw):
                                        def tr():
                                            for half in range(2):
                                                ptx = PT if half == 0 else PT2
                                                rptx = RPT if half == 0 else RPT2
                                                b.op("tensor", lambda e_, half=half, hwh=hwh, ptx=ptx: e_.transpose(
                                                    ptx[:, 0:128],
                                                    hwh[:, half * 128:(half + 1) * 128],
                                                    idb[:]),
                                                    reads=[rhw, Rconst], writes=[rptx])
                                                b.op("vector", lambda e_, fb=fb, t4=t4, half=half, ptx=ptx: e_.tensor_copy(
                                                    D2[:, (2 * fb + half) * 512 + t4 * 128:
                                                          (2 * fb + half) * 512 + (t4 + 1) * 128],
                                                    ptx[:, 0:128]),
                                                    reads=[rptx], writes=[RHF[2 * fb + half]])
                                        return tr
                                    pend_gu.append(mk_tr())
                                    if len(pend_gu) > 1:
                                        pend_gu.pop(0)()
                            while pend_gu:
                                pend_gu.pop(0)()
                        for t4 in range(4):
                            i = q * 4 + t4
                            bap = PA[:, (t4 % 2) * 1024:(t4 % 2) * 1024 + 1024]
                            banks = [RPA[(t4 % 2) * 2], RPA[(t4 % 2) * 2 + 1]]
                            for ic in range(8):
                                for fb2 in range(2):
                                    b.op("tensor", lambda e_, ic=ic, fb2=fb2, t4=t4, bap=bap: e_.matmul(
                                        bap[:, fb2 * 512:(fb2 + 1) * 512],
                                        D2[:, ic * 512 + t4 * 128: ic * 512 + (t4 + 1) * 128],
                                        Q(2, ic * 1024 + fb2 * 512, ic * 1024 + (fb2 + 1) * 512),
                                        start=(ic == 0), stop=(ic == 7)),
                                        reads=[RHF[ic], RQD[ic]], writes=[banks[fb2]])
                            b.op("vector", lambda e_, i=i, fh=fh, bap=bap: e_.tensor_tensor(
                                Bacc[:, i * 2048 + fh * 1024: i * 2048 + (fh + 1) * 1024],
                                bap,
                                Bacc[:, i * 2048 + fh * 1024: i * 2048 + (fh + 1) * 1024],
                                ALU.add),
                                reads=banks + [RB[i]], writes=[RB[i]])

        def phase_x1_ln():
            for i in range(NT):
                xcol = (i % 2) * 2048
                xres = RQ[i % 2]
                b.dma("gpsimd", lambda e, i=i, xcol=xcol: e.dma_start(
                    Q(0, xcol, xcol + 2048), dr["x0"][i * 128:(i + 1) * 128, :]),
                    writes=[xres])
                b.op("vector", lambda e, i=i, xcol=xcol: e.tensor_tensor(
                    D1[:, :], Q(0, xcol, xcol + 2048),
                    Bacc[:, i * 2048:(i + 1) * 2048], ALU.add),
                    reads=[xres, RB[i]], writes=[RD1])
                b.dma("sync", lambda e, i=i: e.dma_start(
                    x1_s[i * 128:(i + 1) * 128, :], D1[:, :]),
                    reads=[RD1], writes=[Rscr["x1_s"]])
                b.op("vector", lambda e: e.reduce_sum(stat[:, 0:1], D1[:, :], AXX),
                     reads=[RD1], writes=[Rstat])
                b.op("scalar", lambda e: e.activation(
                    D2[:, 0:2048], D1[:, :], AF.Square, accum_out=stat[:, 1:2]),
                    reads=[RD1], writes=[RD2, Rstat])
                b.op("vector", lambda e: e.tensor_scalar(
                    stat[:, 2:3], stat[:, 0:1], 1.0 / DM, None, ALU.mult),
                    reads=[Rstat], writes=[Rstat])
                b.op("vector", lambda e: e.tensor_scalar(
                    stat[:, 3:4], stat[:, 1:2], 1.0 / DM, None, ALU.mult),
                    reads=[Rstat], writes=[Rstat])
                b.op("vector", lambda e: e.scalar_tensor_tensor(
                    stat[:, 4:5], stat[:, 2:3], stat[:, 2:3], stat[:, 3:4],
                    ALU.mult, ALU.subtract),
                    reads=[Rstat], writes=[Rstat])
                b.op("scalar", lambda e: e.activation(
                    stat[:, 5:6], stat[:, 4:5], AF.Sqrt, bias=cstc(0), scale=cstc(2)),
                    reads=[Rstat, Rcst], writes=[Rstat])
                b.op("vector", lambda e: e.reciprocal(stat[:, 6:7], stat[:, 5:6]),
                     reads=[Rstat], writes=[Rstat])
                b.op("vector", lambda e: e.tensor_scalar(
                    D2[:, 0:2048], D1[:, :], stat[:, 2:3], stat[:, 6:7],
                    ALU.subtract, ALU.mult),
                    reads=[RD1, Rstat], writes=[RD2])
                b.op("vector", lambda e, i=i: e.reduce_max(
                    stat[:, 32 + i:33 + i], D2[:, 0:2048], AXX,
                    apply_absolute_value=True),
                    reads=[RD2], writes=[Rstat])
                for c in range(16):
                    transpose(A[:, c * 2048 + i * 128: c * 2048 + (i + 1) * 128],
                              D2[:, c * 128:(c + 1) * 128],
                              RA[c], RD2, use2=(c % 2 == 1))

        def phase_l1_proj():
            b.dma("sync", lambda e: e.dma_start(
                statx[:, 0:64].rearrange("p (c j) -> p c j", j=4),
                dr["l1_qcw"].rearrange("(c p) j -> p c j", p=128)), writes=[Rstx])
            b.dma("sync", lambda e: e.dma_start(
                statx[:, 64:128].rearrange("p (c j) -> p c j", j=4),
                dr["l1_kcw"].rearrange("(c p) j -> p c j", p=128)), writes=[Rstx])
            b.dma("sync", lambda e: e.dma_start(
                statx[:, 128:256].rearrange("p (c j) -> p c j", j=4),
                dr["l1_vcw"].rearrange("(c p) j -> p c j", p=128)), writes=[Rstx])
            b.dma("sync", lambda e: e.dma_start(
                statx[:, 256:272].rearrange("p (c o) -> p c o", o=1),
                dr["l1_qcb"].rearrange("o (c p) -> p c o", p=128),
                allow_slow_non_contiguous=True), writes=[Rstx])
            b.dma("sync", lambda e: e.dma_start(
                statx[:, 272:288].rearrange("p (c o) -> p c o", o=1),
                dr["l1_kcb"].rearrange("o (c p) -> p c o", p=128),
                allow_slow_non_contiguous=True), writes=[Rstx])
            b.dma("sync", lambda e: e.dma_start(
                statx[:, 288:320].rearrange("p (c o) -> p c o", o=1),
                dr["l1_vcb"].rearrange("o (c p) -> p c o", p=128),
                allow_slow_non_contiguous=True), writes=[Rstx])

            plans = [
                (dr["l1_wq"], q_s, 4, True, 0, 256),
                (dr["l1_wk"], k_s, 4, True, 64, 272),
                (dr["l1_wv"], v_s, 8, False, 128, 288),
            ]
            ZC, T1 = 0, 516
            for wd, scr, nq, l2, cwo, cbo in plans:
                for fq in range(nq):
                    k_ = fq % 2
                    wload(Q(k_, 0, 8192).rearrange("p (c f) -> p c f", f=512),
                          wd[:, fq * 512:(fq + 1) * 512]
                          .rearrange("(c p) f -> p c f", p=128), RQ[k_])
                    for fc in range(4):
                        ch = fq * 4 + fc
                        b.op("vector", lambda e: e.memset(
                            scanb[:, ZC:ZC + 3], 0.0), writes=[Rscb])
                        for tg in range(4):
                            bank = RPA[tg % 2]
                            bap = PAb(tg % 2)
                            for kc in range(16):
                                b.op("tensor", lambda e, fc=fc, tg=tg, kc=kc, k_=k_, bap=bap: e.matmul(
                                    bap,
                                    Q(k_, kc * 512 + fc * 128, kc * 512 + (fc + 1) * 128),
                                    A[:, kc * 2048 + tg * 512:kc * 2048 + (tg + 1) * 512],
                                    start=(kc == 0), stop=(kc == 15)),
                                    reads=[RA[kc], RQ[k_]], writes=[bank])
                            b.op("vector", lambda e, bap=bap: e.tensor_copy(
                                scanb[:, ZC + 3:ZC + 515], bap),
                                reads=[bank], writes=[Rscb])
                            b.op("vector", lambda e, ch=ch, cwo=cwo: e.tensor_scalar(
                                scanb[:, T1:T1 + 512], scanb[:, ZC:ZC + 512],
                                statx[:, cwo + ch * 4:cwo + ch * 4 + 1], None, ALU.mult),
                                reads=[Rscb, Rstx], writes=[Rscb])
                            for j in (1, 2, 3):
                                b.op("vector", lambda e, j=j, ch=ch, cwo=cwo: e.scalar_tensor_tensor(
                                    scanb[:, T1:T1 + 512], scanb[:, ZC + j:ZC + j + 512],
                                    statx[:, cwo + ch * 4 + j:cwo + ch * 4 + j + 1],
                                    scanb[:, T1:T1 + 512], ALU.mult, ALU.add),
                                    reads=[Rscb, Rstx], writes=[Rscb])
                            b.op("vector", lambda e: e.tensor_copy(
                                scanb[:, ZC:ZC + 3], scanb[:, ZC + 512:ZC + 515]),
                                reads=[Rscb], writes=[Rscb])
                            b.op("scalar", lambda e, tg=tg, ch=ch, cbo=cbo: e.activation(
                                D2[:, tg * 512:(tg + 1) * 512], scanb[:, T1:T1 + 512],
                                AF.Silu, bias=statx[:, cbo + ch:cbo + ch + 1]),
                                reads=[Rscb, Rstx], writes=[RD2])
                            if l2:
                                b.op("vector", lambda e, tg=tg: e.tensor_tensor(
                                    scanb[:, T1:T1 + 512],
                                    D2[:, tg * 512:(tg + 1) * 512],
                                    D2[:, tg * 512:(tg + 1) * 512], ALU.mult),
                                    reads=[RD2], writes=[Rscb])
                                b.op("tensor", lambda e: e.matmul(
                                    PBp[0:1, 0:512], onescb[:], scanb[:, T1:T1 + 512],
                                    start=True, stop=True),
                                    reads=[Rscb, Rconst], writes=[RPB[0]])
                                b.op("scalar", lambda e: e.activation(
                                    D1[0:1, 0:512], PBp[0:1, 0:512], AF.Sqrt),
                                    reads=[RPB[0]], writes=[RD1])
                                b.op("vector", lambda e: e.tensor_scalar_max(
                                    D1[0:1, 0:512], D1[0:1, 0:512], 1e-12),
                                    reads=[RD1], writes=[RD1])
                                b.op("vector", lambda e: e.reciprocal(
                                    D1[0:1, 512:1024], D1[0:1, 0:512]),
                                    reads=[RD1], writes=[RD1])
                                b.op("tensor", lambda e: e.matmul(
                                    PBp[:, 512:1024], onesr[:], D1[0:1, 512:1024],
                                    start=True, stop=True),
                                    reads=[RD1, Rconst], writes=[RPB[1]])
                                b.op("vector", lambda e, tg=tg: e.tensor_tensor(
                                    D2[:, tg * 512:(tg + 1) * 512],
                                    D2[:, tg * 512:(tg + 1) * 512],
                                    PBp[:, 512:1024], ALU.mult),
                                    reads=[RD2, RPB[1]], writes=[RD2])
                        b.dma("sync", lambda e, scr=scr, ch=ch: e.dma_start(
                            scr[ch * 128:(ch + 1) * 128, :], D2[:, 0:2048]),
                            reads=[RD2], writes=[Rscr[{id(q_s): "q_s", id(k_s): "k_s", id(v_s): "v_s"}[id(scr)]]])
            b.dma("gpsimd", lambda e: e.dma_start(
                C1[:, 22528:24576][:, 0:256].rearrange("p (c f) -> p c f", f=16),
                dr["l1_wa"].rearrange("(c p) f -> p c f", p=128)), writes=[Rbob])
            b.dma("gpsimd", lambda e: e.dma_start(
                C1[:, 22528:24576][:, 256:512].rearrange("p (c f) -> p c f", f=16),
                dr["l1_wb"].rearrange("(c p) f -> p c f", p=128)), writes=[Rbob])
            for tg in range(4):
                for which in range(2):
                    bap = PBp[0:16, which * 512:which * 512 + 512]
                    bank = RPB[which]
                    for kc in range(16):
                        b.op("tensor", lambda e, tg=tg, kc=kc, which=which, bap=bap: e.matmul(
                            bap,
                            C1[:, 22528:24576][:, which * 256 + kc * 16:which * 256 + (kc + 1) * 16],
                            A[:, kc * 2048 + tg * 512:kc * 2048 + (tg + 1) * 512],
                            start=(kc == 0), stop=(kc == 15)),
                            reads=[RA[kc], Rbob], writes=[bank])
                b.op("scalar", lambda e: e.activation(
                    statx[0:16, 704:1216], PBp[0:16, 0:512], AF.Sigmoid),
                    reads=[RPB[0]], writes=[Rstx])
                b.op("scalar", lambda e: e.activation(
                    D2[0:16, 0:512], PBp[0:16, 512:1024], AF.Sigmoid),
                    reads=[RPB[1]], writes=[RD2])
                b.op("scalar", lambda e: e.activation(
                    statx[0:16, 704:1216], statx[0:16, 704:1216], AF.Ln),
                    reads=[Rstx], writes=[Rstx])
                for blk in range(4):
                    cidx = tg * 4 + blk
                    b.op("tensor", lambda e, blk=blk: e.transpose(
                        PA[:, 0:16], statx[0:16, 704 + blk * 128:704 + (blk + 1) * 128],
                        idf[0:16, 0:16]),
                        reads=[Rstx, Rconst], writes=[RPA[0]])
                    b.op("vector", lambda e, cidx=cidx: e.tensor_copy(
                        statx[:, 320 + cidx * 16:320 + (cidx + 1) * 16], PA[:, 0:16]),
                        reads=[RPA[0]], writes=[Rstx])
                    b.op("tensor", lambda e, blk=blk: e.transpose(
                        PT[:, 0:16], D2[0:16, blk * 128:(blk + 1) * 128], idb[0:16, 0:16]),
                        reads=[RD2, Rconst], writes=[RPT])
                    b.op("vector", lambda e, cidx=cidx: e.tensor_copy(
                        bT[:, cidx * 16:(cidx + 1) * 16], PT[:, 0:16]),
                        reads=[RPT], writes=[RbT])

        def phase_l1_scan():
            RD2o = Res("d2odd")
            RD1h = [Res(f"st{h}") for h in range(8)]
            Rstx0, Rstx1 = Res("sx0"), Res("sx1")

            def SL(par, off, w):
                if par == 0:
                    return scanb[:, off:off + w], Rscb
                return D2[:, 2048 + off:2048 + off + w], RD2o
            FS = {0: dict(eL=592, eLn=593, negL=594, eLCb=595,
                          expD=736, w1f=864),
                  1: dict(eL=216, eLn=217, negL=218, eLCb=219,
                          expD=88, w1f=992)}
            PBK = {0: dict(b1=(PA, 0, RPA[0]), b2=(PA, 512, RPA[1]),
                           b3=(PA, 1024, RPA[2]), pt=(PT, RPT)),
                   1: dict(b1=(PA, 1536, RPA[3]), b2=(PBp, 0, RPB[0]),
                           b3=(PBp, 512, RPB[1]), pt=(PT2, RPT2))}

            for hf in range(2):
                b.op("vector", lambda e: e.memset(D1[:, :], 0.0), writes=RD1h)
                for c in range(16):
                    d = c % 2
                    base = d * 4096
                    qr = RQ[d]
                    for nm, lo, hi, rows in (("q", 0, 1024, 1024),
                                             ("k", 1024, 2048, 1024),
                                             ("v", 2048, 4096, 2048)):
                        scr = {"q": q_s, "k": k_s, "v": v_s}[nm]
                        b.dma("sync", lambda e, hf=hf, c=c, base=base, lo=lo, scr=scr, rows=rows: e.dma_start(
                            Q(0, base + lo, base + lo + rows)
                            .rearrange("p (h f) -> p h f", f=128),
                            scr[hf * rows:(hf + 1) * rows, c * 128:(c + 1) * 128]
                            .rearrange("(h p) f -> p h f", p=128)),
                            writes=[qr])

                    def qf(h, base=base):
                        return Q(0, base + h * 128, base + (h + 1) * 128)

                    def kf(h, base=base):
                        return Q(0, base + 1024 + h * 128, base + 1024 + (h + 1) * 128)

                    def vf(h, half, base=base):
                        return Q(0, base + 2048 + (2 * h + half) * 128,
                                 base + 2048 + (2 * h + half + 1) * 128)

                    b.op("tensor", lambda e, c=c: e.matmul(
                        PA[:, 1024:1040], muift[:],
                        statx[:, 320 + c * 16:320 + (c + 1) * 16],
                        start=True, stop=True),
                        reads=[Rstx, Rconst], writes=[RPA[2]])
                    b.op("vector", lambda e: e.tensor_copy(
                        statx[:, 576:592], PA[:, 1024:1040]),
                        reads=[RPA[2]], writes=[Rstx])

                    def head_steps(h, kf=kf, qf=qf, vf=vf, qr=qr):
                        par = h % 2
                        hg = hf * 8 + h
                        F = FS[par]
                        BK = PBK[par]
                        pb1, o1_, rb1 = BK["b1"]
                        pb2, o2_, rb2 = BK["b2"]
                        pb3, o3_, rb3 = BK["b3"]
                        ptp, rpt = BK["pt"]
                        Lcol = statx[:, 576 + hg:577 + hg]
                        bcol = bT[:, c * 16 + hg:c * 16 + hg + 1]
                        sst = D1[:, h * 256:(h + 1) * 256]
                        rs = RD1h[h]
                        rsx = Rstx0 if par == 0 else Rstx1
                        lrow = statx[0:1, 608:736] if par == 0 else rw[0:1, 0:128]
                        lrr = rsx if par == 0 else Rrw
                        s0b, s0r = SL(par, 1152, 256)
                        vtm, vtr = SL(par, 1408, 256)
                        ntc, ntr = SL(par, 0, 128)
                        y0, _ = SL(par, 640, 256)
                        y1s, _ = SL(par, 896, 256)
                        ktm, _ = SL(par, 1664, 128)
                        kp, _ = SL(par, 1792, 128)
                        pta, _ = SL(par, 128, 128)
                        ptb, _ = SL(par, 256, 128)
                        pna, _ = SL(par, 384, 128)
                        pnb, _ = SL(par, 512, 128)
                        sres = ntr
                        S = []
                        A_ = S.append
                        A_(lambda: b.op("vector", lambda e: e.tensor_copy(s0b, sst),
                                        reads=[rs], writes=[sres]))
                        for half in range(2):
                            A_(lambda half=half: b.op("tensor", lambda e: e.transpose(
                                ptp[:, 0:128], vf(h, half), idb[:]),
                                reads=[qr, Rconst], writes=[rpt]))
                            A_(lambda half=half: b.op("scalar", lambda e: e.copy(
                                vtm[:, half * 128:(half + 1) * 128], ptp[:, 0:128]),
                                reads=[rpt], writes=[sres]))
                        A_(lambda: b.op("scalar", lambda e: e.activation(
                            statx[:, F["eL"]:F["eL"] + 1], Lcol, AF.Exp),
                            reads=[Rstx], writes=[rsx]))
                        A_(lambda: b.op("vector", lambda e: e.tensor_scalar(
                            statx[:, F["eLn"]:F["eLn"] + 1],
                            statx[:, F["eL"]:F["eL"] + 1], -1.0, None, ALU.mult),
                            reads=[rsx], writes=[rsx]))
                        A_(lambda: b.op("vector", lambda e: e.tensor_scalar(
                            statx[:, F["negL"]:F["negL"] + 1], Lcol, -1.0,
                            None, ALU.mult),
                            reads=[Rstx], writes=[rsx]))
                        A_(lambda: b.op("tensor", lambda e: e.transpose(
                            pb3[0:1, o3_:o3_ + 128], Lcol, idf[:]),
                            reads=[Rstx, Rconst], writes=[rb3]))
                        A_(lambda: b.op("vector", lambda e: e.tensor_copy(
                            lrow, pb3[0:1, o3_:o3_ + 128]),
                            reads=[rb3], writes=[lrr]))
                        A_(lambda: b.op("tensor", lambda e: e.matmul(
                            pb3[:, o3_:o3_ + 128], onesr[:], lrow,
                            start=True, stop=True),
                            reads=[lrr, Rconst], writes=[rb3]))
                        A_(lambda: b.op("vector", lambda e: e.tensor_scalar(
                            pb3[:, o3_ + 128:o3_ + 256],
                            pb3[:, o3_:o3_ + 128],
                            statx[:, F["negL"]:F["negL"] + 1], 0.0,
                            ALU.add, ALU.min),
                            reads=[rb3, rsx], writes=[rb3]))
                        A_(lambda: b.op("scalar", lambda e: e.activation(
                            statx[:, F["expD"]:F["expD"] + 128],
                            pb3[:, o3_ + 128:o3_ + 256], AF.Exp),
                            reads=[rb3], writes=[rsx]))
                        A_(lambda: b.op("tensor", lambda e: e.matmul(
                            pb1[:, o1_:o1_ + 256], kf(h), s0b,
                            start=True, stop=True),
                            reads=[qr, sres], writes=[rb1]))
                        A_(lambda: b.op("vector", lambda e: e.scalar_tensor_tensor(
                            y0, pb1[:, o1_:o1_ + 256],
                            statx[:, F["eLn"]:F["eLn"] + 1], vtm,
                            ALU.mult, ALU.add),
                            reads=[rb1, rsx, sres], writes=[sres]))
                        A_(lambda: b.op("tensor", lambda e: e.matmul(
                            pb2[:, o2_:o2_ + 128], kf(h), kf(h),
                            start=True, stop=True),
                            reads=[qr], writes=[rb2]))
                        A_(lambda: b.op("vector", lambda e: e.tensor_tensor(
                            statx[:, F["w1f"]:F["w1f"] + 128],
                            statx[:, F["expD"]:F["expD"] + 128],
                            pb2[:, o2_:o2_ + 128], ALU.mult),
                            reads=[rsx, rb2], writes=[rsx]))
                        A_(lambda: b.op("vector", lambda e: e.scalar_tensor_tensor(
                            ntc, statx[:, F["w1f"]:F["w1f"] + 128], bcol,
                            msut[:], ALU.mult, ALU.mult),
                            reads=[rsx, RbT, Rconst], writes=[sres]))
                        A_(lambda: b.op("tensor", lambda e: e.transpose(
                            ptp[:, 0:128], ntc, idb[:]),
                            reads=[sres, Rconst], writes=[rpt]))
                        A_(lambda: b.op("scalar", lambda e: e.copy(
                            pna, ptp[:, 0:128]),
                            reads=[rpt], writes=[sres]))
                        A_(lambda: b.op("vector", lambda e: e.tensor_copy(
                            pta, ntc),
                            reads=[sres], writes=[sres]))
                        ycur, ynext = y0, y1s
                        pt_cur, pt_nxt = pta, ptb
                        pn_cur, pn_nxt = pna, pnb
                        for j in range(7):
                            A_(lambda pt_cur=pt_cur, ycur=ycur: b.op("tensor", lambda e: e.matmul(
                                pb1[:, o1_:o1_ + 256], pt_cur, ycur,
                                start=True, stop=True),
                                reads=[sres], writes=[rb1]))
                            A_(lambda ycur=ycur, ynext=ynext: b.op("vector", lambda e: e.tensor_tensor(
                                ynext, pb1[:, o1_:o1_ + 256], ycur, ALU.add),
                                reads=[rb1, sres], writes=[sres]))
                            ycur, ynext = ynext, ycur
                            if j < 6:
                                A_(lambda pn_cur=pn_cur, pt_cur=pt_cur: b.op("tensor", lambda e: e.matmul(
                                    pb2[:, o2_:o2_ + 128], pn_cur, pt_cur,
                                    start=True, stop=True),
                                    reads=[sres], writes=[rb2]))
                                A_(lambda pt_nxt=pt_nxt: b.op("scalar", lambda e: e.copy(
                                    pt_nxt, pb2[:, o2_:o2_ + 128]),
                                    reads=[rb2], writes=[sres]))
                                if j < 5:
                                    A_(lambda pt_cur=pt_cur, pn_cur=pn_cur: b.op("tensor", lambda e: e.matmul(
                                        pb2[:, o2_:o2_ + 128], pt_cur, pn_cur,
                                        start=True, stop=True),
                                        reads=[sres], writes=[rb2]))
                                    A_(lambda pn_nxt=pn_nxt: b.op("scalar", lambda e: e.copy(
                                        pn_nxt, pb2[:, o2_:o2_ + 128]),
                                        reads=[rb2], writes=[sres]))
                                pt_cur, pt_nxt = pt_nxt, pt_cur
                                pn_cur, pn_nxt = pn_nxt, pn_cur
                        R_ = ycur
                        Rtmp = y1s if R_ is y0 else y0
                        A_(lambda: b.op("tensor", lambda e: e.matmul(
                            pb2[:, o2_:o2_ + 128], kf(h), qf(h),
                            start=True, stop=True),
                            reads=[qr], writes=[rb2]))
                        A_(lambda: b.op("vector", lambda e: e.tensor_tensor(
                            statx[:, F["w1f"]:F["w1f"] + 128],
                            statx[:, F["expD"]:F["expD"] + 128],
                            pb2[:, o2_:o2_ + 128], ALU.mult),
                            reads=[rsx, rb2], writes=[rsx]))
                        A_(lambda: b.op("vector", lambda e: e.scalar_tensor_tensor(
                            ntc, statx[:, F["w1f"]:F["w1f"] + 128], bcol,
                            muibt[:], ALU.mult, ALU.mult),
                            reads=[rsx, RbT, Rconst], writes=[sres]))
                        A_(lambda R_=R_: b.op("tensor", lambda e: e.matmul(
                            pb1[:, o1_ + 256:o1_ + 512], ntc, R_,
                            start=True, stop=True, skip_group_check=True),
                            reads=[sres], writes=[rb1]))
                        A_(lambda: b.op("tensor", lambda e: e.matmul(
                            pb1[:, o1_:o1_ + 256], qf(h), s0b,
                            start=True, stop=True, skip_group_check=True),
                            reads=[qr, sres], writes=[rb1]))
                        A_(lambda Rtmp=Rtmp: b.op("vector", lambda e: e.tensor_scalar(
                            Rtmp, pb1[:, o1_:o1_ + 256],
                            statx[:, F["eL"]:F["eL"] + 1], None, ALU.mult),
                            reads=[rb1, rsx], writes=[sres]))
                        A_(lambda Rtmp=Rtmp: b.op("vector", lambda e: e.tensor_tensor(
                            Rtmp, Rtmp, pb1[:, o1_ + 256:o1_ + 512], ALU.add),
                            reads=[sres, rb1], writes=[sres]))
                        for half in range(2):
                            A_(lambda half=half, Rtmp=Rtmp: b.op("tensor", lambda e: e.transpose(
                                ptp[:, 0:128], Rtmp[:, half * 128:(half + 1) * 128],
                                idb[:]),
                                reads=[sres, Rconst], writes=[rpt]))
                            A_(lambda half=half: b.op("scalar", lambda e: e.copy(
                                D2[:, h * 256 + half * 128:h * 256 + (half + 1) * 128],
                                ptp[:, 0:128]),
                                reads=[rpt], writes=[RD2]))
                        A_(lambda: b.op("tensor", lambda e: e.transpose(
                            ptp[:, 0:128], kf(h), idb[:]),
                            reads=[qr, Rconst], writes=[rpt]))
                        A_(lambda: b.op("scalar", lambda e: e.copy(
                            ktm, ptp[:, 0:128]),
                            reads=[rpt], writes=[sres]))
                        A_(lambda: b.op("vector", lambda e: e.tensor_tensor(
                            statx[:, F["negL"]:F["negL"] + 1],
                            statx[:, F["expD"] + 127:F["expD"] + 128], bcol,
                            ALU.mult),
                            reads=[rsx, RbT], writes=[rsx]))
                        A_(lambda: b.op("vector", lambda e: e.tensor_scalar(
                            kp, ktm, statx[:, F["negL"]:F["negL"] + 1],
                            None, ALU.mult),
                            reads=[sres, rsx], writes=[sres]))
                        A_(lambda R_=R_: b.op("tensor", lambda e: e.matmul(
                            pb3[:, o3_ + 256:o3_ + 512], kp, R_,
                            start=True, stop=True, skip_group_check=True),
                            reads=[sres], writes=[rb3]))
                        A_(lambda: b.op("vector", lambda e: e.tensor_tensor(
                            statx[:, F["eLCb"]:F["eLCb"] + 1],
                            statx[:, F["expD"] + 127:F["expD"] + 128],
                            statx[:, F["eL"]:F["eL"] + 1], ALU.mult),
                            reads=[rsx], writes=[rsx]))
                        A_(lambda: b.op("vector", lambda e: e.scalar_tensor_tensor(
                            sst, sst, statx[:, F["eLCb"]:F["eLCb"] + 1],
                            pb3[:, o3_ + 256:o3_ + 512], ALU.mult, ALU.add),
                            reads=[rs, rsx, rb3], writes=[rs]))
                        return S

                    for hp in range(4):
                        s_even = head_steps(2 * hp)
                        s_odd = head_steps(2 * hp + 1)
                        n = max(len(s_even), len(s_odd))
                        for idx in range(n):
                            if idx < len(s_even):
                                s_even[idx]()
                            if idx < len(s_odd):
                                s_odd[idx]()
                    b.dma("sync", lambda e, hf=hf, c=c: e.dma_start(
                        outs_s[hf * 2048:(hf + 1) * 2048, c * 128:(c + 1) * 128]
                        .rearrange("(h p) f -> p h f", p=128),
                        D2[:, 0:2048].rearrange("p (h f) -> p h f", f=128)),
                        reads=[RD2], writes=[Rscr["outs_s"]])

        def phase_l1_wo():
            for kq in range(4):
                wr = [RQ[0], RQ[1]]
                wload(C1[:, 0:16384].rearrange("p (c f) -> p c f", f=2048),
                      dr["l1_wo"][kq * 1024:(kq + 1) * 1024, :]
                      .rearrange("(c p) f -> p c f", p=128), wr)
                for i in range(NT):
                    ob = RO[i % 2]
                    ocol = 16384 + (i % 2) * 1024
                    b.dma("sync", lambda e, i=i, kq=kq, ocol=ocol: e.dma_start(
                        C1[:, ocol:ocol + 1024].rearrange("p (c f) -> p c f", f=128),
                        outs_s[kq * 1024:(kq + 1) * 1024, i * 128:(i + 1) * 128]
                        .rearrange("(c p) f -> p c f", p=128)),
                        reads=[Rscr["outs_s"]], writes=[ob])
                    if kq == 3:
                        b.dma("gpsimd", lambda e, i=i: e.dma_start(
                            D1[:, :], x1_s[i * 128:(i + 1) * 128, :]),
                            reads=[Rscr["x1_s"], RD1], writes=[RD1])
                    for fb in range(4):
                        bank = RPA[fb]
                        bap = PAb(fb)
                        for cc in range(8):
                            b.op("tensor", lambda e, cc=cc, fb=fb, bap=bap, ocol=ocol: e.matmul(
                                bap, C1[:, ocol + cc * 128:ocol + (cc + 1) * 128],
                                C1[:, cc * 2048 + fb * 512:cc * 2048 + (fb + 1) * 512],
                                start=(cc == 0), stop=(cc == 7)),
                                reads=[ob] + wr, writes=[bank])
                        if kq == 0:
                            b.op("vector", lambda e, i=i, fb=fb, bap=bap: e.tensor_copy(
                                Bacc[:, i * 2048 + fb * 512:i * 2048 + (fb + 1) * 512],
                                bap),
                                reads=[bank, RB[i]], writes=[RB[i]])
                        elif kq < 3:
                            b.op("vector", lambda e, i=i, fb=fb, bap=bap: e.tensor_tensor(
                                Bacc[:, i * 2048 + fb * 512:i * 2048 + (fb + 1) * 512],
                                bap,
                                Bacc[:, i * 2048 + fb * 512:i * 2048 + (fb + 1) * 512],
                                ALU.add),
                                reads=[bank, RB[i]], writes=[RB[i]])
                        else:
                            b.op("vector", lambda e, i=i, fb=fb, bap=bap: e.tensor_tensor(
                                scanb[:, 0:512], bap,
                                Bacc[:, i * 2048 + fb * 512:i * 2048 + (fb + 1) * 512],
                                ALU.add),
                                reads=[bank, RB[i]], writes=[Rscb])
                            st_ = 704 if fb % 2 == 0 else 192
                            b.op("vector", lambda e, fb=fb, st_=st_: e.tensor_tensor(
                                statx[:, st_:st_ + 512], scanb[:, 0:512],
                                D1[:, fb * 512:(fb + 1) * 512], ALU.add),
                                reads=[Rscb, RD1], writes=[Rstx])
                            b.dma("sync", lambda e, i=i, fb=fb, st_=st_: e.dma_start(
                                xa_s[i * 128:(i + 1) * 128, fb * 512:(fb + 1) * 512],
                                statx[:, st_:st_ + 512]),
                                reads=[Rstx], writes=[Rscr["xa_s"]])

        def round_min_sign(dst_bf_ap, dst_res, clipv, sg_ap, sg_res, n, par=0):
            ao = 704 if par == 0 else 88
            io = par * 512
            av = statx[:, ao:ao + n]
            it = ITL[:, io:io + n]
            rit = RIT if par == 0 else RIT1
            b.op("scalar", lambda e: e.activation(it, av, AF.Identity),
                 reads=[Rstx], writes=[rit])
            b.op("vector", lambda e: e.scalar_tensor_tensor(
                av, it, -1.0, av, ALU.mult, ALU.add),
                reads=[rit, Rstx], writes=[Rstx])
            b.op("vector", lambda e: e.scalar_tensor_tensor(
                av, av, 0.5, it, ALU.is_ge, ALU.add),
                reads=[Rstx, rit], writes=[Rstx])
            b.op("vector", lambda e: e.scalar_tensor_tensor(
                dst_bf_ap, av, float(clipv), sg_ap,
                ALU.min, ALU.mult),
                reads=[Rstx, sg_res], writes=[dst_res])

        RSGP = [Res("sgp0"), Res("sgp1")]

        def quant_chain2(dst_ap, dst_res, src_f32_ap, inv_scale_ap, clipv,
                         sg_ap, n, par=0, rsrc_extra=()):
            ao = 704 if par == 0 else 88
            b.op("scalar", lambda e: e.activation(
                statx[:, ao:ao + n], src_f32_ap, AF.Abs, scale=inv_scale_ap),
                reads=[Rstx] + list(rsrc_extra), writes=[Rstx])
            b.op("scalar", lambda e: e.activation(
                sg_ap, src_f32_ap, AF.Sign),
                reads=list(rsrc_extra), writes=[RSGP[par]])
            round_min_sign(dst_ap, dst_res, clipv, sg_ap, RSGP[par], n, par=par)

        def quant_chain(dst_bf_ap, src_f32_ap, inv_scale_ap, clipv, sg_ap,
                        n, par=0, rsrc_extra=()):
            ao = 704 if par == 0 else 88
            b.op("scalar", lambda e: e.activation(
                statx[:, ao:ao + n], src_f32_ap, AF.Abs, scale=inv_scale_ap),
                reads=[RD1, Rstx] + list(rsrc_extra), writes=[Rstx])
            b.op("scalar", lambda e: e.activation(
                sg_ap, src_f32_ap, AF.Sign),
                reads=[RD1] + list(rsrc_extra), writes=[RD2])
            round_min_sign(dst_bf_ap, RD2, clipv, sg_ap, RD2, n, par=par)

        def phase_prep_wq():
            RD1h2 = [Res("pd0"), Res("pd1")]
            RD2Q = [Res(f"q2o{i}") for i in range(4)]
            RD2S = Res("q2st")
            for hc in range(64):
                for hf2 in range(2):
                    dst = D1[:, hf2 * 1024:(hf2 + 1) * 1024]
                    b.dma("sync", lambda e, hc=hc, hf2=hf2, dst=dst: e.dma_start(
                        dst, dr["l1_w1"][hc * 128:(hc + 1) * 128,
                                         hf2 * 1024:(hf2 + 1) * 1024]),
                        writes=[RD1h2[hf2]])
                    b.op("vector", lambda e, hc=hc, hf2=hf2, dst=dst: e.reduce_sum(
                        statx[:, 680 + hf2:681 + hf2], dst, AXX,
                        apply_absolute_value=True),
                        reads=[RD1h2[hf2]], writes=[Rstx])
                b.op("vector", lambda e, hc=hc: e.tensor_tensor(
                    statx[:, 682:683], statx[:, 680:681], statx[:, 681:682],
                    ALU.add),
                    reads=[Rstx], writes=[Rstx])
                b.op("vector", lambda e, hc=hc: e.tensor_scalar(
                    statx[:, hc:hc + 1], statx[:, 682:683], 1.0 / DM, None,
                    ALU.mult),
                    reads=[Rstx], writes=[Rstx])
                b.op("vector", lambda e, hc=hc: e.reciprocal(
                    statx[:, 600 + hc:601 + hc], statx[:, 682:683]),
                    reads=[Rstx], writes=[Rstx])
                b.op("vector", lambda e, hc=hc: e.tensor_scalar(
                    statx[:, 600 + hc:601 + hc], statx[:, 600 + hc:601 + hc],
                    float(DM), None, ALU.mult),
                    reads=[Rstx], writes=[Rstx])
            for dc in range(16):
                for pc in range(8):
                    dst = D1[:, (pc % 2) * 1024:(pc % 2) * 1024 + 1024]
                    b.dma("sync", lambda e, dc=dc, pc=pc, dst=dst: e.dma_start(
                        dst, dr["l1_w2"][dc * 128:(dc + 1) * 128,
                                         pc * 1024:(pc + 1) * 1024]),
                        writes=[RD1h2[pc % 2]])
                    b.op("vector", lambda e, pc=pc, dst=dst: e.reduce_sum(
                        statx[:, 684 + pc:685 + pc], dst, AXX,
                        apply_absolute_value=True),
                        reads=[RD1h2[pc % 2]], writes=[Rstx])
                for st in (1, 2, 4):
                    for j in range(0, 8, 2 * st):
                        b.op("vector", lambda e, j=j, st=st: e.tensor_tensor(
                            statx[:, 684 + j:685 + j], statx[:, 684 + j:685 + j],
                            statx[:, 684 + j + st:685 + j + st], ALU.add),
                            reads=[Rstx], writes=[Rstx])
                b.op("vector", lambda e, dc=dc: e.tensor_scalar(
                    statx[:, 64 + dc:65 + dc], statx[:, 684:685], 1.0 / 8192,
                    None, ALU.mult),
                    reads=[Rstx], writes=[Rstx])
                b.op("vector", lambda e, dc=dc: e.reciprocal(
                    statx[:, 664 + dc:665 + dc], statx[:, 684:685]),
                    reads=[Rstx], writes=[Rstx])
                b.op("vector", lambda e, dc=dc: e.tensor_scalar(
                    statx[:, 664 + dc:665 + dc], statx[:, 664 + dc:665 + dc],
                    8192.0, None, ALU.mult),
                    reads=[Rstx], writes=[Rstx])
            for hc in range(64):
                for hf2 in range(2):
                    dst = D1[:, hf2 * 1024:(hf2 + 1) * 1024]
                    b.dma("sync", lambda e, hc=hc, hf2=hf2, dst=dst: e.dma_start(
                        dst, dr["l1_w1"][hc * 128:(hc + 1) * 128,
                                         hf2 * 1024:(hf2 + 1) * 1024]),
                        writes=[RD1h2[hf2]])
                    for sp in range(2):
                        pc = hf2 * 2 + sp
                        quant_chain2(D2[:, pc * 512:(pc + 1) * 512], RD2Q[pc],
                                     D1[:, pc * 512:(pc + 1) * 512],
                                     statx[:, 600 + hc:601 + hc], 1.0,
                                     D2[:, 3072 + (pc % 2) * 512:
                                           3072 + (pc % 2) * 512 + 512], 512,
                                     par=pc % 2,
                                     rsrc_extra=[RD1h2[hf2]])
                for t in range(16):
                    transpose(D2[:, 2048 + (t % 8) * 128:2048 + (t % 8 + 1) * 128],
                              D2[:, t * 128:(t + 1) * 128],
                              RD2S, RD2Q[t // 4], use2=(t % 2 == 1))
                    if t % 8 == 7:
                        b.dma("sync", lambda e, hc=hc, t=t: e.dma_start(
                            wq1t_s[(t - 7) * 128:(t + 1) * 128,
                                   hc * 128:(hc + 1) * 128]
                            .rearrange("(c p) f -> p c f", p=128),
                            D2[:, 2048:3072].rearrange("p (c f) -> p c f", f=128)),
                            reads=[RD2S], writes=[Rscr["wq1t_s"]])
            for dc in range(16):
                for pc4 in range(4):
                    for hf2 in range(2):
                        pp = pc4 * 2 + hf2
                        dst = D1[:, hf2 * 1024:(hf2 + 1) * 1024]
                        b.dma("sync", lambda e, dc=dc, pp=pp, dst=dst: e.dma_start(
                            dst, dr["l1_w2"][dc * 128:(dc + 1) * 128,
                                             pp * 1024:(pp + 1) * 1024]),
                            writes=[RD1h2[hf2]])
                        for sp in range(2):
                            sx = hf2 * 2 + sp
                            quant_chain2(D2[:, sx * 512:(sx + 1) * 512], RD2Q[sx],
                                         D1[:, sx * 512:(sx + 1) * 512],
                                         statx[:, 664 + dc:665 + dc], 1.0,
                                         D2[:, 3072 + (sx % 2) * 512:
                                               3072 + (sx % 2) * 512 + 512], 512,
                                         par=sx % 2,
                                         rsrc_extra=[RD1h2[hf2]])
                    for t in range(16):
                        transpose(D2[:, 2048 + (t % 8) * 128:2048 + (t % 8 + 1) * 128],
                                  D2[:, t * 128:(t + 1) * 128],
                                  RD2S, RD2Q[t // 4], use2=(t % 2 == 1))
                        if t % 8 == 7:
                            b.dma("sync", lambda e, dc=dc, pc4=pc4, t=t: e.dma_start(
                                wq2t_s[pc4 * 2048 + (t - 7) * 128:
                                       pc4 * 2048 + (t + 1) * 128,
                                       dc * 128:(dc + 1) * 128]
                                .rearrange("(c p) f -> p c f", p=128),
                                D2[:, 2048:3072].rearrange("p (c f) -> p c f", f=128)),
                                reads=[RD2S], writes=[Rscr["wq2t_s"]])

        def phase_l1_ffn():
            for i in range(NT):
                b.op("vector", lambda e, i=i: e.tensor_scalar_max(
                    stat[:, 48 + i:49 + i], stat[:, 32 + i:33 + i], 1e-5),
                    reads=[Rstat], writes=[Rstat])
                b.op("vector", lambda e, i=i: e.tensor_scalar(
                    stat[:, 64 + i:65 + i], stat[:, 48 + i:49 + i], 1.0 / 127.0,
                    None, ALU.mult),
                    reads=[Rstat], writes=[Rstat])
                b.op("vector", lambda e, i=i: e.reciprocal(
                    stat[:, 48 + i:49 + i], stat[:, 48 + i:49 + i]),
                    reads=[Rstat], writes=[Rstat])
                b.op("vector", lambda e, i=i: e.tensor_scalar(
                    stat[:, 48 + i:49 + i], stat[:, 48 + i:49 + i], 127.0,
                    None, ALU.mult),
                    reads=[Rstat], writes=[Rstat])

            def build_bcast(colbase, dst_ap, dst_res, nda=True):
                for i in range(NT):
                    b.op("tensor", lambda e, i=i, colbase=colbase: e.transpose(
                        PA[0:1, 1024:1152], stat[:, colbase + i:colbase + i + 1],
                        idf[:]),
                        reads=[Rstat, Rconst], writes=[RPA[2]])
                    b.op("vector", lambda e, i=i: e.tensor_copy(
                        D1[0:1, i * 128:(i + 1) * 128], PA[0:1, 1024:1152]),
                        reads=[RPA[2]], writes=[RD1])
                for fb in range(4):
                    b.op("tensor", lambda e, fb=fb: e.matmul(
                        PA[:, 1024:1536], onesr[:], D1[0:1, fb * 512:(fb + 1) * 512],
                        start=True, stop=True),
                        reads=[RD1, Rconst], writes=[RPA[2]])
                    b.op("vector", lambda e, fb=fb, dst_ap=dst_ap: e.tensor_copy(
                        dst_ap[:, fb * 512:(fb + 1) * 512], PA[:, 1024:1536]),
                        reads=[RPA[2]], writes=[dst_res])

            build_bcast(48, D2[:, 0:2048], RD2)
            build_bcast(64, D2[:, 2048:4096], RD2)
            for i in range(NT):
                b.op("tensor", lambda e, i=i: e.transpose(
                    PA[0:1, 1024:1152], statx[:, 64 + i:65 + i], idf[:]),
                    reads=[Rstx, Rconst], writes=[RPA[2]])
                b.op("vector", lambda e, i=i: e.tensor_copy(
                    D1[0:1, i * 128:(i + 1) * 128], PA[0:1, 1024:1152]),
                    reads=[RPA[2]], writes=[RD1])
            for fb in range(4):
                b.op("tensor", lambda e, fb=fb: e.matmul(
                    PA[:, 1024:1536], onesr[:], D1[0:1, fb * 512:(fb + 1) * 512],
                    start=True, stop=True),
                    reads=[RD1, Rconst], writes=[RPA[2]])
                b.op("vector", lambda e, fb=fb: e.tensor_copy(
                    C1[:, 8192 + fb * 512:8192 + (fb + 1) * 512], PA[:, 1024:1536]),
                    reads=[RPA[2]], writes=[RQ[1]])

            for cc in range(16):
                for pc in range(4):
                    src = A[:, cc * 2048 + pc * 512: cc * 2048 + (pc + 1) * 512]
                    par = pc % 2
                    ao = 704 if par == 0 else 88
                    sgs = scanb[:, par * 512:par * 512 + 512]
                    b.op("vector", lambda e, src=src, pc=pc, ao=ao: e.tensor_tensor(
                        statx[:, ao:ao + 512], src, D2[:, pc * 512:(pc + 1) * 512],
                        ALU.mult),
                        reads=[RA[cc], RD2], writes=[Rstx])
                    b.op("scalar", lambda e, src=src, sgs=sgs: e.activation(
                        sgs, src, AF.Sign),
                        reads=[RA[cc]], writes=[Rscb])
                    b.op("scalar", lambda e, ao=ao: e.activation(
                        statx[:, ao:ao + 512], statx[:, ao:ao + 512], AF.Abs),
                        reads=[Rstx], writes=[Rstx])
                    round_min_sign(src, RA[cc], 127.0, sgs, Rscb, 512, par=par)

            RMX, RMN, S2B, SG2 = 512, 1024, 0, 512
            RBH = [Res(f"h1c{i}") for i in range(64)]
            for g in range(4):
                gcol = g * 512
                for hc in range(64):
                    wk = hc % 2
                    wcol = 16384 + wk * 2048
                    wload(C1[:, wcol:wcol + 2048].rearrange("p (c f) -> p c f", f=128),
                          wq1t_s[:, hc * 128:(hc + 1) * 128]
                          .rearrange("(c p) f -> p c f", p=128), RO[wk])
                    bank = RPA[hc % 2]
                    bap = PAb(hc % 2)
                    for kc in range(16):
                        b.op("tensor", lambda e, hc=hc, kc=kc, gcol=gcol, wcol=wcol, bap=bap: e.matmul(
                            bap,
                            C1[:, wcol + kc * 128:wcol + (kc + 1) * 128],
                            A[:, kc * 2048 + gcol: kc * 2048 + gcol + 512],
                            start=(kc == 0), stop=(kc == 15)),
                            reads=[RA[kc], RO[hc % 2]], writes=[bank])
                    b.op("vector", lambda e, gcol=gcol, bap=bap: e.tensor_tensor(
                        statx[:, 704:1216], bap, D2[:, 2048 + gcol:2048 + gcol + 512],
                        ALU.mult),
                        reads=[bank, RD2], writes=[Rstx])
                    b.op("scalar", lambda e, hc=hc: e.activation(
                        Bacc[:, hc * 512:(hc + 1) * 512], statx[:, 704:1216],
                        AF.Gelu_apprx_tanh, scale=statx[:, hc:hc + 1]),
                        reads=[Rstx], writes=[RBH[hc]])
                    if hc == 0:
                        b.op("vector", lambda e: e.tensor_copy(
                            scanb[:, RMX:RMX + 512], Bacc[:, 0:512]),
                            reads=[RBH[0]], writes=[Rscb])
                        b.op("vector", lambda e: e.tensor_copy(
                            scanb[:, RMN:RMN + 512], Bacc[:, 0:512]),
                            reads=[RBH[0]], writes=[Rscb])
                    else:
                        b.op("vector", lambda e, hc=hc: e.tensor_tensor(
                            scanb[:, RMX:RMX + 512], scanb[:, RMX:RMX + 512],
                            Bacc[:, hc * 512:(hc + 1) * 512], ALU.max),
                            reads=[RBH[hc], Rscb], writes=[Rscb])
                        b.op("vector", lambda e, hc=hc: e.tensor_tensor(
                            scanb[:, RMN:RMN + 512], scanb[:, RMN:RMN + 512],
                            Bacc[:, hc * 512:(hc + 1) * 512], ALU.min),
                            reads=[RBH[hc], Rscb], writes=[Rscb])
                b.op("vector", lambda e: e.tensor_scalar(
                    scanb[:, RMN:RMN + 512], scanb[:, RMN:RMN + 512], -1.0,
                    None, ALU.mult),
                    reads=[Rscb], writes=[Rscb])
                b.op("vector", lambda e: e.tensor_tensor(
                    scanb[:, RMX:RMX + 512], scanb[:, RMX:RMX + 512],
                    scanb[:, RMN:RMN + 512], ALU.max),
                    reads=[Rscb], writes=[Rscb])
                for ts in range(4):
                    b.op("tensor", lambda e, ts=ts: e.transpose(
                        PT[:, 0:128], scanb[:, RMX + ts * 128:RMX + (ts + 1) * 128],
                        idb[:]),
                        reads=[Rscb, Rconst], writes=[RPT])
                    b.op("vector", lambda e, ts=ts: e.reduce_max(
                        stat[:, 80 + ts:81 + ts], PT[:, 0:128], AXX),
                        reads=[RPT], writes=[Rstat])
                    b.op("vector", lambda e, ts=ts: e.tensor_scalar_max(
                        stat[:, 80 + ts:81 + ts], stat[:, 80 + ts:81 + ts], 1e-5),
                        reads=[Rstat], writes=[Rstat])
                    b.op("vector", lambda e, ts=ts: e.tensor_scalar(
                        stat[:, 84 + ts:85 + ts], stat[:, 80 + ts:81 + ts],
                        1.0 / 127.0, None, ALU.mult),
                        reads=[Rstat], writes=[Rstat])
                    b.op("vector", lambda e, ts=ts: e.reciprocal(
                        stat[:, 80 + ts:81 + ts], stat[:, 80 + ts:81 + ts]),
                        reads=[Rstat], writes=[Rstat])
                    b.op("vector", lambda e, ts=ts: e.tensor_scalar(
                        stat[:, 80 + ts:81 + ts], stat[:, 80 + ts:81 + ts], 127.0,
                        None, ALU.mult),
                        reads=[Rstat], writes=[Rstat])
                    b.op("tensor", lambda e, ts=ts: e.transpose(
                        PA[0:1, 1024:1152], stat[:, 80 + ts:81 + ts], idf[:]),
                        reads=[Rstat, Rconst], writes=[RPA[2]])
                    b.op("vector", lambda e, ts=ts: e.tensor_copy(
                        D1[0:1, ts * 128:(ts + 1) * 128], PA[0:1, 1024:1152]),
                        reads=[RPA[2]], writes=[RD1])
                b.op("tensor", lambda e: e.matmul(
                    PA[:, 1024:1536], onesr[:], D1[0:1, 0:512],
                    start=True, stop=True),
                    reads=[RD1, Rconst], writes=[RPA[2]])
                b.op("vector", lambda e: e.tensor_copy(
                    scanb[:, S2B:S2B + 512], PA[:, 1024:1536]),
                    reads=[RPA[2]], writes=[Rscb])
                for hc in range(64):
                    hb = Bacc[:, hc * 512:(hc + 1) * 512]
                    par = hc % 2
                    ao = 704 if par == 0 else 88
                    sgs = scanb[:, SG2 + par * 512:SG2 + par * 512 + 512]
                    b.op("vector", lambda e, hb=hb, ao=ao: e.tensor_tensor(
                        statx[:, ao:ao + 512], hb, scanb[:, S2B:S2B + 512], ALU.mult),
                        reads=[RBH[hc], Rscb], writes=[Rstx])
                    b.op("scalar", lambda e, hb=hb, sgs=sgs: e.activation(
                        sgs, hb, AF.Sign),
                        reads=[RBH[hc]], writes=[Rscb])
                    b.op("scalar", lambda e, ao=ao: e.activation(
                        statx[:, ao:ao + 512], statx[:, ao:ao + 512], AF.Abs),
                        reads=[Rstx], writes=[Rstx])
                    round_min_sign(hb, RBH[hc], 127.0, sgs, Rscb, 512, par=par)
                for fq in range(4):
                    for kc2 in range(64):
                        wk = kc2 % 2
                        wload(Q(0, wk * 512, wk * 512 + 512)
                              .rearrange("p (o f) -> p o f", f=512),
                              wq2t_s[kc2 * 128:(kc2 + 1) * 128,
                                     fq * 512:(fq + 1) * 512]
                              .rearrange("(o p) f -> p o f", p=128), RQ[wk])
                        for ts in range(4):
                            b.op("tensor", lambda e, kc2=kc2, ts=ts, wk=wk: e.matmul(
                                PAb(ts),
                                Bacc[:, kc2 * 512 + ts * 128: kc2 * 512 + (ts + 1) * 128],
                                Q(0, wk * 512, wk * 512 + 512),
                                start=(kc2 == 0), stop=(kc2 == 63)),
                                reads=[RBH[kc2], RQ[wk]], writes=[RPA[ts]])
                    for ts in range(4):
                        i = g * 4 + ts
                        b.dma("gpsimd", lambda e, i=i, fq=fq, ts=ts: e.dma_start(
                            D1[:, ts * 512:(ts + 1) * 512],
                            xa_s[i * 128:(i + 1) * 128, fq * 512:(fq + 1) * 512]),
                            writes=[RD1])
                        st_ = 704 if ts % 2 == 0 else 192
                        b.op("vector", lambda e, ts=ts, fq=fq, st_=st_: e.tensor_tensor(
                            statx[:, st_:st_ + 512], PAb(ts),
                            C1[:, 8192 + fq * 512:8192 + (fq + 1) * 512], ALU.mult),
                            reads=[RPA[ts], RQ[1]], writes=[Rstx])
                        b.op("vector", lambda e, ts=ts, st_=st_: e.scalar_tensor_tensor(
                            statx[:, st_:st_ + 512], statx[:, st_:st_ + 512],
                            stat[:, 84 + ts:85 + ts], D1[:, ts * 512:(ts + 1) * 512],
                            ALU.mult, ALU.add),
                            reads=[Rstx, Rstat, RD1], writes=[Rstx])
                        b.dma("sync", lambda e, i=i, fq=fq, st_=st_: e.dma_start(
                            out[i * 128:(i + 1) * 128, fq * 512:(fq + 1) * 512],
                            statx[:, st_:st_ + 512]),
                            reads=[Rstx], writes=[Rout])

        PH = {"ln": 1, "kv": 2, "qo1": 3, "qo2": 4, "qo": 5, "rt": 6,
              "moe1": 7, "moe": 8, "x1": 9, "l1p": 10, "l1s": 11, "l1a": 12,
              "full": 99}
        lim = PH.get(upto, 99)
        load_consts()
        phase_ln(dr["x0"], s1_track=False)
        if lim >= 2:
            b.phase_barrier()
            phase_l0_kv()
        if lim >= 3:
            b.phase_barrier()
            phase_l0_qo(sub=min(lim - 2, 3))
        if lim >= 6:
            b.phase_barrier()
            phase_router()
        if lim == 7:
            b.phase_barrier()
            phase_l0_moe(nexp=1)
        elif lim >= 8:
            b.phase_barrier()
            phase_l0_moe()
        if lim >= 9:
            b.phase_barrier()
            phase_x1_ln()
        if lim >= 10:
            b.phase_barrier()
            phase_l1_proj()
        if lim >= 11:
            b.phase_barrier()
            phase_l1_scan()
        if lim >= 12:
            b.phase_barrier()
            phase_l1_wo()
        if lim >= 13:
            b.phase_barrier()
            phase_prep_wq()
            b.phase_barrier()
            phase_l1_ffn()

        if dump == "wq1":
            for i in range(NT):
                b.dma("gpsimd", lambda e, i=i: e.dma_start(
                    out[i * 128:(i + 1) * 128, :],
                    wq1t_s[i * 128:(i + 1) * 128, 0:2048]),
                    reads=[Rscr["wq1t_s"]], writes=[Rout])
        elif dump == "wq2":
            for i in range(NT):
                b.dma("gpsimd", lambda e, i=i: e.dma_start(
                    out[i * 128:(i + 1) * 128, :],
                    wq2t_s[i * 128:(i + 1) * 128, 0:2048]),
                    reads=[Rscr["wq2t_s"]], writes=[Rout])
        elif dump == "y1d":
            for c in range(16):
                b.op("vector", lambda e, c=c: e.tensor_copy(
                    D1[:, :], A[:, c * 2048:(c + 1) * 2048]),
                    reads=[RA[c]], writes=[RD1])
                b.dma("gpsimd", lambda e, c=c: e.dma_start(
                    out[c * 128:(c + 1) * 128, :], D1[:, :]),
                    reads=[RD1], writes=[Rout])
        elif dump == "x1":
            for i in range(NT):
                b.dma("gpsimd", lambda e, i=i: e.dma_start(
                    out[i * 128:(i + 1) * 128, :], x1_s[i * 128:(i + 1) * 128, :]),
                    reads=[Rscr["x1_s"]], writes=[Rout])
        elif dump == "l1a":
            for i in range(NT):
                b.dma("gpsimd", lambda e, i=i: e.dma_start(
                    out[i * 128:(i + 1) * 128, :], xa_s[i * 128:(i + 1) * 128, :]),
                    reads=[Rscr["xa_s"]], writes=[Rout])
        elif dump in ("acc", "moe"):
            for i in range(NT):
                b.op("vector", lambda e, i=i: e.tensor_copy(
                    D1[:, :], Bacc[:, i * 2048:(i + 1) * 2048]),
                    reads=[RB[i]], writes=[RD1])
                b.dma("gpsimd", lambda e, i=i: e.dma_start(
                    out[i * 128:(i + 1) * 128, :], D1[:, :]),
                    reads=[RD1], writes=[Rout])
        elif dump == "xn":
            for c in range(16):
                b.op("vector", lambda e, c=c: e.tensor_copy(
                    D1[:, :], A[:, c * 2048:(c + 1) * 2048]),
                    reads=[RA[c]], writes=[RD1])
                b.dma("gpsimd", lambda e, c=c: e.dma_start(
                    out[c * 128:(c + 1) * 128, :], D1[:, :]),
                    reads=[RD1], writes=[Rout])

        tok = b.barrier_token()
        b.emit("sync", lambda e: e.nop(), extra_waits=tok)
        b.run(sems)
    return nc



def make_in_maps(inputs):
    x = np.ascontiguousarray(inputs["L0_attn_hidden_states"])
    maps = []
    common = {
        "l0_wq": np.ascontiguousarray(inputs["L0_attn_Wq"]),
        "l0_wk": np.ascontiguousarray(inputs["L0_attn_Wk"]),
        "l0_wv": np.ascontiguousarray(inputs["L0_attn_Wv"]),
        "l0_wo": np.ascontiguousarray(inputs["L0_attn_Wo"]),
        "l0_bo": np.ascontiguousarray(inputs["L0_attn_bo"]).reshape(1, DM),
        "l0_router": np.ascontiguousarray(inputs["L0_ffn_router_w"]),
        "l0_gate_up": np.ascontiguousarray(inputs["L0_ffn_gate_up_w"]),
        "l0_down": np.ascontiguousarray(inputs["L0_ffn_down_w"]),
        "l0_sh_gate": np.ascontiguousarray(inputs["L0_ffn_shared_gate_w"]),
        "l0_sh_up": np.ascontiguousarray(inputs["L0_ffn_shared_up_w"]),
        "l0_sh_down": np.ascontiguousarray(inputs["L0_ffn_shared_down_w"]),
        "l1_wq": np.ascontiguousarray(inputs["L1_attn_Wq"]),
        "l1_wk": np.ascontiguousarray(inputs["L1_attn_Wk"]),
        "l1_wv": np.ascontiguousarray(inputs["L1_attn_Wv"]),
        "l1_wo": np.ascontiguousarray(inputs["L1_attn_Wo"]),
        "l1_wa": np.ascontiguousarray(inputs["L1_attn_Wa"]),
        "l1_wb": np.ascontiguousarray(inputs["L1_attn_Wb"]),
        "l1_qcw": np.ascontiguousarray(inputs["L1_attn_qcw"]),
        "l1_qcb": np.ascontiguousarray(inputs["L1_attn_qcb"]).reshape(1, DM),
        "l1_kcw": np.ascontiguousarray(inputs["L1_attn_kcw"]),
        "l1_kcb": np.ascontiguousarray(inputs["L1_attn_kcb"]).reshape(1, DM),
        "l1_vcw": np.ascontiguousarray(inputs["L1_attn_vcw"]),
        "l1_vcb": np.ascontiguousarray(inputs["L1_attn_vcb"]).reshape(1, 4096),
        "l1_w1": np.ascontiguousarray(inputs["L1_ffn_w1"]),
        "l1_w2": np.ascontiguousarray(inputs["L1_ffn_w2"]),
    }
    for i in range(8):
        m = dict(common)
        m["x0"] = np.ascontiguousarray(x[i])
        maps.append(m)
    return maps


def kernel(**inputs):
    nc = build_nc()
    maps = make_in_maps(inputs)
    res = run_bass_kernel_spmd(nc, maps, core_ids=list(range(8)))
    return np.stack([np.asarray(res.results[i]["out"], dtype=np.float32)
                     for i in range(8)], axis=0)
